# version 13
# baseline (speedup 1.0000x reference)
import numpy as np
import ml_dtypes
from contextlib import ExitStack

import concourse.bass as bass
import concourse.mybir as mybir
from concourse.bass_utils import run_bass_kernel_spmd
from concourse.bass_core_rust import IndirectOffsetOnAxis
from concourse.expressions import RuntimeValue

dsl = bass.ds

F32 = mybir.dt.float32
BF16 = mybir.dt.bfloat16
AF = mybir.ActivationFunctionType
ALU = mybir.AluOpType
AX = mybir.AxisListType.X

T_TOK = 4096
D = 1024
E = 8
HID = 4096
NHEAD = 8
DH = 128
GLR = 16
GLN = 16.0
EPS = 1e-5
NT = T_TOK // 128
NTG = 8
TG = 512
NKC = D // 128
HKS = float(DH) ** -0.5
BIGI = 1024.0


class Sem:
    __slots__ = ("h", "v")

    def __init__(self, nc, ctx, name):
        self.h = ctx.enter_context(nc.semaphore(name))
        self.v = 0

    def inc(self, ins, n=1):
        ins.then_inc(self.h, n)
        self.v += n
        return self.v

    def wait(self, eng, v=None):
        eng.wait_ge(self.h, self.v if v is None else v)


def build_core_program(phases=("ffn1", "gla", "ffn2")):
    nc = bass.Bass("TRN2", target_bir_lowering=False)
    TEN, VEC, SCA, SYN, GPS = nc.tensor, nc.vector, nc.scalar, nc.sync, nc.gpsimd

    d_x = nc.dram_tensor("x", [T_TOK, D], F32, kind="ExternalInput").ap()
    d_w1 = nc.dram_tensor("w1", [E, HID, D], F32, kind="ExternalInput").ap()
    d_b1 = nc.dram_tensor("b1", [E, HID], F32, kind="ExternalInput").ap()
    d_w2 = nc.dram_tensor("w2", [E, D, HID], F32, kind="ExternalInput").ap()
    d_b2 = nc.dram_tensor("b2", [E, D], F32, kind="ExternalInput").ap()
    d_gw = nc.dram_tensor("gw", [E, D], F32, kind="ExternalInput").ap()
    d_gb = nc.dram_tensor("gb", [E], F32, kind="ExternalInput").ap()
    d_wq = nc.dram_tensor("wq", [D, D], F32, kind="ExternalInput").ap()
    d_wk = nc.dram_tensor("wk", [D, D], F32, kind="ExternalInput").ap()
    d_wv = nc.dram_tensor("wv", [D, D], F32, kind="ExternalInput").ap()
    d_wg = nc.dram_tensor("wg", [D, D], F32, kind="ExternalInput").ap()
    d_wo = nc.dram_tensor("wo", [D, D], F32, kind="ExternalInput").ap()
    d_wgk1 = nc.dram_tensor("wgk1", [D, GLR], F32, kind="ExternalInput").ap()
    d_wgk2 = nc.dram_tensor("wgk2", [GLR, D], F32, kind="ExternalInput").ap()
    d_bgk2 = nc.dram_tensor("bgk2", [D], F32, kind="ExternalInput").ap()
    d_gnw = nc.dram_tensor("gnw", [DH], F32, kind="ExternalInput").ap()
    d_y = nc.dram_tensor("y", [T_TOK, D], F32, kind="ExternalOutput").ap()

    d_w1b = nc.dram_tensor("w1b", [E, HID, D], BF16, kind="Internal").ap()
    d_w2b = nc.dram_tensor("w2b", [E, D, HID], BF16, kind="Internal").ap()
    d_wqb = nc.dram_tensor("wqb", [D, D], BF16, kind="Internal").ap()
    d_wkb = nc.dram_tensor("wkb", [D, D], BF16, kind="Internal").ap()
    d_wvb = nc.dram_tensor("wvb", [D, D], BF16, kind="Internal").ap()
    d_wgb = nc.dram_tensor("wgb", [D, D], BF16, kind="Internal").ap()
    d_wob = nc.dram_tensor("wob", [D, D], BF16, kind="Internal").ap()
    d_wgk1b = nc.dram_tensor("wgk1b", [D, GLR], BF16, kind="Internal").ap()
    d_wgk2b = nc.dram_tensor("wgk2b", [GLR, D], BF16, kind="Internal").ap()
    d_b2b = nc.dram_tensor("b2b", [E, D], BF16, kind="Internal").ap()
    d_xb = nc.dram_tensor("xb16", [T_TOK, D], BF16, kind="Internal").ap()
    d_xnt = nc.dram_tensor("xnt", [NKC, 128, T_TOK], BF16, kind="Internal").ap()
    d_xns = nc.dram_tensor("xns", [T_TOK + 128, D], BF16, kind="Internal").ap()
    d_xs = nc.dram_tensor("xs", [15 * TG, D], BF16, kind="Internal").ap()
    d_ysort = nc.dram_tensor("ysort", [15 * TG, D], BF16, kind="Internal").ap()
    d_srq = nc.dram_tensor("srq", [15 * TG, 1], mybir.dt.int32, kind="Internal").ap()
    d_w1bT = nc.dram_tensor("w1bT", [E, 4, 128, 8 * 1024], BF16,
                            kind="Internal").ap()
    d_w2bT = nc.dram_tensor("w2bT", [E, 4, 128, 8 * 1024], BF16,
                            kind="Internal").ap()
    d_b1t = nc.dram_tensor("b1t", [E, 128, 32], F32, kind="Internal").ap()

    c_id_bf = nc.inline_tensor(np.eye(128, dtype=ml_dtypes.bfloat16), "c_id_bf").ap()
    c_id_f32 = nc.inline_tensor(np.eye(128, dtype=np.float32), "c_id_f32").ap()
    c_ones = nc.inline_tensor(np.ones((1, 128), dtype=ml_dtypes.bfloat16), "c_ones").ap()
    c_iota = nc.inline_tensor(
        (np.arange(8, dtype=np.float32) + BIGI)[None, :].repeat(128, 0), "c_iota").ap()
    c_u = nc.inline_tensor(
        np.triu(np.ones((128, 128), dtype=np.float32)).astype(ml_dtypes.bfloat16),
        "c_u").ap()
    c_cb = nc.inline_tensor(
        np.array([[EPS, 1.0]], dtype=np.float32).repeat(128, 0), "c_cb").ap()
    c_onec = nc.inline_tensor(np.ones((128, 1), dtype=ml_dtypes.bfloat16),
                              "c_onec").ap()
    c_us = nc.inline_tensor(
        np.triu(np.ones((128, 128), dtype=np.float32), 1).astype(
            ml_dtypes.bfloat16), "c_us").ap()
    c_i128 = nc.inline_tensor(np.arange(128, dtype=np.float32)[:, None],
                              "c_i128").ap()

    ctx = ExitStack()
    sem = lambda n: Sem(nc, ctx, n)
    sbuf = lambda n, sh, dt: ctx.enter_context(nc.sbuf_tensor(n, sh, dt))[:]

    X = sbuf("X", [128, NT * D], BF16)
    id_bf = sbuf("id_bf", [128, 128], BF16)
    id_f32 = sbuf("id_f32", [128, 128], F32)
    ones = sbuf("ones", [1, 128], BF16)
    iotaB = sbuf("iotaB", [128, 8], F32)
    u_bf = sbuf("u_bf", [128, 128], BF16)
    masks = sbuf("masks", [128, NT * E], F32)
    b1c = sbuf("b1c", [128, 32 * E], F32)
    gbr = sbuf("gbr", [1, 8], BF16)
    bgk2r = sbuf("bgk2r", [1, D], BF16)
    gwT = sbuf("gwT", [128, NKC * E], BF16)
    gnwb = sbuf("gnwb", [128, 128], F32)
    cb = sbuf("cb", [128, 2], F32)
    onec = sbuf("onec", [128, 1], BF16)
    us_bf = sbuf("us_bf", [128, 128], BF16)
    i128 = sbuf("i128", [128, 1], F32)


    serV, serS = sem("serV"), sem("serS")

    def barV(ins):
        serV.inc(ins)
        serV.wait(VEC)
        return serV.v

    def barS(ins):
        serS.inc(ins)
        serS.wait(SCA)
        return serS.v

    s_cast_g = sem("s_cast_g")
    s_cast_b2 = sem("s_cast_b2")
    s_b2l = sem("s_b2l")
    s_cld = sem("s_cld")
    s_prep = sem("s_prep")
    s_xin2 = [sem("s_xin0"), sem("s_xin1")]
    s_xn = sem("s_xn")
    s_trT = sem("s_trT")
    s_psTev = sem("s_psTev")
    s_xw = [sem("s_xw0"), sem("s_xw1")]
    s_lgT = sem("s_lgT")
    s_wl = [sem("s_wl0"), sem("s_wl1")]
    s_xl = [sem("s_xl0"), sem("s_xl1"), sem("s_xl2")]
    s_s1 = sem("s_s1")
    s_hev = sem("s_hev")
    s_s2 = sem("s_s2")
    s_yev = sem("s_yev")
    s_gld = sem("s_gld")
    s_xgev = sem("s_xgev")
    s_pev = sem("s_pev")
    s_z1ev = sem("s_z1ev")
    s_projT = sem("s_projT")
    s_z1T = sem("s_z1T")
    s_zT = sem("s_zT")
    s_b = sem("s_b")
    s_eb = sem("s_eb")
    s_ehb = sem("s_ehb")
    s_qk = sem("s_qk")
    s_kh = sem("s_kh")
    s_S = sem("s_S")
    s_Sm = sem("s_Sm")
    s_tr = sem("s_tr")
    s_trev = sem("s_trev")
    s_o = sem("s_o")
    s_H = sem("s_H")
    s_silg = sem("s_silg")
    s_ofin = sem("s_ofin")
    s_otr = sem("s_otr")
    s_otev = sem("s_otev")
    s_wo = sem("s_wo")
    s_woev = sem("s_woev")
    s_yw2 = [sem("s_yw0"), sem("s_yw1")]
    s_cw1e = [sem(f"s_cw1_{e}") for e in range(E)]
    s_cw2e = [sem(f"s_cw2_{e}") for e in range(E)]
    s_rl4 = [sem(f"s_rl{i}") for i in range(4)]
    s_rw4 = [sem(f"s_rw{i}") for i in range(4)]
    s_gws = [sem("s_gws0"), sem("s_gws1")]
    s_gq = sem("s_gq")
    s_sc = sem("s_sc")
    s_xupd = sem("s_xupd")
    s_scat = sem("s_scat")
    s_zf = sem("s_zf")

    s_cast_x = sem("s_cast_x")
    GPS.dma_start(out=d_xb, in_=d_x).then_inc(s_cast_x.h, 16)
    s_cast_x.v += 16
    for e in range(E):
        GPS.dma_start(out=d_w1b[e], in_=d_w1[e]).then_inc(s_cw1e[e].h, 16)
        s_cw1e[e].v += 16
    GPS.dma_start(out=d_b2b, in_=d_b2).then_inc(s_cast_b2.h, 16)
    s_cast_b2.v += 16
    for e in range(E):
        GPS.dma_start(out=d_w2b[e], in_=d_w2[e]).then_inc(s_cw2e[e].h, 16)
        s_cw2e[e].v += 16
    for dst, srct in ((d_wqb, d_wq), (d_wkb, d_wk), (d_wvb, d_wv), (d_wgb, d_wg),
                      (d_wob, d_wo), (d_wgk1b, d_wgk1), (d_wgk2b, d_wgk2)):
        GPS.dma_start(out=dst, in_=srct).then_inc(s_cast_g.h, 16)
        s_cast_g.v += 16

    for dst, src in ((id_bf, c_id_bf), (id_f32, c_id_f32), (ones, c_ones),
                     (iotaB, c_iota), (u_bf, c_u), (cb, c_cb),
                     (onec, c_onec), (us_bf, c_us), (i128, c_i128)):
        SYN.dma_start(out=dst, in_=src).then_inc(s_cld.h, 16)
        s_cld.v += 16

    with ExitStack() as p0:
        p_sb = lambda n, sh, dt: p0.enter_context(nc.sbuf_tensor(n, sh, dt))[:]
        b1n = p_sb("b1n", [8, HID], F32)
        gbf = p_sb("gbf", [1, 8], F32)
        bgk2f = p_sb("bgk2f", [1, D], F32)
        gwn = p_sb("gwn", [8, D], F32)
        gwnb = p_sb("gwnb", [8, D], BF16)
        gnr = p_sb("gnr", [1, DH], F32)
        gnrb = p_sb("gnrb", [1, DH], BF16)
        ps_prep = p0.enter_context(nc.psum_tensor("ps_prep", [128, 32], F32))[:]
        ps_gw = p0.enter_context(nc.psum_tensor("ps_gw", [128, 64], BF16))[:]
        ps_gn = p0.enter_context(nc.psum_tensor("ps_gn", [128, 128], F32))[:]

        for dst, srct in ((b1n, d_b1), (gbf, d_gb[None, :]),
                          (bgk2f, d_bgk2[None, :]),
                          (gwn, d_gw), (gnr, d_gnw[None, :])):
            SYN.dma_start(out=dst, in_=srct).then_inc(s_cld.h, 16)
            s_cld.v += 16
        ld_all = s_cld.v

        VEC.wait_ge(s_cld.h, ld_all)
        VEC.tensor_copy(out=gbr, in_=gbf)
        VEC.tensor_copy(out=bgk2r, in_=bgk2f)
        VEC.tensor_copy(out=gwnb, in_=gwn)
        casts_done = barV(VEC.tensor_copy(out=gnrb, in_=gnr))

        TEN.wait_ge(s_cld.h, ld_all)
        TEN.wait_ge(serV.h, casts_done)
        for blk in range(8):
            for j in range(4):
                c = blk * 4 + j
                ins = TEN.transpose(ps_prep[:, j * 8:(j + 1) * 8],
                                    b1n[0:8, c * 128:(c + 1) * 128],
                                    id_f32[0:8, 0:8])
            s_prep.inc(ins)
            s_prep.wait(VEC)
            ev = barV(VEC.tensor_copy(out=b1c[:, blk * 32:(blk + 1) * 32],
                                      in_=ps_prep))
            TEN.wait_ge(serV.h, ev)
        for kc in range(NKC):
            ins = TEN.transpose(ps_gw[:, kc * 8:(kc + 1) * 8],
                                gwnb[0:8, kc * 128:(kc + 1) * 128],
                                id_bf[0:8, 0:8])
        s_prep.inc(ins)
        s_prep.wait(VEC)
        ev = barV(VEC.tensor_copy(out=gwT, in_=ps_gw))
        TEN.wait_ge(serV.h, ev)
        s_prep.inc(TEN.matmul(ps_gn, ones, gnrb, start=True, stop=True))
        s_prep.wait(VEC)
        prep_done_v = barV(VEC.tensor_copy(out=gnwb, in_=ps_gn))
        TEN.wait_ge(serV.h, prep_done_v)
        SCA.wait_ge(serV.h, prep_done_v)
        s_b1t = sem("s_b1t")
        b1t_sb = p_sb("b1t_sb", [128, 32 * E], F32)
        v_b1t = barV(VEC.tensor_copy(
            out=b1t_sb.rearrange("p (e c) -> p e c", e=E),
            in_=b1c.rearrange("p (c e) -> p e c", e=E)))
        SCA.wait_ge(serV.h, v_b1t)
        SCA.dma_start(out=d_b1t.rearrange("e p c -> p e c"),
                      in_=b1t_sb.rearrange("p (e c) -> p e c", e=E)
                      ).then_inc(s_b1t.h, 16)
        s_b1t.v += 16

    SYN.wait_ge(s_cast_x.h, 16)
    Xsrc = d_xb.rearrange("(t p) d -> p t d", p=128)
    Xdst = X.rearrange("p (t d) -> p t d", d=D)
    SYN.dma_start(out=Xdst, in_=Xsrc).then_inc(s_xin2[0].h, 16)
    s_xin2[0].v += 16

    ln_bufs = {}
    _uid = [0]

    def alloc_ln_bufs(scope, with_sqd=False, sqw_tiles=4):
        _uid[0] += 1
        u = _uid[0]
        g = lambda n, sh, dt: scope.enter_context(
            nc.sbuf_tensor(f"{n}_u{u}", sh, dt))[:]
        for nme in ("scol", "nmc", "m2c", "ssqc", "varc", "lnvc", "rstdc",
                    "negmrc"):
            ln_bufs[nme] = [g(f"{nme}{i}", [128, 4], F32) for i in range(2)]
        nsq = sqw_tiles + (1 if with_sqd else 0)
        ln_bufs["sqw"] = g("sqw", [128, nsq * D], BF16)
        ln_bufs["sqd"] = (ln_bufs["sqw"][:, sqw_tiles * D:nsq * D]
                          if with_sqd else None)
        ln_bufs["xn"] = [g(f"xn{i}", [128, D], BF16) for i in range(2)]
        ln_bufs["lgf"] = g("lgf", [128, 8], F32)
        ln_bufs["tmask"] = g("tmask", [128, 8], F32)
        ln_bufs["idxm"] = g("idxm", [128, 1], F32)
        ln_bufs["sq_war"] = [0]
        ln_bufs["xn_gate"] = None

    def emit_ln_xn(tt, k, i, trt0):
        lb = ln_bufs
        if tt >= 2:
            VEC.wait_ge(s_trT.h, trt0 + tt - 1)
            if lb["xn_gate"] is not None:
                lb["xn_gate"](tt)
        s_xn.inc(VEC.tensor_scalar(
            out=lb["xn"][tt % 2], in0=X[:, tt * D:(tt + 1) * D],
            scalar1=lb["rstdc"][k][:, i:i + 1],
            scalar2=lb["negmrc"][k][:, i:i + 1],
            op0=ALU.mult, op1=ALU.add))

    def emit_ln_quad(tt0, nq, k, trt0):
        lb = ln_bufs
        Xq = X[:, tt0 * D:(tt0 + nq) * D]
        X3 = Xq.rearrange("p (t d) -> p t d", d=D)
        sc = lb["scol"][k][:, 0:nq]
        barV(VEC.tensor_reduce(sc, X3, axis=AX, op=ALU.add))
        VEC.tensor_scalar(out=lb["nmc"][k][:, 0:nq], in0=sc, scalar1=-1.0 / D,
                          scalar2=None, op0=ALU.mult)
        barV(VEC.tensor_scalar(out=lb["m2c"][k][:, 0:nq], in0=sc,
                               scalar1=None, scalar2=1.0 / (D * D),
                               op0=ALU.mult, op1=ALU.mult)
             if False else
             VEC.tensor_tensor(lb["m2c"][k][:, 0:nq], sc, sc, op=ALU.mult))
        barV(VEC.tensor_scalar(out=lb["m2c"][k][:, 0:nq],
                               in0=lb["m2c"][k][:, 0:nq], scalar1=1.0 / (D * D),
                               scalar2=None, op0=ALU.mult))
        SCA.wait_ge(serV.h, lb["sq_war"][0])
        v_sq = barS(SCA.activation(lb["sqw"][:, 0:nq * D], Xq, AF.Square))
        VEC.wait_ge(serS.h, v_sq)
        sq3 = lb["sqw"][:, 0:nq * D].rearrange("p (t d) -> p t d", d=D)
        v_red = barV(VEC.tensor_reduce(lb["ssqc"][k][:, 0:nq], sq3, axis=AX,
                                       op=ALU.add))
        lb["sq_war"][0] = v_red
        v_var = barV(VEC.scalar_tensor_tensor(
            out=lb["varc"][k][:, 0:nq], in0=lb["ssqc"][k][:, 0:nq],
            scalar=1.0 / D, in1=lb["m2c"][k][:, 0:nq],
            op0=ALU.mult, op1=ALU.subtract))
        SCA.wait_ge(serV.h, v_var)
        barS(SCA.activation(lb["lnvc"][k][:, 0:nq], lb["varc"][k][:, 0:nq],
                            AF.Ln, bias=cb[:, 0:1]))
        v_rstd = barS(SCA.activation(lb["rstdc"][k][:, 0:nq],
                                     lb["lnvc"][k][:, 0:nq], AF.Exp, scale=-0.5))
        VEC.wait_ge(serS.h, v_rstd)
        barV(VEC.tensor_mul(lb["negmrc"][k][:, 0:nq], lb["nmc"][k][:, 0:nq],
                            lb["rstdc"][k][:, 0:nq]))


    _restaged = [False]

    def emit_ffn_call():
        ret = {}
        with ExitStack() as fs:
            _uid[0] += 1
            _u = _uid[0]
            g = lambda n, sh, dt: fs.enter_context(
                nc.sbuf_tensor(f"{n}_u{_u}", sh, dt))[:]
            alloc_ln_bufs(fs)
            xnts = [g(f"xnts{i}", [128, D], BF16) for i in range(2)]
            b2n = g("b2n", [8, D], BF16)
            mTb = g("mTb", [8, 128], BF16)
            wb1 = [g(f"wb1_{i}", [128, NKC * 1024], BF16) for i in range(2)]
            wb2 = [g(f"wb2_{i}", [128, NKC * 1024], BF16) for i in range(2)]
            xst = [g(f"xst{i}", [128, NKC * TG], BF16) for i in range(2)]
            hT = [g(f"hT{i}", [128, NKC * TG], BF16) for i in range(2)]
            y_acc = g("y_acc", [128, 4 * D], BF16)
            b1sel = [g(f"b1sel{i}", [128, 32], F32) for i in range(2)]
            didxq = g("didxq", [128, NT], mybir.dt.int32)
            tokall = g("tokall", [128, NT], mybir.dt.int32)
            srq_sb = g("srq_sb", [128, 60], mybir.dt.int32)
            zi = g("zi", [128, 60], mybir.dt.int32)
            gst = [g(f"gst{i}", [128, D], BF16) for i in range(2)]
            ystage = [g(f"ystage{i}", [128, D], BF16) for i in range(2)]
            selqi = g("selqi", [1, 16], mybir.dt.int32)
            maskb = g("maskb", [128, NT * E], BF16)
            ibx = g("ibx", [128, 912], F32)
            didxp = g("didxp", [128, NT], mybir.dt.int32)
            selbase = g("selbase", [128, 16], F32)
            wixi = [g(f"wixi{i}", [128, 2], mybir.dt.int32) for i in range(2)]
            with ExitStack() as ps:
                psT = ps.enter_context(nc.psum_tensor(f"psT_u{_u}", [128, D], BF16))[:]
                lg_ps = ps.enter_context(nc.psum_tensor(f"lg_ps_u{_u}", [128, 136], F32))[:]
                s1p = [ps.enter_context(nc.psum_tensor(f"s1p{i}_u{_u}", [128, TG], F32))[:]
                       for i in range(2)]
                s2p = [ps.enter_context(nc.psum_tensor(f"s2p{i}_u{_u}", [128, D], F32))[:]
                       for i in range(2)]

                TRT0, PST0, LGT0 = s_trT.v, s_psTev.v, s_lgT.v
                XN0 = s_xn.v
                ret["YEV0"] = s_yev.v
                S10, HEV0, S20, YEV0 = s_s1.v, s_hev.v, s_s2.v, s_yev.v
                NTI = 15
                NJQ = NTI * 4
                lgev_val = [0] * NT
                b2add_val = [0] * NT
                xw_val = [0] * NT
                wl_val = [0] * NJQ
                b1_val = [0] * NTI
                xl2_val = [0] * NTI
                gws_j_val = [0] * 60
                v_psr = [0] * NT
                ret["XU0"] = s_xupd.v
                XU0, SC0 = s_xupd.v, s_sc.v
                ln_bufs["xn_gate"] = (
                    lambda tt: VEC.wait_ge(s_xw[tt % 2].h, xw_val[tt - 2]))
                sqw = None

                SCA.wait_ge(serV.h, prep_done_v)
                TEN.wait_ge(serV.h, prep_done_v)
                SYN.wait_ge(serV.h, prep_done_v)
                SYN.wait_ge(s_cast_b2.h, 16)
                SYN.dma_start(out=b2n, in_=d_b2b).then_inc(s_b2l.h, 16)
                s_b2l.v += 16
                TEN.wait_ge(s_b2l.h, s_b2l.v)

                sqw = ln_bufs["sqw"]
                barV(VEC.memset(ibx[:, 0:60], 4096.0))
                barV(VEC.tensor_copy(out=zi, in_=ibx[:, 0:60]))
                if not _restaged[0]:
                    v_g0 = barV(VEC.memset(gst[0], 0.0))
                    GPS.wait_ge(serV.h, v_g0)
                    GPS.dma_start(out=d_xns[T_TOK:T_TOK + 128, :],
                                  in_=gst[0]).then_inc(s_zf.h, 16)
                    s_zf.v += 16
                for t_ in range(NT):
                    ins = VEC.tensor_scalar(out=ibx[:, 64 + t_:65 + t_],
                                            in0=i128, scalar1=float(t_ * 128),
                                            scalar2=None, op0=ALU.add)
                barV(ins)
                v_pref = barV(VEC.tensor_copy(out=tokall, in_=ibx[:, 64:96]))
                ln_bufs["sq_war"][0] = v_pref
                GPS.wait_ge(serV.h, v_pref)
                GPS.dma_start(out=d_srq.rearrange("(p t) o -> p (t o)", p=128),
                              in_=zi).then_inc(s_zf.h, 16)
                s_zf.v += 16

                for tt in range(NT):
                    k = tt % 2
                    if tt % 4 == 0:
                        emit_ln_quad(tt, 4, (tt // 4) % 2, TRT0)
                    emit_ln_xn(tt, (tt // 4) % 2, tt % 4, TRT0)
                    TEN.wait_ge(s_xn.h, XN0 + tt + 1)
                    if tt >= 1:
                        TEN.wait_ge(s_psTev.h, PST0 + tt)
                    for j in range(NKC):
                        ins = TEN.transpose(psT[:, j * 128:(j + 1) * 128],
                                            ln_bufs["xn"][k][:, j * 128:(j + 1) * 128],
                                            id_bf)
                    s_trT.inc(ins)
                    VEC.wait_ge(s_trT.h, TRT0 + tt + 1)
                    if tt >= 2:
                        VEC.wait_ge(s_xw[tt % 2].h, xw_val[tt - 2])
                    s_psTev.inc(VEC.tensor_copy(out=xnts[k], in_=psT))

                    def _emit_xnt_write(tw):
                        kk = tw % 2
                        SCA.wait_ge(s_xn.h, XN0 + tw + 1)
                        SCA.dma_start(out=d_xns[tw * 128:(tw + 1) * 128, :],
                                      in_=ln_bufs["xn"][kk]).then_inc(
                            s_xw[tw % 2].h, 16)
                        s_xw[tw % 2].v += 16
                        xw_val[tw] = s_xw[tw % 2].v

                    if tt >= 1:
                        _emit_xnt_write(tt - 1)
                    if tt == NT - 1:
                        _emit_xnt_write(tt)
                    TEN.wait_ge(s_psTev.h, PST0 + tt + 1)
                    if tt >= 1:
                        TEN.wait_ge(serV.h, lgev_val[tt - 1])
                    for kc in range(NKC):
                        TEN.matmul(lg_ps[:, 0:8],
                                   xnts[k][:, kc * 128:(kc + 1) * 128],
                                   gwT[:, kc * 8:(kc + 1) * 8],
                                   start=(kc == 0), stop=False)
                    s_lgT.inc(TEN.matmul(lg_ps[:, 0:8], ones, gbr, start=False,
                                         stop=True))
                    VEC.wait_ge(s_lgT.h, s_lgT.v)
                    lgev_val[tt] = barV(VEC.tensor_copy(out=ln_bufs["lgf"],
                                                        in_=lg_ps[:, 0:8]))
                    barV(VEC.reduce_max(ln_bufs["idxm"], ln_bufs["lgf"], axis=AX))
                    barV(VEC.tensor_scalar(out=ln_bufs["tmask"], in0=ln_bufs["lgf"],
                                           scalar1=ln_bufs["idxm"], scalar2=-BIGI,
                                           op0=ALU.is_ge, op1=ALU.mult))
                    barV(VEC.tensor_add(ln_bufs["tmask"], ln_bufs["tmask"], iotaB))
                    barV(VEC.tensor_reduce(ln_bufs["idxm"], ln_bufs["tmask"],
                                           axis=AX, op=ALU.min))
                    for e in range(E):
                        ins = VEC.tensor_scalar(
                            out=masks[:, (tt * E + e):(tt * E + e + 1)],
                            in0=ln_bufs["idxm"], scalar1=float(e),
                            scalar2=0.5, op0=ALU.is_equal, op1=ALU.mult)
                    v_mask = barV(ins)
                    TEN.wait_ge(serV.h, v_mask)
                    ins = TEN.transpose(lg_ps[0:8, 8:136],
                                        masks[:, tt * E:(tt + 1) * E], id_f32)
                    s_lgT.inc(ins)
                    VEC.wait_ge(s_lgT.h, s_lgT.v)
                    v_mtb = barV(VEC.tensor_copy(out=mTb, in_=lg_ps[0:8, 8:136]))
                    TEN.wait_ge(serV.h, v_mtb)
                    if tt >= 2:
                        TEN.wait_ge(serV.h, b2add_val[tt - 2])
                    for nh in range(2):
                        ins = TEN.matmul(s2p[tt % 2][:, nh * TG:(nh + 1) * TG],
                                         mTb, b2n[0:8, nh * TG:(nh + 1) * TG],
                                         start=True, stop=True)
                        s_lgT.inc(ins)
                    VEC.wait_ge(s_lgT.h, s_lgT.v)
                    for nh in range(2):
                        Xh = X[:, tt * D + nh * TG:tt * D + (nh + 1) * TG]
                        ins = VEC.tensor_add(Xh, Xh,
                                             s2p[tt % 2][:, nh * TG:(nh + 1) * TG])
                    b2add_val[tt] = barV(ins)

                if not _restaged[0]:
                    rst = xst[0]
                    rl_val = [0] * 4
                    rw_val = [0] * 4
                    g2 = 0
                    for e in range(E):
                        for dsrc, dT, csem in ((d_w1b, d_w1bT, s_cw1e[e]),
                                               (d_w2b, d_w2bT, s_cw2e[e])):
                            for ti in range(32):
                                q_, kc_ = divmod(ti, 8)
                                sl = g2 % 4
                                if ti == 0:
                                    SYN.wait_ge(csem.h, csem.v)
                                if g2 >= 4:
                                    SYN.wait_ge(s_rw4[sl].h, rw_val[sl])
                                if dsrc is d_w1b:
                                    wsrc = dsrc[e, q_ * 1024:(q_ + 1) * 1024,
                                                kc_ * 128:(kc_ + 1) * 128]
                                else:
                                    wsrc = dsrc[e, :, q_ * 1024 + kc_ * 128:
                                                q_ * 1024 + (kc_ + 1) * 128]
                                SYN.dma_start(
                                    out=rst[:, sl * 1024:(sl + 1) * 1024],
                                    in_=wsrc, transpose=True).then_inc(
                                    s_rl4[sl].h, 16)
                                s_rl4[sl].v += 16
                                rl_val[sl] = s_rl4[sl].v
                                SCA.wait_ge(s_rl4[sl].h, rl_val[sl])
                                SCA.dma_start(
                                    out=dT[e, q_, :,
                                           kc_ * 1024:(kc_ + 1) * 1024],
                                    in_=rst[:, sl * 1024:(sl + 1) * 1024]
                                ).then_inc(s_rw4[sl].h, 16)
                                s_rw4[sl].v += 16
                                rw_val[sl] = s_rw4[sl].v
                                g2 += 1
                    _restaged[0] = True

                ib = ibx[0:1, 0:912]
                onesf = ibx[0:1, 0:128]
                Z32 = ib[:, 512:544]
                v0 = barV(VEC.tensor_scalar(out=maskb, in0=masks, scalar1=2.0,
                                            scalar2=None, op0=ALU.mult))
                TEN.wait_ge(serV.h, v0)
                for tt in range(NT):
                    ins = TEN.matmul(s1p[0][0:1, tt * 8:(tt + 1) * 8], onec,
                                     maskb[:, tt * 8:(tt + 1) * 8],
                                     start=True, stop=True)
                s_lgT.inc(ins)
                VEC.wait_ge(s_lgT.h, s_lgT.v)
                barV(VEC.tensor_copy(out=ib[:, 0:256], in_=s1p[0][0:1, 0:256]))
                barV(VEC.tensor_copy(
                    out=ib[:, 256:512].rearrange("p (e t) -> p e t", t=NT),
                    in_=ib[:, 0:256].rearrange("p (t e) -> p e t", e=E)))
                barV(VEC.memset(Z32, 0.0))
                for e in range(E):
                    ins = VEC.tensor_tensor_scan(
                        out=ib[:, 544 + e * NT:544 + (e + 1) * NT],
                        data0=ib[:, 256 + e * NT:256 + (e + 1) * NT],
                        data1=Z32, initial=0.0, op0=ALU.add, op1=ALU.add)
                barV(ins)
                barV(VEC.tensor_sub(ib[:, 256:512], ib[:, 544:800],
                                    ib[:, 256:512]))
                barV(VEC.tensor_copy(
                    out=ib[:, 800:808],
                    in_=ib[:, 544:800].rearrange(
                        "p (e t) -> p e t", t=NT)[:, :, NT - 1]))
                barV(VEC.memset(ib[:, 816:824], 0.0))
                for jth in range(8):
                    barV(VEC.tensor_scalar(out=ib[:, 808:816],
                                           in0=ib[:, 800:808],
                                           scalar1=float(512 * jth),
                                           scalar2=512.0, op0=ALU.is_gt,
                                           op1=ALU.mult))
                    barV(VEC.tensor_add(ib[:, 816:824], ib[:, 816:824],
                                        ib[:, 808:816]))
                barV(VEC.tensor_tensor_scan(out=ib[:, 824:832],
                                            data0=ib[:, 816:824],
                                            data1=Z32[:, 0:8], initial=0.0,
                                            op0=ALU.add, op1=ALU.add))
                barV(VEC.tensor_sub(ib[:, 832:840], ib[:, 824:832],
                                    ib[:, 816:824]))
                for e in range(E):
                    ins = VEC.tensor_scalar(
                        out=ib[:, 256 + e * NT:256 + (e + 1) * NT],
                        in0=ib[:, 256 + e * NT:256 + (e + 1) * NT],
                        scalar1=ib[:, 832 + e:833 + e], scalar2=None,
                        op0=ALU.add)
                barV(ins)
                barV(VEC.tensor_copy(
                    out=ib[:, 544:800].rearrange("p (t e) -> p t e", e=E),
                    in_=ib[:, 256:512].rearrange("p (e t) -> p t e", t=NT)))
                for i in range(NTI):
                    barV(VEC.tensor_scalar(out=ib[:, 856:864],
                                           in0=ib[:, 832:840],
                                           scalar1=float(512 * i),
                                           scalar2=None, op0=ALU.is_le))
                    barV(VEC.tensor_reduce(ib[:, 840 + i:841 + i],
                                           ib[:, 856:864], axis=AX,
                                           op=ALU.add))
                barV(VEC.tensor_scalar(out=ib[:, 840:855], in0=ib[:, 840:855],
                                       scalar1=-1.0, scalar2=None,
                                       op0=ALU.add))
                v_selq = barV(VEC.tensor_copy(out=selqi[:, 0:NTI],
                                              in_=ib[:, 840:855]))
                barV(VEC.tensor_scalar(out=ib[:, 256:271], in0=ib[:, 840:855],
                                       scalar1=512.0, scalar2=None,
                                       op0=ALU.mult))
                v_1f = barV(VEC.memset(onesf, 1.0))
                TEN.wait_ge(serV.h, v_1f)
                ins = TEN.matmul(lg_ps[:, 16:31], onesf, ib[:, 256:271],
                                 start=True, stop=True)
                s_lgT.inc(ins)
                VEC.wait_ge(s_lgT.h, s_lgT.v)
                barV(VEC.tensor_copy(out=selbase[:, 0:15],
                                     in_=lg_ps[:, 16:31]))

                for tt in range(NT):
                    TEN.wait_ge(serV.h, v_selq if tt == 0 else v_psr[tt - 1])
                    TEN.matmul(lg_ps[:, 8:16], onesf,
                               ib[:, 544 + tt * 8:544 + (tt + 1) * 8],
                               start=True, stop=False)
                    ins = TEN.matmul(lg_ps[:, 8:16], us_bf,
                                     maskb[:, tt * 8:(tt + 1) * 8],
                                     start=False, stop=True)
                    s_lgT.inc(ins)
                    VEC.wait_ge(s_lgT.h, s_lgT.v)
                    v_psr[tt] = barV(VEC.tensor_mul(
                        ln_bufs["tmask"], lg_ps[:, 8:16],
                        maskb[:, tt * 8:(tt + 1) * 8]))
                    idxm = ln_bufs["idxm"]
                    barV(VEC.tensor_reduce(idxm, ln_bufs["tmask"],
                                           axis=AX, op=ALU.add))
                    barV(VEC.tensor_copy(out=didxp[:, tt:tt + 1], in_=idxm))
                    lo_ = ln_bufs["tmask"][:, 0:1]
                    t1_ = ln_bufs["tmask"][:, 1:2]
                    if tt == 0:
                        VEC.wait_ge(s_zf.h, s_zf.v)
                    barV(VEC.tensor_copy(out=zi[:, 8:9], in_=idxm))
                    barV(VEC.tensor_scalar(out=zi[:, 9:10], in0=zi[:, 8:9],
                                           scalar1=127, scalar2=None,
                                           op0=ALU.bitwise_and))
                    barV(VEC.tensor_copy(out=lo_, in_=zi[:, 9:10]))
                    barV(VEC.tensor_sub(t1_, idxm, lo_))
                    barV(VEC.tensor_scalar(out=t1_, in0=t1_,
                                           scalar1=1.0 / 128.0, scalar2=None,
                                           op0=ALU.mult))
                    barV(VEC.scalar_tensor_tensor(out=idxm, in0=lo_,
                                                  scalar=60.0, in1=t1_,
                                                  op0=ALU.mult, op1=ALU.add))
                    v_d = barV(VEC.tensor_copy(out=didxq[:, tt:tt + 1],
                                               in_=idxm))
                    GPS.wait_ge(serV.h, v_d)
                    GPS.wait_ge(s_zf.h, s_zf.v) if tt == 0 else \
                        GPS.wait_ge(s_scat.h, s_scat.v)
                    GPS.indirect_dma_start(
                        out=d_srq,
                        out_offset=IndirectOffsetOnAxis(didxq[:, tt:tt + 1], 0),
                        in_=tokall[:, tt:tt + 1], in_offset=None,
                    ).then_inc(s_scat.h, 16)
                    s_scat.v += 16

                GPS.wait_ge(s_scat.h, s_scat.v)
                GPS.dma_start(
                    out=srq_sb,
                    in_=d_srq.rearrange("(p t) o -> p (t o)", p=128)
                ).then_inc(s_gq.h, 16)
                s_gq.v += 16
                GPS.wait_ge(s_gq.h, s_gq.v)
                GPS.wait_ge(s_xw[0].h, xw_val[30])
                GPS.wait_ge(s_xw[1].h, xw_val[31])
                GPS.wait_ge(s_zf.h, s_zf.v)
                for j in range(60):
                    sl = j % 2
                    if j >= 2:
                        GPS.wait_ge(s_gws[sl].h, gws_j_val[j - 2])
                    GPS.indirect_dma_start(
                        out=gst[sl], out_offset=None, in_=d_xns,
                        in_offset=IndirectOffsetOnAxis(srq_sb[:, j:j + 1], 0),
                    ).then_inc(s_gws[sl].h, 16)
                    s_gws[sl].v += 16
                    GPS.wait_ge(s_gws[sl].h, s_gws[sl].v)
                    GPS.dma_start(out=d_xs[j * 128:(j + 1) * 128, :],
                                  in_=gst[sl]).then_inc(s_gws[sl].h, 16)
                    s_gws[sl].v += 16
                    gws_j_val[j] = s_gws[sl].v

                d_w1bT1 = d_w1bT.rearrange("e q p c -> (e q p) c")
                d_w2bT1 = d_w2bT.rearrange("e q p c -> (e q p) c")
                d_b1t1 = d_b1t.rearrange("e p c -> (e p) c")
                wix_val = [0] * NJQ

                def emit_wix(jq):
                    i, q = divmod(jq, 4)
                    sl = jq % 2
                    if jq >= 2:
                        VEC.wait_ge(s_wl[sl].h, wl_val[jq - 2])
                    ins = VEC.tensor_scalar(out=wixi[sl][:, 0:1], in0=i128,
                                            scalar1=selbase[:, i:i + 1],
                                            scalar2=float(q * 128),
                                            op0=ALU.add, op1=ALU.add)
                    if q == 0:
                        ins = VEC.scalar_tensor_tensor(
                            out=wixi[sl][:, 1:2], in0=selbase[:, i:i + 1],
                            scalar=0.25, in1=i128, op0=ALU.mult,
                            op1=ALU.add)
                    wix_val[jq] = barV(ins)

                def emit_wdyn(jq):
                    i, q = divmod(jq, 4)
                    slot = jq % 2
                    sl = jq % 2
                    if jq < 2:
                        for s4_ in range(4):
                            GPS.wait_ge(s_rw4[s4_].h, s_rw4[s4_].v)
                        GPS.wait_ge(s_b1t.h, s_b1t.v)
                    if jq >= 2:
                        GPS.wait_ge(s_s2.h, S20 + 4 * (jq - 1))
                    GPS.wait_ge(serV.h, wix_val[jq])
                    for dT1, wb_ in ((d_w1bT1, wb1), (d_w2bT1, wb2)):
                        GPS.indirect_dma_start(
                            out=wb_[slot], out_offset=None, in_=dT1,
                            in_offset=IndirectOffsetOnAxis(
                                wixi[sl][:, 0:1], 0),
                        ).then_inc(s_wl[slot].h, 16)
                        s_wl[slot].v += 16
                    if q == 0:
                        if i >= 2:
                            GPS.wait_ge(s_hev.h, HEV0 + 32 * (i - 1))
                        GPS.indirect_dma_start(
                            out=b1sel[i % 2], out_offset=None, in_=d_b1t1,
                            in_offset=IndirectOffsetOnAxis(
                                wixi[sl][:, 1:2], 0),
                        ).then_inc(s_wl[slot].h, 16)
                        s_wl[slot].v += 16
                        b1_val[i] = s_wl[slot].v
                    wl_val[jq] = s_wl[slot].v

                def emit_xst(i):
                    sl2 = i % 2
                    SYN.wait_ge(s_gws[(4 * i + 3) % 2].h, gws_j_val[4 * i + 3])
                    SYN.wait_ge(s_gws[(4 * i + 2) % 2].h, gws_j_val[4 * i + 2])
                    if i < 2:
                        for s4_ in range(4):
                            SYN.wait_ge(s_rw4[s4_].h, s_rw4[s4_].v)
                    if i >= 2:
                        SYN.wait_ge(s_s1.h, S10 + 32 * (i - 1))
                    for kc in range(NKC):
                        SYN.dma_start(
                            out=xst[sl2][:, kc * TG:(kc + 1) * TG],
                            in_=d_xs[i * TG:(i + 1) * TG,
                                     kc * 128:(kc + 1) * 128],
                            transpose=True).then_inc(s_xl[sl2].h, 16)
                        s_xl[sl2].v += 16
                    xl2_val[i] = s_xl[sl2].v

                def emit_s1s(jq):
                    i, q = divmod(jq, 4)
                    p = jq % 2
                    pj = jq % 2
                    TEN.wait_ge(s_wl[p].h, wl_val[jq])
                    TEN.wait_ge(s_xl[i % 2].h, xl2_val[i])
                    for hm in range(8):
                        gidx = jq * 8 + hm
                        if gidx >= 2:
                            TEN.wait_ge(s_hev.h, HEV0 + gidx - 1)
                        bank = s1p[hm % 2]
                        for kc in range(NKC):
                            ins = TEN.matmul(
                                bank,
                                wb1[p][:, kc * 1024 + hm * 128:
                                       kc * 1024 + (hm + 1) * 128],
                                xst[i % 2][:, kc * TG:(kc + 1) * TG],
                                start=(kc == 0), stop=(kc == NKC - 1))
                        s_s1.inc(ins)
                        SCA.wait_ge(s_s1.h, S10 + gidx + 1)
                        if hm == 0:
                            if jq >= 2:
                                SCA.wait_ge(s_s2.h, S20 + 4 * (jq - 1))
                            SCA.wait_ge(s_wl[0].h, b1_val[i])
                        s_hev.inc(SCA.activation(
                            hT[pj][:, hm * TG:(hm + 1) * TG], bank, AF.Relu,
                            bias=b1sel[i % 2][:, (q * 8 + hm):
                                              (q * 8 + hm + 1)]))

                def emit_s2s(jq):
                    i, q = divmod(jq, 4)
                    p = jq % 2
                    pj = jq % 2
                    TEN.wait_ge(s_hev.h, HEV0 + (jq + 1) * 8)
                    for tc in range(4):
                        c2 = jq * 4 + tc
                        if c2 >= 2:
                            TEN.wait_ge(s_yev.h, YEV0 + c2 - 1)
                        else:
                            TEN.wait_ge(serV.h, b2add_val[NT - 2 + c2])
                        bank = s2p[tc % 2]
                        for nh in range(2):
                            half = bank[:, nh * TG:(nh + 1) * TG]
                            for hc in range(NKC):
                                ins = TEN.matmul(
                                    half,
                                    hT[pj][:, hc * TG + tc * 128:
                                           hc * TG + (tc + 1) * 128],
                                    wb2[p][:, hc * 1024 + nh * TG:
                                           hc * 1024 + (nh + 1) * TG],
                                    start=(hc == 0),
                                    stop=(hc == NKC - 1))
                        s_s2.inc(ins)
                        VEC.wait_ge(s_s2.h, S20 + c2 + 1)
                        ysl = y_acc[:, tc * 1024:(tc + 1) * 1024]
                        if q == 0:
                            if i >= 1 and tc == 0:
                                VEC.wait_ge(s_sc.h, SC0 + 64 * i)
                            ins2 = VEC.tensor_copy(out=ysl, in_=bank)
                        else:
                            ins2 = VEC.tensor_add(ysl, ysl, bank)
                        s_yev.inc(ins2)
                    if q == 3:
                        GPS.wait_ge(s_yev.h, YEV0 + 4 * (jq + 1))
                        for tc in range(4):
                            r0_ = i * TG + tc * 128
                            GPS.dma_start(
                                out=d_ysort[r0_:r0_ + 128, :],
                                in_=y_acc[:, tc * 1024:(tc + 1) * 1024],
                            ).then_inc(s_sc.h, 16)
                            s_sc.v += 16

                emit_wix(0)
                emit_wdyn(0)
                emit_wix(1)
                emit_wdyn(1)
                emit_xst(0)
                emit_xst(1)
                emit_s1s(0)
                for jq in range(NJQ):
                    if jq + 2 < NJQ:
                        emit_wix(jq + 2)
                        emit_wdyn(jq + 2)
                    if jq + 1 < NJQ:
                        if (jq + 1) % 4 == 0:
                            ii = (jq + 1) // 4
                            if ii + 1 < NTI:
                                emit_xst(ii + 1)
                        emit_s1s(jq + 1)
                    emit_s2s(jq)

                SC_ALL = s_sc.v
                for jt in range(NT):
                    if jt == 0:
                        GPS.wait_ge(s_sc.h, SC_ALL)
                    if jt >= 2:
                        GPS.wait_ge(s_xupd.h, XU0 + jt - 1)
                    GPS.wait_ge(serV.h, v_psr[NT - 1])
                    GPS.indirect_dma_start(
                        out=ystage[jt % 2], out_offset=None, in_=d_ysort,
                        in_offset=IndirectOffsetOnAxis(didxp[:, jt:jt + 1],
                                                       0),
                    ).then_inc(s_gws[jt % 2].h, 16)
                    s_gws[jt % 2].v += 16
                    VEC.wait_ge(s_gws[jt % 2].h, s_gws[jt % 2].v)
                    Xsl = X[:, jt * D:(jt + 1) * D]
                    s_xupd.inc(VEC.scalar_tensor_tensor(
                        out=Xsl, in0=ystage[jt % 2], scalar=0.5, in1=Xsl,
                        op0=ALU.mult, op1=ALU.add))

                TEN.wait_ge(s_yev.h, s_yev.v)
                SCA.wait_ge(s_yev.h, s_yev.v)
                VEC.wait_ge(s_hev.h, s_hev.v)
        return ret

    def emit_gla_call():
        with ExitStack() as gs:
            _uid[0] += 1
            _u = _uid[0]
            g = lambda n, sh, dt: gs.enter_context(
                nc.sbuf_tensor(f"{n}_u{_u}", sh, dt))[:]
            alloc_ln_bufs(gs, with_sqd=True, sqw_tiles=2)
            wqs = g("wqs", [128, NKC * 1024], BF16)
            wks = g("wks", [128, NKC * 1024], BF16)
            wvs = g("wvs", [128, NKC * 1024], BF16)
            wgs = g("wgs", [128, NKC * 1024], BF16)
            wos = g("wos", [128, NKC * 1024], BF16)
            wgk1s = g("wgk1s", [128, NKC * GLR], BF16)
            wgk2s = g("wgk2s", [GLR, D], BF16)
            xg = g("xg", [128, NKC * 256], BF16)
            qT = g("qT", [128, NHEAD * 256], BF16)
            kTf = g("kTf", [128, NHEAD * 256], BF16)
            z1s = g("z1s", [GLR, 256], BF16)
            gks = g("gks", [128, 2 * D], BF16)
            enbt = g("enbt", [128, D], BF16)
            ktt = g("ktt", [128, D], BF16)
            vtok = g("vtok", [128, D], BF16)
            gtok = g("gtok", [128, D], BF16)
            eb = g("eb", [128, D], BF16)
            enb = g("enb", [128, D], BF16)
            ebC = g("ebC", [128, 8], F32)
            qt = g("qt", [128, D], BF16)
            kt = g("kt", [128, D], BF16)
            smb = g("smb", [128, D], BF16)
            ofb = g("ofb", [128, D], BF16)
            oTa = g("oTa", [128, D], BF16)
            ssqa = g("ssqa", [128, 8], F32)
            lnva = g("lnva", [128, 8], F32)
            rsta = g("rsta", [128, 8], F32)
            Hf = g("Hf", [128, D], F32)
            Hb = g("Hb", [128, D], BF16)

            with ExitStack() as ps:
                pE = ps.enter_context(
                    nc.psum_tensor(f"pE_u{_u}", [128, 512], F32))[:]
                pA = ps.enter_context(
                    nc.psum_tensor(f"pA_u{_u}", [128, D], F32))[:]
                pB = ps.enter_context(
                    nc.psum_tensor(f"pB_u{_u}", [128, D], F32))[:]
                pC = ps.enter_context(
                    nc.psum_tensor(f"pC_u{_u}", [128, D], F32))[:]
                ptr = ps.enter_context(
                    nc.psum_tensor(f"ptr_u{_u}", [128, D], BF16))[:]

                silg = ln_bufs["sqw"][:, 0:D]
                SYN.wait_ge(serV.h, prep_done_v)
                SYN.wait_ge(s_cast_g.h, s_cast_g.v)
                for dst, srct in ((wqs, d_wqb), (wks, d_wkb), (wvs, d_wvb),
                                  (wgs, d_wgb), (wos, d_wob)):
                    dv = dst.rearrange("p (k m) -> p k m", k=NKC)
                    sv = srct.rearrange("(k p) m -> p k m", p=128)
                    SYN.dma_start(out=dv, in_=sv).then_inc(s_gld.h, 16)
                    s_gld.v += 16
                dv = wgk1s.rearrange("p (k m) -> p k m", k=NKC)
                sv = d_wgk1b.rearrange("(k p) m -> p k m", p=128)
                SYN.dma_start(out=dv, in_=sv).then_inc(s_gld.h, 16)
                s_gld.v += 16
                SYN.dma_start(out=wgk2s, in_=d_wgk2b).then_inc(s_gld.h, 16)
                s_gld.v += 16
                gld_all = s_gld.v

                VEC.memset(Hf, 0.0)
                hinit = barV(VEC.memset(Hb, 0.0))
                TEN.wait_ge(serV.h, hinit)
                TEN.wait_ge(serV.h, prep_done_v)
                SCA.wait_ge(serV.h, prep_done_v)

                XGE0, Z1T0, Z1E0, ZT0 = s_xgev.v, s_z1T.v, s_z1ev.v, s_zT.v
                TRT0g = s_trT.v
                XN0g, PRJ0, PEV0 = s_xn.v, s_projT.v, s_pev.v
                B0, S0_, O0, H0_, WO0, WOE0 = (s_b.v, s_S.v, s_o.v, s_H.v,
                                               s_wo.v, s_woev.v)
                gk_vals = []
                osq_vals = []
                state_vals = []
                ofb_vals = []
                sm_vals = []
                otev_vals = []
                sq_vals = []
                pev_cnt = 0

                x3 = xg.rearrange("p (j c) -> p j c", j=NKC)
                qT3 = qT.rearrange("p (h t) -> p h t", t=256)
                kT3 = kTf.rearrange("p (h t) -> p h t", t=256)

                def hview(t):
                    return t.rearrange("p (h c) -> p h c", c=128)

                for cg in range(16):
                    emit_ln_quad(cg * 2, 2, cg % 2, TRT0g)
                    for lt in range(2):
                        tt = cg * 2 + lt
                        k = tt % 2
                        emit_ln_xn(tt, cg % 2, lt, TRT0g)
                        TEN.wait_ge(s_xn.h, XN0g + tt + 1)
                        if cg > 0:
                            TEN.wait_ge(serV.h, otev_vals[2 * cg - 1])
                        if tt >= 1:
                            TEN.wait_ge(s_xgev.h, XGE0 + tt)
                        for j in range(NKC):
                            ins = TEN.transpose(
                                ptr[:, j * 128:(j + 1) * 128],
                                ln_bufs["xn"][k][:, j * 128:(j + 1) * 128], id_bf)
                        s_trT.inc(ins)
                        VEC.wait_ge(s_trT.h, s_trT.v)
                        if cg > 0 and lt == 0:
                            VEC.wait_ge(s_z1T.h, Z1T0 + cg)
                            VEC.wait_ge(s_projT.h, PRJ0 + 28 * cg)
                        dstv = x3[:, :, lt * 128:(lt + 1) * 128]
                        srcv = ptr.rearrange("p (j c) -> p j c", j=NKC)
                        s_xgev.inc(VEC.tensor_copy(out=dstv, in_=srcv))

                    TEN.wait_ge(s_gld.h, gld_all)
                    TEN.wait_ge(s_xgev.h, XGE0 + 2 * (cg + 1))
                    if cg > 0:
                        TEN.wait_ge(s_z1ev.h, Z1E0 + cg)
                        TEN.wait_ge(serV.h, state_vals[2 * cg - 1])
                        SCA.wait_ge(serV.h, state_vals[2 * cg - 1])
                    TEN.wait_ge(s_pev.h, PEV0 + pev_cnt)
                    for kc in range(NKC):
                        ins = TEN.matmul(pE[0:GLR, 0:256],
                                         wgk1s[:, kc * GLR:(kc + 1) * GLR],
                                         xg[:, kc * 256:(kc + 1) * 256],
                                         start=(kc == 0), stop=(kc == NKC - 1))
                    s_z1T.inc(ins)
                    s_z1T.wait(VEC)
                    s_z1ev.inc(VEC.tensor_copy(out=z1s, in_=pE[0:GLR, 0:256]))

                    proj_groups = []
                    for wsb, dstT, scale in ((wqs, qT, HKS), (wks, kTf, 1.0)):
                        for mq in range(8):
                            proj_groups.append((wsb, dstT, scale, mq))

                    def emit_proj_group(idx):
                        nonlocal pev_cnt
                        wsb, dstT, scale, mq = proj_groups[idx]
                        region = (pE[:, 0:256] if idx % 2 == 0
                                  else pB[:, 512:768])
                        if pev_cnt >= 2:
                            TEN.wait_ge(s_pev.h, PEV0 + pev_cnt - 1)
                        for kc in range(NKC):
                            ins = TEN.matmul(
                                region,
                                wsb[:, kc * 1024 + mq * 128:
                                    kc * 1024 + (mq + 1) * 128],
                                xg[:, kc * 256:(kc + 1) * 256],
                                start=(kc == 0), stop=(kc == NKC - 1))
                        s_projT.inc(ins)
                        s_projT.wait(SCA)
                        s_pev.inc(SCA.activation(
                            dstT[:, mq * 256:(mq + 1) * 256], region,
                            AF.Copy, scale=scale))
                        pev_cnt += 1

                    for tc in range(2):
                        ch_prev_reader = 2 * (cg - 1) + tc + 1
                        TEN.wait_ge(s_z1ev.h, Z1E0 + cg + 1)
                        if cg > 0 and tc == 0:
                            TEN.wait_ge(s_woev.h, WOE0 + 2 * cg)
                        if tc == 1:
                            TEN.wait_ge(serV.h, gk_vals[2 * cg])
                        for nh in range(2):
                            half = pC[:, nh * TG:(nh + 1) * TG]
                            TEN.matmul(half, z1s[:, tc * 128:(tc + 1) * 128],
                                       wgk2s[:, nh * TG:(nh + 1) * TG],
                                       start=True, stop=False)
                            ins = TEN.matmul(half, ones,
                                             bgk2r[:, nh * TG:(nh + 1) * TG],
                                             start=False, stop=True)
                        s_zT.inc(ins)
                        for gi in range(8):
                            emit_proj_group(tc * 8 + gi)
                        SCA.wait_ge(s_zT.h, ZT0 + 2 * cg + tc + 1)
                        barS(SCA.activation(ln_bufs["sqd"], pC, AF.Abs))
                        barS(SCA.activation(ln_bufs["sqd"], ln_bufs["sqd"],
                                            AF.Exp, scale=-1.0))
                        v_zs = barS(SCA.activation(ln_bufs["sqd"], ln_bufs["sqd"],
                                                   AF.Ln, bias=cb[:, 1:2]))
                        VEC.wait_ge(serS.h, v_zs)
                        VEC.wait_ge(s_zT.h, ZT0 + 2 * cg + tc + 1)
                        if cg > 0:
                            VEC.wait_ge(s_b.h, B0 + ch_prev_reader)
                        gk_sl = gks[:, tc * D:(tc + 1) * D]
                        barV(VEC.tensor_scalar(out=gk_sl, in0=pC, scalar1=0.0,
                                               scalar2=None, op0=ALU.min))
                        v_gk = barV(VEC.tensor_sub(gk_sl, gk_sl, ln_bufs["sqd"]))
                        gk_vals.append(v_gk)

                    for lt in range(2):
                        ch = cg * 2 + lt
                        TEN.wait_ge(serV.h, gk_vals[2 * cg + lt])
                        if ch > 0:
                            TEN.wait_ge(serV.h, state_vals[ch - 1])
                            TEN.wait_ge(serV.h, ofb_vals[ch - 1])
                        TEN.wait_ge(s_pev.h, PEV0 + pev_cnt)
                        for nh in range(2):
                            ins = TEN.matmul(
                                pB[:, nh * TG:(nh + 1) * TG], u_bf,
                                gks[:, lt * D + nh * TG:lt * D + (nh + 1) * TG],
                                start=True, stop=True)
                        for h in range(NHEAD):
                            ins = TEN.matmul(
                                pA[:, h * 128:(h + 1) * 128],
                                gks[:, lt * D + h * 128:lt * D + (h + 1) * 128],
                                u_bf, start=True, stop=True)
                        s_b.inc(ins)
                        s_b.wait(SCA)
                        SCA.activation(enbt, pB, AF.Exp, scale=-1.0 / GLN)
                        SCA.activation(eb, pA, AF.Exp, scale=1.0 / GLN)
                        SCA.activation(enb, pA, AF.Exp, scale=-1.0 / GLN)
                        v_ebs = barS(SCA.activation(
                            ebC, hview(pA)[:, :, 127], AF.Exp, scale=1.0 / GLN))
                        if ch > 0:
                            SCA.wait_ge(s_o.h, O0 + ch)
                            SCA.wait_ge(s_H.h, H0_ + ch)
                            VEC.wait_ge(s_o.h, O0 + ch)
                            VEC.wait_ge(s_H.h, H0_ + ch)
                        g_pev_val = [0]
                        for wsb, kind in ((wvs, "v"), (wgs, "g"), (wks, "k")):
                            for nh in range(2):
                                TEN.wait_ge(s_pev.h, PEV0 + pev_cnt)
                                for kc in range(NKC):
                                    ins = TEN.matmul(
                                        pE,
                                        xg[:, kc * 256 + lt * 128:
                                           kc * 256 + (lt + 1) * 128],
                                        wsb[:, kc * 1024 + nh * TG:
                                            kc * 1024 + (nh + 1) * TG],
                                        start=(kc == 0), stop=(kc == NKC - 1))
                                s_projT.inc(ins)
                                if kind == "k":
                                    s_projT.wait(VEC)
                                    VEC.wait_ge(serS.h, v_ebs)
                                    ins2 = VEC.tensor_mul(
                                        ktt[:, nh * TG:(nh + 1) * TG], pE,
                                        enbt[:, nh * TG:(nh + 1) * TG])
                                    s_pev.inc(ins2)
                                else:
                                    dst = vtok if kind == "v" else gtok
                                    s_projT.wait(SCA)
                                    s_pev.inc(SCA.activation(
                                        dst[:, nh * TG:(nh + 1) * TG], pE,
                                        AF.Copy))
                                    if kind == "g":
                                        g_pev_val[0] = s_pev.v
                                pev_cnt += 1
                        if ch > 0:
                            SCA.wait_ge(serV.h, ofb_vals[ch - 1])
                        SCA.wait_ge(s_pev.h, g_pev_val[0])
                        barS(SCA.activation(silg, gtok, AF.Exp, scale=-1.0))
                        barS(SCA.activation(silg, silg, AF.Ln, bias=cb[:, 1:2]))
                        v_sg = barS(SCA.activation(silg, silg, AF.Exp,
                                                   scale=-1.0))
                        sq_vals.append(v_sg)
                        VEC.wait_ge(serS.h, v_sg)
                        for h in range(NHEAD):
                            sl = slice(h * 128, (h + 1) * 128)
                            ins = VEC.tensor_mul(gtok[:, sl], gtok[:, sl], gnwb)
                        barV(ins)
                        barV(VEC.tensor_mul(silg, gtok, silg))
                        VEC.wait_ge(serS.h, v_ebs)
                        VEC.tensor_mul(hview(qt), qT3[:, :, lt * 128:(lt + 1) * 128],
                                       hview(eb))
                        v_qt = barV(VEC.tensor_mul(
                            hview(kt), kT3[:, :, lt * 128:(lt + 1) * 128],
                            hview(enb)))
                        TEN.wait_ge(serV.h, v_qt)
                        TEN.wait_ge(serV.h, gk_vals[2 * cg + 1])
                        if ch > 0:
                            TEN.wait_ge(s_woev.h, WOE0 + ch)
                        for h in range(NHEAD):
                            ins = TEN.matmul(
                                pC[:, h * 128:(h + 1) * 128],
                                kt[:, h * 128:(h + 1) * 128],
                                qt[:, h * 128:(h + 1) * 128],
                                start=True, stop=True)
                        s_S.inc(ins)
                        s_S.wait(VEC)
                        if ch > 0:
                            VEC.wait_ge(serS.h, osq_vals[ch - 1])
                        for h in range(NHEAD):
                            ins = VEC.tensor_mul(smb[:, h * 128:(h + 1) * 128],
                                                 pC[:, h * 128:(h + 1) * 128],
                                                 u_bf)
                        v_sm = barV(ins)
                        sm_vals.append(v_sm)
                        if ch > 0:
                            TEN.wait_ge(serV.h, state_vals[ch - 1])
                        for h in range(NHEAD):
                            sl = slice(h * 128, (h + 1) * 128)
                            ins = TEN.matmul(pB[:, sl], ktt[:, sl], vtok[:, sl],
                                             start=True, stop=True)
                        s_H.inc(ins)
                        TEN.wait_ge(serV.h, v_sm)
                        for h in range(NHEAD):
                            sl = slice(h * 128, (h + 1) * 128)
                            TEN.matmul(pA[:, sl], smb[:, sl], vtok[:, sl],
                                       start=True, stop=False)
                            ins = TEN.matmul(pA[:, sl], qt[:, sl], Hb[:, sl],
                                             start=False, stop=True)
                        s_o.inc(ins)
                        s_o.wait(SCA)
                        if ch > 0:
                            SCA.wait_ge(serV.h, sm_vals[ch - 1] if False else v_sm)
                        v_osq = barS(SCA.activation(smb, pA, AF.Square))
                        osq_vals.append(v_osq)
                        VEC.wait_ge(serS.h, v_osq)
                        v_ssq = barV(VEC.tensor_reduce(
                            ssqa, hview(smb), axis=AX, op=ALU.add))
                        SCA.wait_ge(serV.h, v_ssq)
                        barS(SCA.activation(lnva, ssqa, AF.Ln, scale=1.0 / DH,
                                            bias=cb[:, 0:1]))
                        v_rst = barS(SCA.activation(rsta, lnva, AF.Exp,
                                                    scale=-0.5))
                        VEC.wait_ge(serS.h, v_rst)
                        for h in range(NHEAD):
                            sl = slice(h * 128, (h + 1) * 128)
                            ins = VEC.scalar_tensor_tensor(
                                out=ofb[:, sl], in0=pA[:, sl],
                                scalar=rsta[:, h:h + 1], in1=silg[:, sl],
                                op0=ALU.mult, op1=ALU.mult)
                        v_ofb = barV(ins)
                        ofb_vals.append(v_ofb)
                        ln_bufs["sq_war"][0] = v_ofb
                        s_H.wait(VEC)
                        barV(VEC.tensor_add(Hf, Hf, pB))
                        for h in range(NHEAD):
                            sl = slice(h * 128, (h + 1) * 128)
                            ins = VEC.tensor_scalar(
                                out=Hf[:, sl], in0=Hf[:, sl],
                                scalar1=ebC[:, h:h + 1], scalar2=None,
                                op0=ALU.mult)
                        barV(ins)
                        s_o.wait(VEC)
                        v_state = barV(VEC.tensor_copy(out=Hb, in_=Hf))
                        state_vals.append(v_state)
                        TEN.wait_ge(serV.h, v_ofb)
                        if ch > 0:
                            TEN.wait_ge(serV.h, otev_vals[ch - 1])
                        for h in range(NHEAD):
                            sl = slice(h * 128, (h + 1) * 128)
                            ins = TEN.transpose(ptr[:, sl], ofb[:, sl], id_bf)
                        s_otr.inc(ins)
                        s_otr.wait(VEC)
                        v_ot = barV(VEC.tensor_copy(out=oTa, in_=ptr))
                        otev_vals.append(v_ot)
                        TEN.wait_ge(serV.h, v_ot)
                        for nh in range(2):
                            half = pC[:, nh * TG:(nh + 1) * TG]
                            for hh in range(NHEAD):
                                ins = TEN.matmul(
                                    half, oTa[:, hh * 128:(hh + 1) * 128],
                                    wos[:, hh * 1024 + nh * TG:
                                        hh * 1024 + (nh + 1) * TG],
                                    start=(hh == 0), stop=(hh == NHEAD - 1))
                        s_wo.inc(ins)
                        s_wo.wait(VEC)
                        Xsl = X[:, ch * D:(ch + 1) * D]
                        s_woev.inc(VEC.tensor_add(Xsl, Xsl, pC))
                TEN.wait_ge(s_woev.h, s_woev.v)
                SCA.wait_ge(s_woev.h, s_woev.v)

    def x_fence():
        for eng in (TEN, VEC, SCA, SYN):
            for s in (s_xin2[0], s_yev, s_woev, s_xupd):
                s.wait(eng)

    x_fence()
    last_ffn = None
    if "ffn1" in phases:
        last_ffn = emit_ffn_call()
        x_fence()
    if "gla" in phases:
        emit_gla_call()
        last_ffn = None
        x_fence()
    if "ffn2" in phases:
        last_ffn = emit_ffn_call()
        x_fence()

    for s in (s_xin2[0], s_yev, s_woev):
        s.wait(GPS)
    for jt in range(NT):
        if last_ffn is not None:
            GPS.wait_ge(s_xupd.h, last_ffn["XU0"] + jt + 1)
        GPS.dma_start(out=d_y[jt * 128:(jt + 1) * 128, :],
                      in_=X[:, jt * D:(jt + 1) * D]).then_inc(s_yw2[0].h, 16)
        s_yw2[0].v += 16
    SCA.wait_ge(s_yw2[0].h, s_yw2[0].v)

    ctx.close()
    return nc


def _in_map(inputs, core):
    a = lambda v: np.ascontiguousarray(np.asarray(v, dtype=np.float32))
    return {
        "x": a(inputs["L0_attn_x"][core]),
        "w1": a(inputs["L0_ffn_w1"]),
        "b1": a(inputs["L0_ffn_b1"]),
        "w2": a(inputs["L0_ffn_w2"]),
        "b2": a(inputs["L0_ffn_b2"]),
        "gw": a(inputs["L0_ffn_gate_w"]),
        "gb": a(inputs["L0_ffn_gate_b"]),
        "wq": a(inputs["L0_attn_Wq"]),
        "wk": a(inputs["L0_attn_Wk"]),
        "wv": a(inputs["L0_attn_Wv"]),
        "wg": a(inputs["L0_attn_Wg"]),
        "wo": a(inputs["L0_attn_Wo"]),
        "wgk1": a(inputs["L0_attn_Wgk1"]),
        "wgk2": a(inputs["L0_attn_Wgk2"]),
        "bgk2": a(inputs["L0_attn_bgk2"]),
        "gnw": a(inputs["L0_attn_g_norm_w"]),
    }


def kernel(**inputs):
    n = 8
    nc = build_core_program()
    in_maps = [_in_map(inputs, c) for c in range(n)]
    res = run_bass_kernel_spmd(nc, in_maps, core_ids=list(range(n)))
    return np.stack([res.results[c]["y"] for c in range(n)], axis=0)


# version 14
# speedup vs baseline: 1.0075x; 1.0075x over previous
import numpy as np
import ml_dtypes
from contextlib import ExitStack

import concourse.bass as bass
import concourse.mybir as mybir
from concourse.bass_utils import run_bass_kernel_spmd
from concourse.bass_core_rust import IndirectOffsetOnAxis
from concourse.expressions import RuntimeValue

dsl = bass.ds

F32 = mybir.dt.float32
BF16 = mybir.dt.bfloat16
AF = mybir.ActivationFunctionType
ALU = mybir.AluOpType
AX = mybir.AxisListType.X

T_TOK = 4096
D = 1024
E = 8
HID = 4096
NHEAD = 8
DH = 128
GLR = 16
GLN = 16.0
EPS = 1e-5
NT = T_TOK // 128
NTG = 8
TG = 512
NKC = D // 128
HKS = float(DH) ** -0.5
BIGI = 1024.0


class Sem:
    __slots__ = ("h", "v")

    def __init__(self, nc, ctx, name):
        self.h = ctx.enter_context(nc.semaphore(name))
        self.v = 0

    def inc(self, ins, n=1):
        ins.then_inc(self.h, n)
        self.v += n
        return self.v

    def wait(self, eng, v=None):
        eng.wait_ge(self.h, self.v if v is None else v)


def build_core_program(phases=("ffn1", "gla", "ffn2")):
    nc = bass.Bass("TRN2", target_bir_lowering=False)
    TEN, VEC, SCA, SYN, GPS = nc.tensor, nc.vector, nc.scalar, nc.sync, nc.gpsimd

    d_x = nc.dram_tensor("x", [T_TOK, D], F32, kind="ExternalInput").ap()
    d_w1 = nc.dram_tensor("w1", [E, HID, D], F32, kind="ExternalInput").ap()
    d_b1 = nc.dram_tensor("b1", [E, HID], F32, kind="ExternalInput").ap()
    d_w2 = nc.dram_tensor("w2", [E, D, HID], F32, kind="ExternalInput").ap()
    d_b2 = nc.dram_tensor("b2", [E, D], F32, kind="ExternalInput").ap()
    d_gw = nc.dram_tensor("gw", [E, D], F32, kind="ExternalInput").ap()
    d_gb = nc.dram_tensor("gb", [E], F32, kind="ExternalInput").ap()
    d_wq = nc.dram_tensor("wq", [D, D], F32, kind="ExternalInput").ap()
    d_wk = nc.dram_tensor("wk", [D, D], F32, kind="ExternalInput").ap()
    d_wv = nc.dram_tensor("wv", [D, D], F32, kind="ExternalInput").ap()
    d_wg = nc.dram_tensor("wg", [D, D], F32, kind="ExternalInput").ap()
    d_wo = nc.dram_tensor("wo", [D, D], F32, kind="ExternalInput").ap()
    d_wgk1 = nc.dram_tensor("wgk1", [D, GLR], F32, kind="ExternalInput").ap()
    d_wgk2 = nc.dram_tensor("wgk2", [GLR, D], F32, kind="ExternalInput").ap()
    d_bgk2 = nc.dram_tensor("bgk2", [D], F32, kind="ExternalInput").ap()
    d_gnw = nc.dram_tensor("gnw", [DH], F32, kind="ExternalInput").ap()
    d_y = nc.dram_tensor("y", [T_TOK, D], F32, kind="ExternalOutput").ap()

    d_w1b = nc.dram_tensor("w1b", [E, HID, D], BF16, kind="Internal").ap()
    d_w2b = nc.dram_tensor("w2b", [E, D, HID], BF16, kind="Internal").ap()
    d_wqb = nc.dram_tensor("wqb", [D, D], BF16, kind="Internal").ap()
    d_wkb = nc.dram_tensor("wkb", [D, D], BF16, kind="Internal").ap()
    d_wvb = nc.dram_tensor("wvb", [D, D], BF16, kind="Internal").ap()
    d_wgb = nc.dram_tensor("wgb", [D, D], BF16, kind="Internal").ap()
    d_wob = nc.dram_tensor("wob", [D, D], BF16, kind="Internal").ap()
    d_wgk1b = nc.dram_tensor("wgk1b", [D, GLR], BF16, kind="Internal").ap()
    d_wgk2b = nc.dram_tensor("wgk2b", [GLR, D], BF16, kind="Internal").ap()
    d_b2b = nc.dram_tensor("b2b", [E, D], BF16, kind="Internal").ap()
    d_xb = nc.dram_tensor("xb16", [T_TOK, D], BF16, kind="Internal").ap()
    d_xnt = nc.dram_tensor("xnt", [NKC, 128, T_TOK], BF16, kind="Internal").ap()
    d_xns = nc.dram_tensor("xns", [T_TOK + 128, D], BF16, kind="Internal").ap()
    d_xs = nc.dram_tensor("xs", [15 * TG, D], BF16, kind="Internal").ap()
    d_ysort = nc.dram_tensor("ysort", [15 * TG, D], BF16, kind="Internal").ap()
    d_srq = nc.dram_tensor("srq", [15 * TG, 1], mybir.dt.int32, kind="Internal").ap()
    d_w1bT = nc.dram_tensor("w1bT", [E, 4, 128, 8 * 1024], BF16,
                            kind="Internal").ap()
    d_w2bT = nc.dram_tensor("w2bT", [E, 4, 128, 8 * 1024], BF16,
                            kind="Internal").ap()
    d_b1t = nc.dram_tensor("b1t", [E, 128, 32], F32, kind="Internal").ap()

    c_id_bf = nc.inline_tensor(np.eye(128, dtype=ml_dtypes.bfloat16), "c_id_bf").ap()
    c_id_f32 = nc.inline_tensor(np.eye(128, dtype=np.float32), "c_id_f32").ap()
    c_ones = nc.inline_tensor(np.ones((1, 128), dtype=ml_dtypes.bfloat16), "c_ones").ap()
    c_iota = nc.inline_tensor(
        (np.arange(8, dtype=np.float32) + BIGI)[None, :].repeat(128, 0), "c_iota").ap()
    c_u = nc.inline_tensor(
        np.triu(np.ones((128, 128), dtype=np.float32)).astype(ml_dtypes.bfloat16),
        "c_u").ap()
    c_cb = nc.inline_tensor(
        np.array([[EPS, 1.0]], dtype=np.float32).repeat(128, 0), "c_cb").ap()
    c_onec = nc.inline_tensor(np.ones((128, 1), dtype=ml_dtypes.bfloat16),
                              "c_onec").ap()
    c_us = nc.inline_tensor(
        np.triu(np.ones((128, 128), dtype=np.float32), 1).astype(
            ml_dtypes.bfloat16), "c_us").ap()
    c_i128 = nc.inline_tensor(np.arange(128, dtype=np.float32)[:, None],
                              "c_i128").ap()

    ctx = ExitStack()
    sem = lambda n: Sem(nc, ctx, n)
    sbuf = lambda n, sh, dt: ctx.enter_context(nc.sbuf_tensor(n, sh, dt))[:]

    X = sbuf("X", [128, NT * D], BF16)
    id_bf = sbuf("id_bf", [128, 128], BF16)
    id_f32 = sbuf("id_f32", [128, 128], F32)
    ones = sbuf("ones", [1, 128], BF16)
    iotaB = sbuf("iotaB", [128, 8], F32)
    u_bf = sbuf("u_bf", [128, 128], BF16)
    masks = sbuf("masks", [128, NT * E], F32)
    b1c = sbuf("b1c", [128, 32 * E], F32)
    gbr = sbuf("gbr", [1, 8], BF16)
    bgk2r = sbuf("bgk2r", [1, D], BF16)
    gwT = sbuf("gwT", [128, NKC * E], BF16)
    gnwb = sbuf("gnwb", [128, 128], F32)
    cb = sbuf("cb", [128, 2], F32)
    onec = sbuf("onec", [128, 1], BF16)
    us_bf = sbuf("us_bf", [128, 128], BF16)
    i128 = sbuf("i128", [128, 1], F32)


    serV, serS = sem("serV"), sem("serS")

    def barV(ins):
        serV.inc(ins)
        serV.wait(VEC)
        return serV.v

    def barS(ins):
        serS.inc(ins)
        serS.wait(SCA)
        return serS.v

    s_cast_g = sem("s_cast_g")
    s_cast_b2 = sem("s_cast_b2")
    s_b2l = sem("s_b2l")
    s_cld = sem("s_cld")
    s_prep = sem("s_prep")
    s_xin2 = [sem("s_xin0"), sem("s_xin1")]
    s_xn = sem("s_xn")
    s_trT = sem("s_trT")
    s_psTev = sem("s_psTev")
    s_xw = [sem("s_xw0"), sem("s_xw1")]
    s_lgT = sem("s_lgT")
    s_wl = [sem("s_wl0"), sem("s_wl1")]
    s_xl = [sem("s_xl0"), sem("s_xl1"), sem("s_xl2")]
    s_s1 = sem("s_s1")
    s_hev = sem("s_hev")
    s_s2 = sem("s_s2")
    s_yev = sem("s_yev")
    s_gld = sem("s_gld")
    s_xgev = sem("s_xgev")
    s_pev = sem("s_pev")
    s_z1ev = sem("s_z1ev")
    s_projT = sem("s_projT")
    s_z1T = sem("s_z1T")
    s_zT = sem("s_zT")
    s_b = sem("s_b")
    s_eb = sem("s_eb")
    s_ehb = sem("s_ehb")
    s_qk = sem("s_qk")
    s_kh = sem("s_kh")
    s_S = sem("s_S")
    s_Sm = sem("s_Sm")
    s_tr = sem("s_tr")
    s_trev = sem("s_trev")
    s_o = sem("s_o")
    s_H = sem("s_H")
    s_silg = sem("s_silg")
    s_ofin = sem("s_ofin")
    s_otr = sem("s_otr")
    s_otev = sem("s_otev")
    s_wo = sem("s_wo")
    s_woev = sem("s_woev")
    s_yw2 = [sem("s_yw0"), sem("s_yw1")]
    s_cw1e = [sem(f"s_cw1_{e}") for e in range(E)]
    s_cw2e = [sem(f"s_cw2_{e}") for e in range(E)]
    s_rl4 = [sem(f"s_rl{i}") for i in range(4)]
    s_rw4 = [sem(f"s_rw{i}") for i in range(4)]
    s_gws = [sem("s_gws0"), sem("s_gws1")]
    s_gq = sem("s_gq")
    s_sc = sem("s_sc")
    s_xupd = sem("s_xupd")
    s_scat = sem("s_scat")
    s_zf = sem("s_zf")

    s_cast_x = sem("s_cast_x")
    GPS.dma_start(out=d_xb, in_=d_x).then_inc(s_cast_x.h, 16)
    s_cast_x.v += 16
    GPS.dma_start(out=d_b2b, in_=d_b2).then_inc(s_cast_b2.h, 16)
    s_cast_b2.v += 16
    for e in range(E):
        GPS.dma_start(out=d_w1b[e], in_=d_w1[e]).then_inc(s_cw1e[e].h, 16)
        s_cw1e[e].v += 16
        GPS.dma_start(out=d_w2b[e], in_=d_w2[e]).then_inc(s_cw2e[e].h, 16)
        s_cw2e[e].v += 16

    for dst, src in ((id_bf, c_id_bf), (id_f32, c_id_f32), (ones, c_ones),
                     (iotaB, c_iota), (u_bf, c_u), (cb, c_cb),
                     (onec, c_onec), (us_bf, c_us), (i128, c_i128)):
        SYN.dma_start(out=dst, in_=src).then_inc(s_cld.h, 16)
        s_cld.v += 16

    with ExitStack() as p0:
        p_sb = lambda n, sh, dt: p0.enter_context(nc.sbuf_tensor(n, sh, dt))[:]
        b1n = p_sb("b1n", [8, HID], F32)
        gbf = p_sb("gbf", [1, 8], F32)
        bgk2f = p_sb("bgk2f", [1, D], F32)
        gwn = p_sb("gwn", [8, D], F32)
        gwnb = p_sb("gwnb", [8, D], BF16)
        gnr = p_sb("gnr", [1, DH], F32)
        gnrb = p_sb("gnrb", [1, DH], BF16)
        ps_prep = p0.enter_context(nc.psum_tensor("ps_prep", [128, 32], F32))[:]
        ps_gw = p0.enter_context(nc.psum_tensor("ps_gw", [128, 64], BF16))[:]
        ps_gn = p0.enter_context(nc.psum_tensor("ps_gn", [128, 128], F32))[:]

        for dst, srct in ((b1n, d_b1), (gbf, d_gb[None, :]),
                          (bgk2f, d_bgk2[None, :]),
                          (gwn, d_gw), (gnr, d_gnw[None, :])):
            SYN.dma_start(out=dst, in_=srct).then_inc(s_cld.h, 16)
            s_cld.v += 16
        ld_all = s_cld.v

        VEC.wait_ge(s_cld.h, ld_all)
        VEC.tensor_copy(out=gbr, in_=gbf)
        VEC.tensor_copy(out=bgk2r, in_=bgk2f)
        VEC.tensor_copy(out=gwnb, in_=gwn)
        casts_done = barV(VEC.tensor_copy(out=gnrb, in_=gnr))

        TEN.wait_ge(s_cld.h, ld_all)
        TEN.wait_ge(serV.h, casts_done)
        for blk in range(8):
            for j in range(4):
                c = blk * 4 + j
                ins = TEN.transpose(ps_prep[:, j * 8:(j + 1) * 8],
                                    b1n[0:8, c * 128:(c + 1) * 128],
                                    id_f32[0:8, 0:8])
            s_prep.inc(ins)
            s_prep.wait(VEC)
            ev = barV(VEC.tensor_copy(out=b1c[:, blk * 32:(blk + 1) * 32],
                                      in_=ps_prep))
            TEN.wait_ge(serV.h, ev)
        for kc in range(NKC):
            ins = TEN.transpose(ps_gw[:, kc * 8:(kc + 1) * 8],
                                gwnb[0:8, kc * 128:(kc + 1) * 128],
                                id_bf[0:8, 0:8])
        s_prep.inc(ins)
        s_prep.wait(VEC)
        ev = barV(VEC.tensor_copy(out=gwT, in_=ps_gw))
        TEN.wait_ge(serV.h, ev)
        s_prep.inc(TEN.matmul(ps_gn, ones, gnrb, start=True, stop=True))
        s_prep.wait(VEC)
        prep_done_v = barV(VEC.tensor_copy(out=gnwb, in_=ps_gn))
        TEN.wait_ge(serV.h, prep_done_v)
        SCA.wait_ge(serV.h, prep_done_v)
        s_b1t = sem("s_b1t")
        b1t_sb = p_sb("b1t_sb", [128, 32 * E], F32)
        v_b1t = barV(VEC.tensor_copy(
            out=b1t_sb.rearrange("p (e c) -> p e c", e=E),
            in_=b1c.rearrange("p (c e) -> p e c", e=E)))
        SCA.wait_ge(serV.h, v_b1t)
        SCA.dma_start(out=d_b1t.rearrange("e p c -> p e c"),
                      in_=b1t_sb.rearrange("p (e c) -> p e c", e=E)
                      ).then_inc(s_b1t.h, 16)
        s_b1t.v += 16

    SYN.wait_ge(s_cast_x.h, 16)
    Xsrc = d_xb.rearrange("(t p) d -> p t d", p=128)
    Xdst = X.rearrange("p (t d) -> p t d", d=D)
    SYN.dma_start(out=Xdst, in_=Xsrc).then_inc(s_xin2[0].h, 16)
    s_xin2[0].v += 16

    ln_bufs = {}
    _uid = [0]

    def alloc_ln_bufs(scope, with_sqd=False, sqw_tiles=4):
        _uid[0] += 1
        u = _uid[0]
        g = lambda n, sh, dt: scope.enter_context(
            nc.sbuf_tensor(f"{n}_u{u}", sh, dt))[:]
        for nme in ("scol", "nmc", "m2c", "ssqc", "varc", "lnvc", "rstdc",
                    "negmrc"):
            ln_bufs[nme] = [g(f"{nme}{i}", [128, 4], F32) for i in range(2)]
        nsq = sqw_tiles + (1 if with_sqd else 0)
        ln_bufs["sqw"] = g("sqw", [128, nsq * D], BF16)
        ln_bufs["sqd"] = (ln_bufs["sqw"][:, sqw_tiles * D:nsq * D]
                          if with_sqd else None)
        ln_bufs["xn"] = [g(f"xn{i}", [128, D], BF16) for i in range(2)]
        ln_bufs["lgf"] = g("lgf", [128, 8], F32)
        ln_bufs["tmask"] = g("tmask", [128, 8], F32)
        ln_bufs["idxm"] = g("idxm", [128, 1], F32)
        ln_bufs["sq_war"] = [0]
        ln_bufs["xn_gate"] = None

    def emit_ln_xn(tt, k, i, trt0):
        lb = ln_bufs
        if tt >= 2:
            VEC.wait_ge(s_trT.h, trt0 + tt - 1)
            if lb["xn_gate"] is not None:
                lb["xn_gate"](tt)
        s_xn.inc(VEC.tensor_scalar(
            out=lb["xn"][tt % 2], in0=X[:, tt * D:(tt + 1) * D],
            scalar1=lb["rstdc"][k][:, i:i + 1],
            scalar2=lb["negmrc"][k][:, i:i + 1],
            op0=ALU.mult, op1=ALU.add))

    def emit_ln_quad(tt0, nq, k, trt0):
        lb = ln_bufs
        Xq = X[:, tt0 * D:(tt0 + nq) * D]
        X3 = Xq.rearrange("p (t d) -> p t d", d=D)
        sc = lb["scol"][k][:, 0:nq]
        barV(VEC.tensor_reduce(sc, X3, axis=AX, op=ALU.add))
        VEC.tensor_scalar(out=lb["nmc"][k][:, 0:nq], in0=sc, scalar1=-1.0 / D,
                          scalar2=None, op0=ALU.mult)
        barV(VEC.tensor_scalar(out=lb["m2c"][k][:, 0:nq], in0=sc,
                               scalar1=None, scalar2=1.0 / (D * D),
                               op0=ALU.mult, op1=ALU.mult)
             if False else
             VEC.tensor_tensor(lb["m2c"][k][:, 0:nq], sc, sc, op=ALU.mult))
        barV(VEC.tensor_scalar(out=lb["m2c"][k][:, 0:nq],
                               in0=lb["m2c"][k][:, 0:nq], scalar1=1.0 / (D * D),
                               scalar2=None, op0=ALU.mult))
        SCA.wait_ge(serV.h, lb["sq_war"][0])
        v_sq = barS(SCA.activation(lb["sqw"][:, 0:nq * D], Xq, AF.Square))
        VEC.wait_ge(serS.h, v_sq)
        sq3 = lb["sqw"][:, 0:nq * D].rearrange("p (t d) -> p t d", d=D)
        v_red = barV(VEC.tensor_reduce(lb["ssqc"][k][:, 0:nq], sq3, axis=AX,
                                       op=ALU.add))
        lb["sq_war"][0] = v_red
        v_var = barV(VEC.scalar_tensor_tensor(
            out=lb["varc"][k][:, 0:nq], in0=lb["ssqc"][k][:, 0:nq],
            scalar=1.0 / D, in1=lb["m2c"][k][:, 0:nq],
            op0=ALU.mult, op1=ALU.subtract))
        SCA.wait_ge(serV.h, v_var)
        barS(SCA.activation(lb["lnvc"][k][:, 0:nq], lb["varc"][k][:, 0:nq],
                            AF.Ln, bias=cb[:, 0:1]))
        v_rstd = barS(SCA.activation(lb["rstdc"][k][:, 0:nq],
                                     lb["lnvc"][k][:, 0:nq], AF.Exp, scale=-0.5))
        VEC.wait_ge(serS.h, v_rstd)
        barV(VEC.tensor_mul(lb["negmrc"][k][:, 0:nq], lb["nmc"][k][:, 0:nq],
                            lb["rstdc"][k][:, 0:nq]))


    _restaged = [False]

    def emit_ffn_call():
        ret = {}
        with ExitStack() as fs:
            _uid[0] += 1
            _u = _uid[0]
            g = lambda n, sh, dt: fs.enter_context(
                nc.sbuf_tensor(f"{n}_u{_u}", sh, dt))[:]
            alloc_ln_bufs(fs)
            xnts = [g(f"xnts{i}", [128, D], BF16) for i in range(2)]
            b2n = g("b2n", [8, D], BF16)
            mTb = g("mTb", [8, 128], BF16)
            wb1 = [g(f"wb1_{i}", [128, NKC * 1024], BF16) for i in range(2)]
            wb2 = [g(f"wb2_{i}", [128, NKC * 1024], BF16) for i in range(2)]
            xst = [g(f"xst{i}", [128, NKC * TG], BF16) for i in range(2)]
            hT = [g(f"hT{i}", [128, NKC * TG], BF16) for i in range(2)]
            y_acc = g("y_acc", [128, 4 * D], BF16)
            b1sel = [g(f"b1sel{i}", [128, 32], F32) for i in range(2)]
            didxq = g("didxq", [128, NT], mybir.dt.int32)
            tokall = g("tokall", [128, NT], mybir.dt.int32)
            srq_sb = g("srq_sb", [128, 60], mybir.dt.int32)
            zi = g("zi", [128, 60], mybir.dt.int32)
            gst = [g(f"gst{i}", [128, D], BF16) for i in range(2)]
            ystage = [g(f"ystage{i}", [128, D], BF16) for i in range(2)]
            selqi = g("selqi", [1, 16], mybir.dt.int32)
            maskb = g("maskb", [128, NT * E], BF16)
            ibx = g("ibx", [128, 912], F32)
            didxp = g("didxp", [128, NT], mybir.dt.int32)
            selbase = g("selbase", [128, 16], F32)
            wixi = [g(f"wixi{i}", [128, 2], mybir.dt.int32) for i in range(2)]
            with ExitStack() as ps:
                psT = ps.enter_context(nc.psum_tensor(f"psT_u{_u}", [128, D], BF16))[:]
                lg_ps = ps.enter_context(nc.psum_tensor(f"lg_ps_u{_u}", [128, 136], F32))[:]
                s1p = [ps.enter_context(nc.psum_tensor(f"s1p{i}_u{_u}", [128, TG], F32))[:]
                       for i in range(2)]
                s2p = [ps.enter_context(nc.psum_tensor(f"s2p{i}_u{_u}", [128, D], F32))[:]
                       for i in range(2)]

                TRT0, PST0, LGT0 = s_trT.v, s_psTev.v, s_lgT.v
                XN0 = s_xn.v
                ret["YEV0"] = s_yev.v
                S10, HEV0, S20, YEV0 = s_s1.v, s_hev.v, s_s2.v, s_yev.v
                NTI = 15
                NJQ = NTI * 4
                lgev_val = [0] * NT
                b2add_val = [0] * NT
                xw_val = [0] * NT
                wl_val = [0] * NJQ
                b1_val = [0] * NTI
                xl2_val = [0] * NTI
                gws_j_val = [0] * 60
                v_psr = [0] * NT
                ret["XU0"] = s_xupd.v
                XU0, SC0 = s_xupd.v, s_sc.v
                ln_bufs["xn_gate"] = (
                    lambda tt: VEC.wait_ge(s_xw[tt % 2].h, xw_val[tt - 2]))
                sqw = None

                SCA.wait_ge(serV.h, prep_done_v)
                TEN.wait_ge(serV.h, prep_done_v)
                SYN.wait_ge(serV.h, prep_done_v)
                SYN.wait_ge(s_cast_b2.h, 16)
                SYN.dma_start(out=b2n, in_=d_b2b).then_inc(s_b2l.h, 16)
                s_b2l.v += 16
                TEN.wait_ge(s_b2l.h, s_b2l.v)

                sqw = ln_bufs["sqw"]
                barV(VEC.memset(ibx[:, 0:60], 4096.0))
                barV(VEC.tensor_copy(out=zi, in_=ibx[:, 0:60]))
                if not _restaged[0]:
                    v_g0 = barV(VEC.memset(gst[0], 0.0))
                    GPS.wait_ge(serV.h, v_g0)
                    GPS.dma_start(out=d_xns[T_TOK:T_TOK + 128, :],
                                  in_=gst[0]).then_inc(s_zf.h, 16)
                    s_zf.v += 16
                for t_ in range(NT):
                    ins = VEC.tensor_scalar(out=ibx[:, 64 + t_:65 + t_],
                                            in0=i128, scalar1=float(t_ * 128),
                                            scalar2=None, op0=ALU.add)
                barV(ins)
                v_pref = barV(VEC.tensor_copy(out=tokall, in_=ibx[:, 64:96]))
                ln_bufs["sq_war"][0] = v_pref
                GPS.wait_ge(serV.h, v_pref)
                GPS.dma_start(out=d_srq.rearrange("(p t) o -> p (t o)", p=128),
                              in_=zi).then_inc(s_zf.h, 16)
                s_zf.v += 16

                for tt in range(NT):
                    k = tt % 2
                    if tt % 4 == 0:
                        emit_ln_quad(tt, 4, (tt // 4) % 2, TRT0)
                    emit_ln_xn(tt, (tt // 4) % 2, tt % 4, TRT0)
                    TEN.wait_ge(s_xn.h, XN0 + tt + 1)
                    if tt >= 1:
                        TEN.wait_ge(s_psTev.h, PST0 + tt)
                    for j in range(NKC):
                        ins = TEN.transpose(psT[:, j * 128:(j + 1) * 128],
                                            ln_bufs["xn"][k][:, j * 128:(j + 1) * 128],
                                            id_bf)
                    s_trT.inc(ins)
                    VEC.wait_ge(s_trT.h, TRT0 + tt + 1)
                    if tt >= 2:
                        VEC.wait_ge(s_xw[tt % 2].h, xw_val[tt - 2])
                    s_psTev.inc(VEC.tensor_copy(out=xnts[k], in_=psT))

                    def _emit_xnt_write(tw):
                        kk = tw % 2
                        SCA.wait_ge(s_xn.h, XN0 + tw + 1)
                        SCA.dma_start(out=d_xns[tw * 128:(tw + 1) * 128, :],
                                      in_=ln_bufs["xn"][kk]).then_inc(
                            s_xw[tw % 2].h, 16)
                        s_xw[tw % 2].v += 16
                        xw_val[tw] = s_xw[tw % 2].v

                    if tt >= 1:
                        _emit_xnt_write(tt - 1)
                    if tt == NT - 1:
                        _emit_xnt_write(tt)
                    TEN.wait_ge(s_psTev.h, PST0 + tt + 1)
                    if tt >= 1:
                        TEN.wait_ge(serV.h, lgev_val[tt - 1])
                    for kc in range(NKC):
                        TEN.matmul(lg_ps[:, 0:8],
                                   xnts[k][:, kc * 128:(kc + 1) * 128],
                                   gwT[:, kc * 8:(kc + 1) * 8],
                                   start=(kc == 0), stop=False)
                    s_lgT.inc(TEN.matmul(lg_ps[:, 0:8], ones, gbr, start=False,
                                         stop=True))
                    VEC.wait_ge(s_lgT.h, s_lgT.v)
                    lgev_val[tt] = barV(VEC.tensor_copy(out=ln_bufs["lgf"],
                                                        in_=lg_ps[:, 0:8]))
                    barV(VEC.reduce_max(ln_bufs["idxm"], ln_bufs["lgf"], axis=AX))
                    barV(VEC.tensor_scalar(out=ln_bufs["tmask"], in0=ln_bufs["lgf"],
                                           scalar1=ln_bufs["idxm"], scalar2=-BIGI,
                                           op0=ALU.is_ge, op1=ALU.mult))
                    barV(VEC.tensor_add(ln_bufs["tmask"], ln_bufs["tmask"], iotaB))
                    barV(VEC.tensor_reduce(ln_bufs["idxm"], ln_bufs["tmask"],
                                           axis=AX, op=ALU.min))
                    for e in range(E):
                        ins = VEC.tensor_scalar(
                            out=masks[:, (tt * E + e):(tt * E + e + 1)],
                            in0=ln_bufs["idxm"], scalar1=float(e),
                            scalar2=0.5, op0=ALU.is_equal, op1=ALU.mult)
                    v_mask = barV(ins)
                    TEN.wait_ge(serV.h, v_mask)
                    ins = TEN.transpose(lg_ps[0:8, 8:136],
                                        masks[:, tt * E:(tt + 1) * E], id_f32)
                    s_lgT.inc(ins)
                    VEC.wait_ge(s_lgT.h, s_lgT.v)
                    v_mtb = barV(VEC.tensor_copy(out=mTb, in_=lg_ps[0:8, 8:136]))
                    TEN.wait_ge(serV.h, v_mtb)
                    if tt >= 2:
                        TEN.wait_ge(serV.h, b2add_val[tt - 2])
                    for nh in range(2):
                        ins = TEN.matmul(s2p[tt % 2][:, nh * TG:(nh + 1) * TG],
                                         mTb, b2n[0:8, nh * TG:(nh + 1) * TG],
                                         start=True, stop=True)
                        s_lgT.inc(ins)
                    VEC.wait_ge(s_lgT.h, s_lgT.v)
                    for nh in range(2):
                        Xh = X[:, tt * D + nh * TG:tt * D + (nh + 1) * TG]
                        ins = VEC.tensor_add(Xh, Xh,
                                             s2p[tt % 2][:, nh * TG:(nh + 1) * TG])
                    b2add_val[tt] = barV(ins)

                if not _restaged[0]:
                    rst = xst[0]
                    rl_val = [0] * 4
                    rw_val = [0] * 4
                    g2 = 0
                    for e in range(E):
                        for dsrc, dT, csem in ((d_w1b, d_w1bT, s_cw1e[e]),
                                               (d_w2b, d_w2bT, s_cw2e[e])):
                            for ti in range(32):
                                q_, kc_ = divmod(ti, 8)
                                sl = g2 % 4
                                if ti == 0:
                                    SYN.wait_ge(csem.h, csem.v)
                                if g2 >= 4:
                                    SYN.wait_ge(s_rw4[sl].h, rw_val[sl])
                                if dsrc is d_w1b:
                                    wsrc = dsrc[e, q_ * 1024:(q_ + 1) * 1024,
                                                kc_ * 128:(kc_ + 1) * 128]
                                else:
                                    wsrc = dsrc[e, :, q_ * 1024 + kc_ * 128:
                                                q_ * 1024 + (kc_ + 1) * 128]
                                SYN.dma_start(
                                    out=rst[:, sl * 1024:(sl + 1) * 1024],
                                    in_=wsrc, transpose=True).then_inc(
                                    s_rl4[sl].h, 16)
                                s_rl4[sl].v += 16
                                rl_val[sl] = s_rl4[sl].v
                                SCA.wait_ge(s_rl4[sl].h, rl_val[sl])
                                SCA.dma_start(
                                    out=dT[e, q_, :,
                                           kc_ * 1024:(kc_ + 1) * 1024],
                                    in_=rst[:, sl * 1024:(sl + 1) * 1024]
                                ).then_inc(s_rw4[sl].h, 16)
                                s_rw4[sl].v += 16
                                rw_val[sl] = s_rw4[sl].v
                                g2 += 1
                    _restaged[0] = True

                ib = ibx[0:1, 0:912]
                onesf = ibx[0:1, 0:128]
                Z32 = ib[:, 512:544]
                v0 = barV(VEC.tensor_scalar(out=maskb, in0=masks, scalar1=2.0,
                                            scalar2=None, op0=ALU.mult))
                TEN.wait_ge(serV.h, v0)
                for tt in range(NT):
                    ins = TEN.matmul(s1p[0][0:1, tt * 8:(tt + 1) * 8], onec,
                                     maskb[:, tt * 8:(tt + 1) * 8],
                                     start=True, stop=True)
                s_lgT.inc(ins)
                VEC.wait_ge(s_lgT.h, s_lgT.v)
                barV(VEC.tensor_copy(out=ib[:, 0:256], in_=s1p[0][0:1, 0:256]))
                barV(VEC.tensor_copy(
                    out=ib[:, 256:512].rearrange("p (e t) -> p e t", t=NT),
                    in_=ib[:, 0:256].rearrange("p (t e) -> p e t", e=E)))
                barV(VEC.memset(Z32, 0.0))
                for e in range(E):
                    ins = VEC.tensor_tensor_scan(
                        out=ib[:, 544 + e * NT:544 + (e + 1) * NT],
                        data0=ib[:, 256 + e * NT:256 + (e + 1) * NT],
                        data1=Z32, initial=0.0, op0=ALU.add, op1=ALU.add)
                barV(ins)
                barV(VEC.tensor_sub(ib[:, 256:512], ib[:, 544:800],
                                    ib[:, 256:512]))
                barV(VEC.tensor_copy(
                    out=ib[:, 800:808],
                    in_=ib[:, 544:800].rearrange(
                        "p (e t) -> p e t", t=NT)[:, :, NT - 1]))
                barV(VEC.memset(ib[:, 816:824], 0.0))
                for jth in range(8):
                    barV(VEC.tensor_scalar(out=ib[:, 808:816],
                                           in0=ib[:, 800:808],
                                           scalar1=float(512 * jth),
                                           scalar2=512.0, op0=ALU.is_gt,
                                           op1=ALU.mult))
                    barV(VEC.tensor_add(ib[:, 816:824], ib[:, 816:824],
                                        ib[:, 808:816]))
                barV(VEC.tensor_tensor_scan(out=ib[:, 824:832],
                                            data0=ib[:, 816:824],
                                            data1=Z32[:, 0:8], initial=0.0,
                                            op0=ALU.add, op1=ALU.add))
                barV(VEC.tensor_sub(ib[:, 832:840], ib[:, 824:832],
                                    ib[:, 816:824]))
                for e in range(E):
                    ins = VEC.tensor_scalar(
                        out=ib[:, 256 + e * NT:256 + (e + 1) * NT],
                        in0=ib[:, 256 + e * NT:256 + (e + 1) * NT],
                        scalar1=ib[:, 832 + e:833 + e], scalar2=None,
                        op0=ALU.add)
                barV(ins)
                barV(VEC.tensor_copy(
                    out=ib[:, 544:800].rearrange("p (t e) -> p t e", e=E),
                    in_=ib[:, 256:512].rearrange("p (e t) -> p t e", t=NT)))
                for i in range(NTI):
                    barV(VEC.tensor_scalar(out=ib[:, 856:864],
                                           in0=ib[:, 832:840],
                                           scalar1=float(512 * i),
                                           scalar2=None, op0=ALU.is_le))
                    barV(VEC.tensor_reduce(ib[:, 840 + i:841 + i],
                                           ib[:, 856:864], axis=AX,
                                           op=ALU.add))
                barV(VEC.tensor_scalar(out=ib[:, 840:855], in0=ib[:, 840:855],
                                       scalar1=-1.0, scalar2=None,
                                       op0=ALU.add))
                v_selq = barV(VEC.tensor_copy(out=selqi[:, 0:NTI],
                                              in_=ib[:, 840:855]))
                barV(VEC.tensor_scalar(out=ib[:, 256:271], in0=ib[:, 840:855],
                                       scalar1=512.0, scalar2=None,
                                       op0=ALU.mult))
                v_1f = barV(VEC.memset(onesf, 1.0))
                TEN.wait_ge(serV.h, v_1f)
                ins = TEN.matmul(lg_ps[:, 16:31], onesf, ib[:, 256:271],
                                 start=True, stop=True)
                s_lgT.inc(ins)
                VEC.wait_ge(s_lgT.h, s_lgT.v)
                barV(VEC.tensor_copy(out=selbase[:, 0:15],
                                     in_=lg_ps[:, 16:31]))

                for tt in range(NT):
                    TEN.wait_ge(serV.h, v_selq if tt == 0 else v_psr[tt - 1])
                    TEN.matmul(lg_ps[:, 8:16], onesf,
                               ib[:, 544 + tt * 8:544 + (tt + 1) * 8],
                               start=True, stop=False)
                    ins = TEN.matmul(lg_ps[:, 8:16], us_bf,
                                     maskb[:, tt * 8:(tt + 1) * 8],
                                     start=False, stop=True)
                    s_lgT.inc(ins)
                    VEC.wait_ge(s_lgT.h, s_lgT.v)
                    v_psr[tt] = barV(VEC.tensor_mul(
                        ln_bufs["tmask"], lg_ps[:, 8:16],
                        maskb[:, tt * 8:(tt + 1) * 8]))
                    idxm = ln_bufs["idxm"]
                    barV(VEC.tensor_reduce(idxm, ln_bufs["tmask"],
                                           axis=AX, op=ALU.add))
                    barV(VEC.tensor_copy(out=didxp[:, tt:tt + 1], in_=idxm))
                    lo_ = ln_bufs["tmask"][:, 0:1]
                    t1_ = ln_bufs["tmask"][:, 1:2]
                    if tt == 0:
                        VEC.wait_ge(s_zf.h, s_zf.v)
                    barV(VEC.tensor_copy(out=zi[:, 8:9], in_=idxm))
                    barV(VEC.tensor_scalar(out=zi[:, 9:10], in0=zi[:, 8:9],
                                           scalar1=127, scalar2=None,
                                           op0=ALU.bitwise_and))
                    barV(VEC.tensor_copy(out=lo_, in_=zi[:, 9:10]))
                    barV(VEC.tensor_sub(t1_, idxm, lo_))
                    barV(VEC.tensor_scalar(out=t1_, in0=t1_,
                                           scalar1=1.0 / 128.0, scalar2=None,
                                           op0=ALU.mult))
                    barV(VEC.scalar_tensor_tensor(out=idxm, in0=lo_,
                                                  scalar=60.0, in1=t1_,
                                                  op0=ALU.mult, op1=ALU.add))
                    v_d = barV(VEC.tensor_copy(out=didxq[:, tt:tt + 1],
                                               in_=idxm))
                    GPS.wait_ge(serV.h, v_d)
                    GPS.wait_ge(s_zf.h, s_zf.v) if tt == 0 else \
                        GPS.wait_ge(s_scat.h, s_scat.v)
                    GPS.indirect_dma_start(
                        out=d_srq,
                        out_offset=IndirectOffsetOnAxis(didxq[:, tt:tt + 1], 0),
                        in_=tokall[:, tt:tt + 1], in_offset=None,
                    ).then_inc(s_scat.h, 16)
                    s_scat.v += 16

                GPS.wait_ge(s_scat.h, s_scat.v)
                GPS.dma_start(
                    out=srq_sb,
                    in_=d_srq.rearrange("(p t) o -> p (t o)", p=128)
                ).then_inc(s_gq.h, 16)
                s_gq.v += 16
                GPS.wait_ge(s_gq.h, s_gq.v)
                GPS.wait_ge(s_xw[0].h, xw_val[30])
                GPS.wait_ge(s_xw[1].h, xw_val[31])
                GPS.wait_ge(s_zf.h, s_zf.v)
                for j in range(60):
                    sl = j % 2
                    if j >= 2:
                        GPS.wait_ge(s_gws[sl].h, gws_j_val[j - 2])
                    GPS.indirect_dma_start(
                        out=gst[sl], out_offset=None, in_=d_xns,
                        in_offset=IndirectOffsetOnAxis(srq_sb[:, j:j + 1], 0),
                    ).then_inc(s_gws[sl].h, 16)
                    s_gws[sl].v += 16
                    GPS.wait_ge(s_gws[sl].h, s_gws[sl].v)
                    GPS.dma_start(out=d_xs[j * 128:(j + 1) * 128, :],
                                  in_=gst[sl]).then_inc(s_gws[sl].h, 16)
                    s_gws[sl].v += 16
                    gws_j_val[j] = s_gws[sl].v

                d_w1bT1 = d_w1bT.rearrange("e q p c -> (e q p) c")
                d_w2bT1 = d_w2bT.rearrange("e q p c -> (e q p) c")
                d_b1t1 = d_b1t.rearrange("e p c -> (e p) c")
                wix_val = [0] * NJQ

                def emit_wix(jq):
                    i, q = divmod(jq, 4)
                    sl = jq % 2
                    if jq >= 2:
                        VEC.wait_ge(s_wl[sl].h, wl_val[jq - 2])
                    ins = VEC.tensor_scalar(out=wixi[sl][:, 0:1], in0=i128,
                                            scalar1=selbase[:, i:i + 1],
                                            scalar2=float(q * 128),
                                            op0=ALU.add, op1=ALU.add)
                    if q == 0:
                        ins = VEC.scalar_tensor_tensor(
                            out=wixi[sl][:, 1:2], in0=selbase[:, i:i + 1],
                            scalar=0.25, in1=i128, op0=ALU.mult,
                            op1=ALU.add)
                    wix_val[jq] = barV(ins)

                def emit_wdyn(jq):
                    i, q = divmod(jq, 4)
                    slot = jq % 2
                    sl = jq % 2
                    if jq < 2:
                        for s4_ in range(4):
                            GPS.wait_ge(s_rw4[s4_].h, s_rw4[s4_].v)
                        GPS.wait_ge(s_b1t.h, s_b1t.v)
                    if jq >= 2:
                        GPS.wait_ge(s_s2.h, S20 + 4 * (jq - 1))
                    GPS.wait_ge(serV.h, wix_val[jq])
                    for dT1, wb_ in ((d_w1bT1, wb1), (d_w2bT1, wb2)):
                        GPS.indirect_dma_start(
                            out=wb_[slot], out_offset=None, in_=dT1,
                            in_offset=IndirectOffsetOnAxis(
                                wixi[sl][:, 0:1], 0),
                        ).then_inc(s_wl[slot].h, 16)
                        s_wl[slot].v += 16
                    if q == 0:
                        if i >= 2:
                            GPS.wait_ge(s_hev.h, HEV0 + 32 * (i - 1))
                        GPS.indirect_dma_start(
                            out=b1sel[i % 2], out_offset=None, in_=d_b1t1,
                            in_offset=IndirectOffsetOnAxis(
                                wixi[sl][:, 1:2], 0),
                        ).then_inc(s_wl[slot].h, 16)
                        s_wl[slot].v += 16
                        b1_val[i] = s_wl[slot].v
                    wl_val[jq] = s_wl[slot].v

                def emit_xst(i):
                    sl2 = i % 2
                    SYN.wait_ge(s_gws[(4 * i + 3) % 2].h, gws_j_val[4 * i + 3])
                    SYN.wait_ge(s_gws[(4 * i + 2) % 2].h, gws_j_val[4 * i + 2])
                    if i < 2:
                        for s4_ in range(4):
                            SYN.wait_ge(s_rw4[s4_].h, s_rw4[s4_].v)
                    if i >= 2:
                        SYN.wait_ge(s_s1.h, S10 + 32 * (i - 1))
                    for kc in range(NKC):
                        SYN.dma_start(
                            out=xst[sl2][:, kc * TG:(kc + 1) * TG],
                            in_=d_xs[i * TG:(i + 1) * TG,
                                     kc * 128:(kc + 1) * 128],
                            transpose=True).then_inc(s_xl[sl2].h, 16)
                        s_xl[sl2].v += 16
                    xl2_val[i] = s_xl[sl2].v

                def emit_s1s(jq):
                    i, q = divmod(jq, 4)
                    p = jq % 2
                    pj = jq % 2
                    TEN.wait_ge(s_wl[p].h, wl_val[jq])
                    TEN.wait_ge(s_xl[i % 2].h, xl2_val[i])
                    for hm in range(8):
                        gidx = jq * 8 + hm
                        if gidx >= 2:
                            TEN.wait_ge(s_hev.h, HEV0 + gidx - 1)
                        bank = s1p[hm % 2]
                        for kc in range(NKC):
                            ins = TEN.matmul(
                                bank,
                                wb1[p][:, kc * 1024 + hm * 128:
                                       kc * 1024 + (hm + 1) * 128],
                                xst[i % 2][:, kc * TG:(kc + 1) * TG],
                                start=(kc == 0), stop=(kc == NKC - 1))
                        s_s1.inc(ins)
                        SCA.wait_ge(s_s1.h, S10 + gidx + 1)
                        if hm == 0:
                            if jq >= 2:
                                SCA.wait_ge(s_s2.h, S20 + 4 * (jq - 1))
                            SCA.wait_ge(s_wl[0].h, b1_val[i])
                        s_hev.inc(SCA.activation(
                            hT[pj][:, hm * TG:(hm + 1) * TG], bank, AF.Relu,
                            bias=b1sel[i % 2][:, (q * 8 + hm):
                                              (q * 8 + hm + 1)]))

                def emit_s2s(jq):
                    i, q = divmod(jq, 4)
                    p = jq % 2
                    pj = jq % 2
                    TEN.wait_ge(s_hev.h, HEV0 + (jq + 1) * 8)
                    for tc in range(4):
                        c2 = jq * 4 + tc
                        if c2 >= 2:
                            TEN.wait_ge(s_yev.h, YEV0 + c2 - 1)
                        else:
                            TEN.wait_ge(serV.h, b2add_val[NT - 2 + c2])
                        bank = s2p[tc % 2]
                        for nh in range(2):
                            half = bank[:, nh * TG:(nh + 1) * TG]
                            for hc in range(NKC):
                                ins = TEN.matmul(
                                    half,
                                    hT[pj][:, hc * TG + tc * 128:
                                           hc * TG + (tc + 1) * 128],
                                    wb2[p][:, hc * 1024 + nh * TG:
                                           hc * 1024 + (nh + 1) * TG],
                                    start=(hc == 0),
                                    stop=(hc == NKC - 1))
                        s_s2.inc(ins)
                        VEC.wait_ge(s_s2.h, S20 + c2 + 1)
                        ysl = y_acc[:, tc * 1024:(tc + 1) * 1024]
                        if q == 0:
                            if i >= 1 and tc == 0:
                                VEC.wait_ge(s_sc.h, SC0 + 64 * i)
                            ins2 = VEC.tensor_copy(out=ysl, in_=bank)
                        else:
                            ins2 = VEC.tensor_add(ysl, ysl, bank)
                        s_yev.inc(ins2)
                    if q == 3:
                        GPS.wait_ge(s_yev.h, YEV0 + 4 * (jq + 1))
                        for tc in range(4):
                            r0_ = i * TG + tc * 128
                            GPS.dma_start(
                                out=d_ysort[r0_:r0_ + 128, :],
                                in_=y_acc[:, tc * 1024:(tc + 1) * 1024],
                            ).then_inc(s_sc.h, 16)
                            s_sc.v += 16

                emit_wix(0)
                emit_wdyn(0)
                emit_wix(1)
                emit_wdyn(1)
                emit_xst(0)
                emit_xst(1)
                emit_s1s(0)
                for jq in range(NJQ):
                    if jq + 2 < NJQ:
                        emit_wix(jq + 2)
                        emit_wdyn(jq + 2)
                    if jq + 1 < NJQ:
                        if (jq + 1) % 4 == 0:
                            ii = (jq + 1) // 4
                            if ii + 1 < NTI:
                                emit_xst(ii + 1)
                        emit_s1s(jq + 1)
                    emit_s2s(jq)

                SC_ALL = s_sc.v
                for jt in range(NT):
                    if jt == 0:
                        GPS.wait_ge(s_sc.h, SC_ALL)
                    if jt >= 2:
                        GPS.wait_ge(s_xupd.h, XU0 + jt - 1)
                    GPS.wait_ge(serV.h, v_psr[NT - 1])
                    GPS.indirect_dma_start(
                        out=ystage[jt % 2], out_offset=None, in_=d_ysort,
                        in_offset=IndirectOffsetOnAxis(didxp[:, jt:jt + 1],
                                                       0),
                    ).then_inc(s_gws[jt % 2].h, 16)
                    s_gws[jt % 2].v += 16
                    VEC.wait_ge(s_gws[jt % 2].h, s_gws[jt % 2].v)
                    Xsl = X[:, jt * D:(jt + 1) * D]
                    s_xupd.inc(VEC.scalar_tensor_tensor(
                        out=Xsl, in0=ystage[jt % 2], scalar=0.5, in1=Xsl,
                        op0=ALU.mult, op1=ALU.add))

                TEN.wait_ge(s_yev.h, s_yev.v)
                SCA.wait_ge(s_yev.h, s_yev.v)
                VEC.wait_ge(s_hev.h, s_hev.v)
        return ret

    def emit_gla_call():
        for dst, srct in ((d_wqb, d_wq), (d_wkb, d_wk), (d_wvb, d_wv),
                          (d_wgb, d_wg), (d_wob, d_wo), (d_wgk1b, d_wgk1),
                          (d_wgk2b, d_wgk2)):
            GPS.dma_start(out=dst, in_=srct).then_inc(s_cast_g.h, 16)
            s_cast_g.v += 16
        with ExitStack() as gs:
            _uid[0] += 1
            _u = _uid[0]
            g = lambda n, sh, dt: gs.enter_context(
                nc.sbuf_tensor(f"{n}_u{_u}", sh, dt))[:]
            alloc_ln_bufs(gs, with_sqd=True, sqw_tiles=2)
            wqs = g("wqs", [128, NKC * 1024], BF16)
            wks = g("wks", [128, NKC * 1024], BF16)
            wvs = g("wvs", [128, NKC * 1024], BF16)
            wgs = g("wgs", [128, NKC * 1024], BF16)
            wos = g("wos", [128, NKC * 1024], BF16)
            wgk1s = g("wgk1s", [128, NKC * GLR], BF16)
            wgk2s = g("wgk2s", [GLR, D], BF16)
            xg = g("xg", [128, NKC * 256], BF16)
            qT = g("qT", [128, NHEAD * 256], BF16)
            kTf = g("kTf", [128, NHEAD * 256], BF16)
            z1s = g("z1s", [GLR, 256], BF16)
            gks = g("gks", [128, 2 * D], BF16)
            enbt = g("enbt", [128, D], BF16)
            ktt = g("ktt", [128, D], BF16)
            vtok = g("vtok", [128, D], BF16)
            gtok = g("gtok", [128, D], BF16)
            eb = g("eb", [128, D], BF16)
            enb = g("enb", [128, D], BF16)
            ebC = g("ebC", [128, 8], F32)
            qt = g("qt", [128, D], BF16)
            kt = g("kt", [128, D], BF16)
            smb = g("smb", [128, D], BF16)
            ofb = g("ofb", [128, D], BF16)
            oTa = g("oTa", [128, D], BF16)
            ssqa = g("ssqa", [128, 8], F32)
            lnva = g("lnva", [128, 8], F32)
            rsta = g("rsta", [128, 8], F32)
            Hf = g("Hf", [128, D], F32)
            Hb = g("Hb", [128, D], BF16)

            with ExitStack() as ps:
                pE = ps.enter_context(
                    nc.psum_tensor(f"pE_u{_u}", [128, 512], F32))[:]
                pA = ps.enter_context(
                    nc.psum_tensor(f"pA_u{_u}", [128, D], F32))[:]
                pB = ps.enter_context(
                    nc.psum_tensor(f"pB_u{_u}", [128, D], F32))[:]
                pC = ps.enter_context(
                    nc.psum_tensor(f"pC_u{_u}", [128, D], F32))[:]
                ptr = ps.enter_context(
                    nc.psum_tensor(f"ptr_u{_u}", [128, D], BF16))[:]

                silg = ln_bufs["sqw"][:, 0:D]
                SYN.wait_ge(serV.h, prep_done_v)
                SYN.wait_ge(s_cast_g.h, s_cast_g.v)
                for dst, srct in ((wqs, d_wqb), (wks, d_wkb), (wvs, d_wvb),
                                  (wgs, d_wgb), (wos, d_wob)):
                    dv = dst.rearrange("p (k m) -> p k m", k=NKC)
                    sv = srct.rearrange("(k p) m -> p k m", p=128)
                    SYN.dma_start(out=dv, in_=sv).then_inc(s_gld.h, 16)
                    s_gld.v += 16
                dv = wgk1s.rearrange("p (k m) -> p k m", k=NKC)
                sv = d_wgk1b.rearrange("(k p) m -> p k m", p=128)
                SYN.dma_start(out=dv, in_=sv).then_inc(s_gld.h, 16)
                s_gld.v += 16
                SYN.dma_start(out=wgk2s, in_=d_wgk2b).then_inc(s_gld.h, 16)
                s_gld.v += 16
                gld_all = s_gld.v

                VEC.memset(Hf, 0.0)
                hinit = barV(VEC.memset(Hb, 0.0))
                TEN.wait_ge(serV.h, hinit)
                TEN.wait_ge(serV.h, prep_done_v)
                SCA.wait_ge(serV.h, prep_done_v)

                XGE0, Z1T0, Z1E0, ZT0 = s_xgev.v, s_z1T.v, s_z1ev.v, s_zT.v
                TRT0g = s_trT.v
                XN0g, PRJ0, PEV0 = s_xn.v, s_projT.v, s_pev.v
                B0, S0_, O0, H0_, WO0, WOE0 = (s_b.v, s_S.v, s_o.v, s_H.v,
                                               s_wo.v, s_woev.v)
                gk_vals = []
                osq_vals = []
                state_vals = []
                ofb_vals = []
                sm_vals = []
                otev_vals = []
                sq_vals = []
                pev_cnt = 0

                x3 = xg.rearrange("p (j c) -> p j c", j=NKC)
                qT3 = qT.rearrange("p (h t) -> p h t", t=256)
                kT3 = kTf.rearrange("p (h t) -> p h t", t=256)

                def hview(t):
                    return t.rearrange("p (h c) -> p h c", c=128)

                for cg in range(16):
                    emit_ln_quad(cg * 2, 2, cg % 2, TRT0g)
                    for lt in range(2):
                        tt = cg * 2 + lt
                        k = tt % 2
                        emit_ln_xn(tt, cg % 2, lt, TRT0g)
                        TEN.wait_ge(s_xn.h, XN0g + tt + 1)
                        if cg > 0:
                            TEN.wait_ge(serV.h, otev_vals[2 * cg - 1])
                        if tt >= 1:
                            TEN.wait_ge(s_xgev.h, XGE0 + tt)
                        for j in range(NKC):
                            ins = TEN.transpose(
                                ptr[:, j * 128:(j + 1) * 128],
                                ln_bufs["xn"][k][:, j * 128:(j + 1) * 128], id_bf)
                        s_trT.inc(ins)
                        VEC.wait_ge(s_trT.h, s_trT.v)
                        if cg > 0 and lt == 0:
                            VEC.wait_ge(s_z1T.h, Z1T0 + cg)
                            VEC.wait_ge(s_projT.h, PRJ0 + 28 * cg)
                        dstv = x3[:, :, lt * 128:(lt + 1) * 128]
                        srcv = ptr.rearrange("p (j c) -> p j c", j=NKC)
                        s_xgev.inc(VEC.tensor_copy(out=dstv, in_=srcv))

                    TEN.wait_ge(s_gld.h, gld_all)
                    TEN.wait_ge(s_xgev.h, XGE0 + 2 * (cg + 1))
                    if cg > 0:
                        TEN.wait_ge(s_z1ev.h, Z1E0 + cg)
                        TEN.wait_ge(serV.h, state_vals[2 * cg - 1])
                        SCA.wait_ge(serV.h, state_vals[2 * cg - 1])
                    TEN.wait_ge(s_pev.h, PEV0 + pev_cnt)
                    for kc in range(NKC):
                        ins = TEN.matmul(pE[0:GLR, 0:256],
                                         wgk1s[:, kc * GLR:(kc + 1) * GLR],
                                         xg[:, kc * 256:(kc + 1) * 256],
                                         start=(kc == 0), stop=(kc == NKC - 1))
                    s_z1T.inc(ins)
                    s_z1T.wait(VEC)
                    s_z1ev.inc(VEC.tensor_copy(out=z1s, in_=pE[0:GLR, 0:256]))

                    proj_groups = []
                    for wsb, dstT, scale in ((wqs, qT, HKS), (wks, kTf, 1.0)):
                        for mq in range(8):
                            proj_groups.append((wsb, dstT, scale, mq))

                    def emit_proj_group(idx):
                        nonlocal pev_cnt
                        wsb, dstT, scale, mq = proj_groups[idx]
                        region = (pE[:, 0:256] if idx % 2 == 0
                                  else pB[:, 512:768])
                        if pev_cnt >= 2:
                            TEN.wait_ge(s_pev.h, PEV0 + pev_cnt - 1)
                        for kc in range(NKC):
                            ins = TEN.matmul(
                                region,
                                wsb[:, kc * 1024 + mq * 128:
                                    kc * 1024 + (mq + 1) * 128],
                                xg[:, kc * 256:(kc + 1) * 256],
                                start=(kc == 0), stop=(kc == NKC - 1))
                        s_projT.inc(ins)
                        s_projT.wait(SCA)
                        s_pev.inc(SCA.activation(
                            dstT[:, mq * 256:(mq + 1) * 256], region,
                            AF.Copy, scale=scale))
                        pev_cnt += 1

                    for tc in range(2):
                        ch_prev_reader = 2 * (cg - 1) + tc + 1
                        TEN.wait_ge(s_z1ev.h, Z1E0 + cg + 1)
                        if cg > 0 and tc == 0:
                            TEN.wait_ge(s_woev.h, WOE0 + 2 * cg)
                        if tc == 1:
                            TEN.wait_ge(serV.h, gk_vals[2 * cg])
                        for nh in range(2):
                            half = pC[:, nh * TG:(nh + 1) * TG]
                            TEN.matmul(half, z1s[:, tc * 128:(tc + 1) * 128],
                                       wgk2s[:, nh * TG:(nh + 1) * TG],
                                       start=True, stop=False)
                            ins = TEN.matmul(half, ones,
                                             bgk2r[:, nh * TG:(nh + 1) * TG],
                                             start=False, stop=True)
                        s_zT.inc(ins)
                        for gi in range(8):
                            emit_proj_group(tc * 8 + gi)
                        SCA.wait_ge(s_zT.h, ZT0 + 2 * cg + tc + 1)
                        barS(SCA.activation(ln_bufs["sqd"], pC, AF.Abs))
                        barS(SCA.activation(ln_bufs["sqd"], ln_bufs["sqd"],
                                            AF.Exp, scale=-1.0))
                        v_zs = barS(SCA.activation(ln_bufs["sqd"], ln_bufs["sqd"],
                                                   AF.Ln, bias=cb[:, 1:2]))
                        VEC.wait_ge(serS.h, v_zs)
                        VEC.wait_ge(s_zT.h, ZT0 + 2 * cg + tc + 1)
                        if cg > 0:
                            VEC.wait_ge(s_b.h, B0 + ch_prev_reader)
                        gk_sl = gks[:, tc * D:(tc + 1) * D]
                        barV(VEC.tensor_scalar(out=gk_sl, in0=pC, scalar1=0.0,
                                               scalar2=None, op0=ALU.min))
                        v_gk = barV(VEC.tensor_sub(gk_sl, gk_sl, ln_bufs["sqd"]))
                        gk_vals.append(v_gk)

                    for lt in range(2):
                        ch = cg * 2 + lt
                        TEN.wait_ge(serV.h, gk_vals[2 * cg + lt])
                        if ch > 0:
                            TEN.wait_ge(serV.h, state_vals[ch - 1])
                            TEN.wait_ge(serV.h, ofb_vals[ch - 1])
                        TEN.wait_ge(s_pev.h, PEV0 + pev_cnt)
                        for nh in range(2):
                            ins = TEN.matmul(
                                pB[:, nh * TG:(nh + 1) * TG], u_bf,
                                gks[:, lt * D + nh * TG:lt * D + (nh + 1) * TG],
                                start=True, stop=True)
                        for h in range(NHEAD):
                            ins = TEN.matmul(
                                pA[:, h * 128:(h + 1) * 128],
                                gks[:, lt * D + h * 128:lt * D + (h + 1) * 128],
                                u_bf, start=True, stop=True)
                        s_b.inc(ins)
                        s_b.wait(SCA)
                        SCA.activation(enbt, pB, AF.Exp, scale=-1.0 / GLN)
                        SCA.activation(eb, pA, AF.Exp, scale=1.0 / GLN)
                        SCA.activation(enb, pA, AF.Exp, scale=-1.0 / GLN)
                        v_ebs = barS(SCA.activation(
                            ebC, hview(pA)[:, :, 127], AF.Exp, scale=1.0 / GLN))
                        if ch > 0:
                            SCA.wait_ge(s_o.h, O0 + ch)
                            SCA.wait_ge(s_H.h, H0_ + ch)
                            VEC.wait_ge(s_o.h, O0 + ch)
                            VEC.wait_ge(s_H.h, H0_ + ch)
                        g_pev_val = [0]
                        for wsb, kind in ((wvs, "v"), (wgs, "g"), (wks, "k")):
                            for nh in range(2):
                                TEN.wait_ge(s_pev.h, PEV0 + pev_cnt)
                                for kc in range(NKC):
                                    ins = TEN.matmul(
                                        pE,
                                        xg[:, kc * 256 + lt * 128:
                                           kc * 256 + (lt + 1) * 128],
                                        wsb[:, kc * 1024 + nh * TG:
                                            kc * 1024 + (nh + 1) * TG],
                                        start=(kc == 0), stop=(kc == NKC - 1))
                                s_projT.inc(ins)
                                if kind == "k":
                                    s_projT.wait(VEC)
                                    VEC.wait_ge(serS.h, v_ebs)
                                    ins2 = VEC.tensor_mul(
                                        ktt[:, nh * TG:(nh + 1) * TG], pE,
                                        enbt[:, nh * TG:(nh + 1) * TG])
                                    s_pev.inc(ins2)
                                else:
                                    dst = vtok if kind == "v" else gtok
                                    s_projT.wait(SCA)
                                    s_pev.inc(SCA.activation(
                                        dst[:, nh * TG:(nh + 1) * TG], pE,
                                        AF.Copy))
                                    if kind == "g":
                                        g_pev_val[0] = s_pev.v
                                pev_cnt += 1
                        if ch > 0:
                            SCA.wait_ge(serV.h, ofb_vals[ch - 1])
                        SCA.wait_ge(s_pev.h, g_pev_val[0])
                        barS(SCA.activation(silg, gtok, AF.Exp, scale=-1.0))
                        barS(SCA.activation(silg, silg, AF.Ln, bias=cb[:, 1:2]))
                        v_sg = barS(SCA.activation(silg, silg, AF.Exp,
                                                   scale=-1.0))
                        sq_vals.append(v_sg)
                        VEC.wait_ge(serS.h, v_sg)
                        for h in range(NHEAD):
                            sl = slice(h * 128, (h + 1) * 128)
                            ins = VEC.tensor_mul(gtok[:, sl], gtok[:, sl], gnwb)
                        barV(ins)
                        barV(VEC.tensor_mul(silg, gtok, silg))
                        VEC.wait_ge(serS.h, v_ebs)
                        VEC.tensor_mul(hview(qt), qT3[:, :, lt * 128:(lt + 1) * 128],
                                       hview(eb))
                        v_qt = barV(VEC.tensor_mul(
                            hview(kt), kT3[:, :, lt * 128:(lt + 1) * 128],
                            hview(enb)))
                        TEN.wait_ge(serV.h, v_qt)
                        TEN.wait_ge(serV.h, gk_vals[2 * cg + 1])
                        if ch > 0:
                            TEN.wait_ge(s_woev.h, WOE0 + ch)
                        for h in range(NHEAD):
                            ins = TEN.matmul(
                                pC[:, h * 128:(h + 1) * 128],
                                kt[:, h * 128:(h + 1) * 128],
                                qt[:, h * 128:(h + 1) * 128],
                                start=True, stop=True)
                        s_S.inc(ins)
                        s_S.wait(VEC)
                        if ch > 0:
                            VEC.wait_ge(serS.h, osq_vals[ch - 1])
                        for h in range(NHEAD):
                            ins = VEC.tensor_mul(smb[:, h * 128:(h + 1) * 128],
                                                 pC[:, h * 128:(h + 1) * 128],
                                                 u_bf)
                        v_sm = barV(ins)
                        sm_vals.append(v_sm)
                        if ch > 0:
                            TEN.wait_ge(serV.h, state_vals[ch - 1])
                        for h in range(NHEAD):
                            sl = slice(h * 128, (h + 1) * 128)
                            ins = TEN.matmul(pB[:, sl], ktt[:, sl], vtok[:, sl],
                                             start=True, stop=True)
                        s_H.inc(ins)
                        TEN.wait_ge(serV.h, v_sm)
                        for h in range(NHEAD):
                            sl = slice(h * 128, (h + 1) * 128)
                            TEN.matmul(pA[:, sl], smb[:, sl], vtok[:, sl],
                                       start=True, stop=False)
                            ins = TEN.matmul(pA[:, sl], qt[:, sl], Hb[:, sl],
                                             start=False, stop=True)
                        s_o.inc(ins)
                        s_o.wait(SCA)
                        if ch > 0:
                            SCA.wait_ge(serV.h, sm_vals[ch - 1] if False else v_sm)
                        v_osq = barS(SCA.activation(smb, pA, AF.Square))
                        osq_vals.append(v_osq)
                        VEC.wait_ge(serS.h, v_osq)
                        v_ssq = barV(VEC.tensor_reduce(
                            ssqa, hview(smb), axis=AX, op=ALU.add))
                        SCA.wait_ge(serV.h, v_ssq)
                        barS(SCA.activation(lnva, ssqa, AF.Ln, scale=1.0 / DH,
                                            bias=cb[:, 0:1]))
                        v_rst = barS(SCA.activation(rsta, lnva, AF.Exp,
                                                    scale=-0.5))
                        VEC.wait_ge(serS.h, v_rst)
                        for h in range(NHEAD):
                            sl = slice(h * 128, (h + 1) * 128)
                            ins = VEC.scalar_tensor_tensor(
                                out=ofb[:, sl], in0=pA[:, sl],
                                scalar=rsta[:, h:h + 1], in1=silg[:, sl],
                                op0=ALU.mult, op1=ALU.mult)
                        v_ofb = barV(ins)
                        ofb_vals.append(v_ofb)
                        ln_bufs["sq_war"][0] = v_ofb
                        s_H.wait(VEC)
                        barV(VEC.tensor_add(Hf, Hf, pB))
                        for h in range(NHEAD):
                            sl = slice(h * 128, (h + 1) * 128)
                            ins = VEC.tensor_scalar(
                                out=Hf[:, sl], in0=Hf[:, sl],
                                scalar1=ebC[:, h:h + 1], scalar2=None,
                                op0=ALU.mult)
                        barV(ins)
                        s_o.wait(VEC)
                        v_state = barV(VEC.tensor_copy(out=Hb, in_=Hf))
                        state_vals.append(v_state)
                        TEN.wait_ge(serV.h, v_ofb)
                        if ch > 0:
                            TEN.wait_ge(serV.h, otev_vals[ch - 1])
                        for h in range(NHEAD):
                            sl = slice(h * 128, (h + 1) * 128)
                            ins = TEN.transpose(ptr[:, sl], ofb[:, sl], id_bf)
                        s_otr.inc(ins)
                        s_otr.wait(VEC)
                        v_ot = barV(VEC.tensor_copy(out=oTa, in_=ptr))
                        otev_vals.append(v_ot)
                        TEN.wait_ge(serV.h, v_ot)
                        for nh in range(2):
                            half = pC[:, nh * TG:(nh + 1) * TG]
                            for hh in range(NHEAD):
                                ins = TEN.matmul(
                                    half, oTa[:, hh * 128:(hh + 1) * 128],
                                    wos[:, hh * 1024 + nh * TG:
                                        hh * 1024 + (nh + 1) * TG],
                                    start=(hh == 0), stop=(hh == NHEAD - 1))
                        s_wo.inc(ins)
                        s_wo.wait(VEC)
                        Xsl = X[:, ch * D:(ch + 1) * D]
                        s_woev.inc(VEC.tensor_add(Xsl, Xsl, pC))
                TEN.wait_ge(s_woev.h, s_woev.v)
                SCA.wait_ge(s_woev.h, s_woev.v)

    def x_fence():
        for eng in (TEN, VEC, SCA, SYN):
            for s in (s_xin2[0], s_yev, s_woev, s_xupd):
                s.wait(eng)

    x_fence()
    last_ffn = None
    if "ffn1" in phases:
        last_ffn = emit_ffn_call()
        x_fence()
    if "gla" in phases:
        emit_gla_call()
        last_ffn = None
        x_fence()
    if "ffn2" in phases:
        last_ffn = emit_ffn_call()
        x_fence()

    for s in (s_xin2[0], s_yev, s_woev):
        s.wait(GPS)
    for jt in range(NT):
        if last_ffn is not None:
            GPS.wait_ge(s_xupd.h, last_ffn["XU0"] + jt + 1)
        GPS.dma_start(out=d_y[jt * 128:(jt + 1) * 128, :],
                      in_=X[:, jt * D:(jt + 1) * D]).then_inc(s_yw2[0].h, 16)
        s_yw2[0].v += 16
    SCA.wait_ge(s_yw2[0].h, s_yw2[0].v)

    ctx.close()
    return nc


def _in_map(inputs, core):
    a = lambda v: np.ascontiguousarray(np.asarray(v, dtype=np.float32))
    return {
        "x": a(inputs["L0_attn_x"][core]),
        "w1": a(inputs["L0_ffn_w1"]),
        "b1": a(inputs["L0_ffn_b1"]),
        "w2": a(inputs["L0_ffn_w2"]),
        "b2": a(inputs["L0_ffn_b2"]),
        "gw": a(inputs["L0_ffn_gate_w"]),
        "gb": a(inputs["L0_ffn_gate_b"]),
        "wq": a(inputs["L0_attn_Wq"]),
        "wk": a(inputs["L0_attn_Wk"]),
        "wv": a(inputs["L0_attn_Wv"]),
        "wg": a(inputs["L0_attn_Wg"]),
        "wo": a(inputs["L0_attn_Wo"]),
        "wgk1": a(inputs["L0_attn_Wgk1"]),
        "wgk2": a(inputs["L0_attn_Wgk2"]),
        "bgk2": a(inputs["L0_attn_bgk2"]),
        "gnw": a(inputs["L0_attn_g_norm_w"]),
    }


def kernel(**inputs):
    n = 8
    nc = build_core_program()
    in_maps = [_in_map(inputs, c) for c in range(n)]
    res = run_bass_kernel_spmd(nc, in_maps, core_ids=list(range(n)))
    return np.stack([res.results[c]["y"] for c in range(n)], axis=0)


# version 15 (final)
# speedup vs baseline: 1.0193x; 1.0117x over previous
import numpy as np
import ml_dtypes
from contextlib import ExitStack

import concourse.bass as bass
import concourse.mybir as mybir
from concourse.bass_utils import run_bass_kernel_spmd
from concourse.bass_core_rust import IndirectOffsetOnAxis
from concourse.expressions import RuntimeValue

dsl = bass.ds

F32 = mybir.dt.float32
BF16 = mybir.dt.bfloat16
AF = mybir.ActivationFunctionType
ALU = mybir.AluOpType
AX = mybir.AxisListType.X

T_TOK = 4096
D = 1024
E = 8
HID = 4096
NHEAD = 8
DH = 128
GLR = 16
GLN = 16.0
EPS = 1e-5
NT = T_TOK // 128
NTG = 8
TG = 512
NKC = D // 128
HKS = float(DH) ** -0.5
BIGI = 1024.0


class Sem:
    __slots__ = ("h", "v")

    def __init__(self, nc, ctx, name):
        self.h = ctx.enter_context(nc.semaphore(name))
        self.v = 0

    def inc(self, ins, n=1):
        ins.then_inc(self.h, n)
        self.v += n
        return self.v

    def wait(self, eng, v=None):
        eng.wait_ge(self.h, self.v if v is None else v)


def build_core_program(phases=("ffn1", "gla", "ffn2")):
    nc = bass.Bass("TRN2", target_bir_lowering=False)
    TEN, VEC, SCA, SYN, GPS = nc.tensor, nc.vector, nc.scalar, nc.sync, nc.gpsimd

    d_x = nc.dram_tensor("x", [T_TOK, D], F32, kind="ExternalInput").ap()
    d_w1 = nc.dram_tensor("w1", [E, HID, D], F32, kind="ExternalInput").ap()
    d_b1 = nc.dram_tensor("b1", [E, HID], F32, kind="ExternalInput").ap()
    d_w2 = nc.dram_tensor("w2", [E, D, HID], F32, kind="ExternalInput").ap()
    d_b2 = nc.dram_tensor("b2", [E, D], F32, kind="ExternalInput").ap()
    d_gw = nc.dram_tensor("gw", [E, D], F32, kind="ExternalInput").ap()
    d_gb = nc.dram_tensor("gb", [E], F32, kind="ExternalInput").ap()
    d_wq = nc.dram_tensor("wq", [D, D], F32, kind="ExternalInput").ap()
    d_wk = nc.dram_tensor("wk", [D, D], F32, kind="ExternalInput").ap()
    d_wv = nc.dram_tensor("wv", [D, D], F32, kind="ExternalInput").ap()
    d_wg = nc.dram_tensor("wg", [D, D], F32, kind="ExternalInput").ap()
    d_wo = nc.dram_tensor("wo", [D, D], F32, kind="ExternalInput").ap()
    d_wgk1 = nc.dram_tensor("wgk1", [D, GLR], F32, kind="ExternalInput").ap()
    d_wgk2 = nc.dram_tensor("wgk2", [GLR, D], F32, kind="ExternalInput").ap()
    d_bgk2 = nc.dram_tensor("bgk2", [D], F32, kind="ExternalInput").ap()
    d_gnw = nc.dram_tensor("gnw", [DH], F32, kind="ExternalInput").ap()
    d_y = nc.dram_tensor("y", [T_TOK, D], F32, kind="ExternalOutput").ap()

    d_w1b = nc.dram_tensor("w1b", [E, HID, D], BF16, kind="Internal").ap()
    d_w2b = nc.dram_tensor("w2b", [E, D, HID], BF16, kind="Internal").ap()
    d_wqb = nc.dram_tensor("wqb", [D, D], BF16, kind="Internal").ap()
    d_wkb = nc.dram_tensor("wkb", [D, D], BF16, kind="Internal").ap()
    d_wvb = nc.dram_tensor("wvb", [D, D], BF16, kind="Internal").ap()
    d_wgb = nc.dram_tensor("wgb", [D, D], BF16, kind="Internal").ap()
    d_wob = nc.dram_tensor("wob", [D, D], BF16, kind="Internal").ap()
    d_wgk1b = nc.dram_tensor("wgk1b", [D, GLR], BF16, kind="Internal").ap()
    d_wgk2b = nc.dram_tensor("wgk2b", [GLR, D], BF16, kind="Internal").ap()
    d_b2b = nc.dram_tensor("b2b", [E, D], BF16, kind="Internal").ap()
    d_xb = nc.dram_tensor("xb16", [T_TOK, D], BF16, kind="Internal").ap()
    d_xnt = nc.dram_tensor("xnt", [NKC, 128, T_TOK], BF16, kind="Internal").ap()
    d_xns = nc.dram_tensor("xns", [T_TOK + 128, D], BF16, kind="Internal").ap()
    d_xs = nc.dram_tensor("xs", [15 * TG, D], BF16, kind="Internal").ap()
    d_ysort = nc.dram_tensor("ysort", [15 * TG, D], BF16, kind="Internal").ap()
    d_srq = nc.dram_tensor("srq", [15 * TG, 1], mybir.dt.int32, kind="Internal").ap()
    d_w1bT = nc.dram_tensor("w1bT", [E, 4, 128, 8 * 1024], BF16,
                            kind="Internal").ap()
    d_w2bT = nc.dram_tensor("w2bT", [E, 4, 128, 8 * 1024], BF16,
                            kind="Internal").ap()
    d_b1t = nc.dram_tensor("b1t", [E, 128, 32], F32, kind="Internal").ap()

    c_id_bf = nc.inline_tensor(np.eye(128, dtype=ml_dtypes.bfloat16), "c_id_bf").ap()
    c_id_f32 = nc.inline_tensor(np.eye(128, dtype=np.float32), "c_id_f32").ap()
    c_ones = nc.inline_tensor(np.ones((1, 128), dtype=ml_dtypes.bfloat16), "c_ones").ap()
    c_iota = nc.inline_tensor(
        (np.arange(8, dtype=np.float32) + BIGI)[None, :].repeat(128, 0), "c_iota").ap()
    c_u = nc.inline_tensor(
        np.triu(np.ones((128, 128), dtype=np.float32)).astype(ml_dtypes.bfloat16),
        "c_u").ap()
    c_cb = nc.inline_tensor(
        np.array([[EPS, 1.0]], dtype=np.float32).repeat(128, 0), "c_cb").ap()
    c_onec = nc.inline_tensor(np.ones((128, 1), dtype=ml_dtypes.bfloat16),
                              "c_onec").ap()
    c_us = nc.inline_tensor(
        np.triu(np.ones((128, 128), dtype=np.float32), 1).astype(
            ml_dtypes.bfloat16), "c_us").ap()
    c_i128 = nc.inline_tensor(np.arange(128, dtype=np.float32)[:, None],
                              "c_i128").ap()

    ctx = ExitStack()
    sem = lambda n: Sem(nc, ctx, n)
    sbuf = lambda n, sh, dt: ctx.enter_context(nc.sbuf_tensor(n, sh, dt))[:]

    X = sbuf("X", [128, NT * D], BF16)
    id_bf = sbuf("id_bf", [128, 128], BF16)
    id_f32 = sbuf("id_f32", [128, 128], F32)
    ones = sbuf("ones", [1, 128], BF16)
    iotaB = sbuf("iotaB", [128, 8], F32)
    u_bf = sbuf("u_bf", [128, 128], BF16)
    masks = sbuf("masks", [128, NT * E], F32)
    b1c = sbuf("b1c", [128, 32 * E], F32)
    gbr = sbuf("gbr", [1, 8], BF16)
    bgk2r = sbuf("bgk2r", [1, D], BF16)
    gwT = sbuf("gwT", [128, NKC * E], BF16)
    gnwb = sbuf("gnwb", [128, 128], F32)
    cb = sbuf("cb", [128, 2], F32)
    onec = sbuf("onec", [128, 1], BF16)
    us_bf = sbuf("us_bf", [128, 128], BF16)
    i128 = sbuf("i128", [128, 1], F32)


    serV, serS = sem("serV"), sem("serS")

    def barV(ins):
        serV.inc(ins)
        serV.wait(VEC)
        return serV.v

    def barS(ins):
        serS.inc(ins)
        serS.wait(SCA)
        return serS.v

    s_cast_g = sem("s_cast_g")
    s_cast_b2 = sem("s_cast_b2")
    s_b2l = sem("s_b2l")
    s_cld = sem("s_cld")
    s_prep = sem("s_prep")
    s_xin2 = [sem("s_xin0"), sem("s_xin1")]
    s_xn = sem("s_xn")
    s_trT = sem("s_trT")
    s_psTev = sem("s_psTev")
    s_xw = [sem("s_xw0"), sem("s_xw1")]
    s_lgT = sem("s_lgT")
    s_wl = [sem("s_wl0"), sem("s_wl1")]
    s_xl = [sem("s_xl0"), sem("s_xl1"), sem("s_xl2")]
    s_s1 = sem("s_s1")
    s_hev = sem("s_hev")
    s_s2 = sem("s_s2")
    s_yev = sem("s_yev")
    s_gld = sem("s_gld")
    s_xgev = sem("s_xgev")
    s_pev = sem("s_pev")
    s_z1ev = sem("s_z1ev")
    s_projT = sem("s_projT")
    s_z1T = sem("s_z1T")
    s_zT = sem("s_zT")
    s_b = sem("s_b")
    s_eb = sem("s_eb")
    s_ehb = sem("s_ehb")
    s_qk = sem("s_qk")
    s_kh = sem("s_kh")
    s_S = sem("s_S")
    s_Sm = sem("s_Sm")
    s_tr = sem("s_tr")
    s_trev = sem("s_trev")
    s_o = sem("s_o")
    s_H = sem("s_H")
    s_silg = sem("s_silg")
    s_ofin = sem("s_ofin")
    s_otr = sem("s_otr")
    s_otev = sem("s_otev")
    s_wo = sem("s_wo")
    s_woev = sem("s_woev")
    s_yw2 = [sem("s_yw0"), sem("s_yw1")]
    s_cw1e = [sem(f"s_cw1_{e}") for e in range(E)]
    s_cw2e = [sem(f"s_cw2_{e}") for e in range(E)]
    s_rl4 = [sem(f"s_rl{i}") for i in range(4)]
    s_rw4 = [sem(f"s_rw{i}") for i in range(4)]
    s_gws = [sem("s_gws0"), sem("s_gws1")]
    s_gq = sem("s_gq")
    s_sc = sem("s_sc")
    s_xupd = sem("s_xupd")
    s_b1g = sem("s_b1g")
    s_scat = sem("s_scat")
    s_zf = sem("s_zf")

    s_cast_x = sem("s_cast_x")
    GPS.dma_start(out=d_xb, in_=d_x).then_inc(s_cast_x.h, 16)
    s_cast_x.v += 16
    GPS.dma_start(out=d_b2b, in_=d_b2).then_inc(s_cast_b2.h, 16)
    s_cast_b2.v += 16
    for e in range(E):
        GPS.dma_start(out=d_w1b[e], in_=d_w1[e]).then_inc(s_cw1e[e].h, 16)
        s_cw1e[e].v += 16
        GPS.dma_start(out=d_w2b[e], in_=d_w2[e]).then_inc(s_cw2e[e].h, 16)
        s_cw2e[e].v += 16

    for dst, src in ((id_bf, c_id_bf), (id_f32, c_id_f32), (ones, c_ones),
                     (iotaB, c_iota), (u_bf, c_u), (cb, c_cb),
                     (onec, c_onec), (us_bf, c_us), (i128, c_i128)):
        SYN.dma_start(out=dst, in_=src).then_inc(s_cld.h, 16)
        s_cld.v += 16

    with ExitStack() as p0:
        p_sb = lambda n, sh, dt: p0.enter_context(nc.sbuf_tensor(n, sh, dt))[:]
        b1n = p_sb("b1n", [8, HID], F32)
        gbf = p_sb("gbf", [1, 8], F32)
        bgk2f = p_sb("bgk2f", [1, D], F32)
        gwn = p_sb("gwn", [8, D], F32)
        gwnb = p_sb("gwnb", [8, D], BF16)
        gnr = p_sb("gnr", [1, DH], F32)
        gnrb = p_sb("gnrb", [1, DH], BF16)
        ps_prep = p0.enter_context(nc.psum_tensor("ps_prep", [128, 32], F32))[:]
        ps_gw = p0.enter_context(nc.psum_tensor("ps_gw", [128, 64], BF16))[:]
        ps_gn = p0.enter_context(nc.psum_tensor("ps_gn", [128, 128], F32))[:]

        for dst, srct in ((b1n, d_b1), (gbf, d_gb[None, :]),
                          (bgk2f, d_bgk2[None, :]),
                          (gwn, d_gw), (gnr, d_gnw[None, :])):
            SYN.dma_start(out=dst, in_=srct).then_inc(s_cld.h, 16)
            s_cld.v += 16
        ld_all = s_cld.v

        VEC.wait_ge(s_cld.h, ld_all)
        VEC.tensor_copy(out=gbr, in_=gbf)
        VEC.tensor_copy(out=bgk2r, in_=bgk2f)
        VEC.tensor_copy(out=gwnb, in_=gwn)
        casts_done = barV(VEC.tensor_copy(out=gnrb, in_=gnr))

        TEN.wait_ge(s_cld.h, ld_all)
        TEN.wait_ge(serV.h, casts_done)
        for blk in range(8):
            for j in range(4):
                c = blk * 4 + j
                ins = TEN.transpose(ps_prep[:, j * 8:(j + 1) * 8],
                                    b1n[0:8, c * 128:(c + 1) * 128],
                                    id_f32[0:8, 0:8])
            s_prep.inc(ins)
            s_prep.wait(VEC)
            ev = barV(VEC.tensor_copy(out=b1c[:, blk * 32:(blk + 1) * 32],
                                      in_=ps_prep))
            TEN.wait_ge(serV.h, ev)
        for kc in range(NKC):
            ins = TEN.transpose(ps_gw[:, kc * 8:(kc + 1) * 8],
                                gwnb[0:8, kc * 128:(kc + 1) * 128],
                                id_bf[0:8, 0:8])
        s_prep.inc(ins)
        s_prep.wait(VEC)
        ev = barV(VEC.tensor_copy(out=gwT, in_=ps_gw))
        TEN.wait_ge(serV.h, ev)
        s_prep.inc(TEN.matmul(ps_gn, ones, gnrb, start=True, stop=True))
        s_prep.wait(VEC)
        prep_done_v = barV(VEC.tensor_copy(out=gnwb, in_=ps_gn))
        TEN.wait_ge(serV.h, prep_done_v)
        SCA.wait_ge(serV.h, prep_done_v)
        s_b1t = sem("s_b1t")
        b1t_sb = p_sb("b1t_sb", [128, 32 * E], F32)
        v_b1t = barV(VEC.tensor_copy(
            out=b1t_sb.rearrange("p (e c) -> p e c", e=E),
            in_=b1c.rearrange("p (c e) -> p e c", e=E)))
        SCA.wait_ge(serV.h, v_b1t)
        SCA.dma_start(out=d_b1t.rearrange("e p c -> p e c"),
                      in_=b1t_sb.rearrange("p (e c) -> p e c", e=E)
                      ).then_inc(s_b1t.h, 16)
        s_b1t.v += 16

    SYN.wait_ge(s_cast_x.h, 16)
    Xsrc = d_xb.rearrange("(t p) d -> p t d", p=128)
    Xdst = X.rearrange("p (t d) -> p t d", d=D)
    SYN.dma_start(out=Xdst, in_=Xsrc).then_inc(s_xin2[0].h, 16)
    s_xin2[0].v += 16

    ln_bufs = {}
    _uid = [0]

    def alloc_ln_bufs(scope, with_sqd=False, sqw_tiles=4):
        _uid[0] += 1
        u = _uid[0]
        g = lambda n, sh, dt: scope.enter_context(
            nc.sbuf_tensor(f"{n}_u{u}", sh, dt))[:]
        for nme in ("scol", "nmc", "m2c", "ssqc", "varc", "lnvc", "rstdc",
                    "negmrc"):
            ln_bufs[nme] = [g(f"{nme}{i}", [128, 4], F32) for i in range(2)]
        nsq = sqw_tiles + (1 if with_sqd else 0)
        ln_bufs["sqw"] = g("sqw", [128, nsq * D], BF16)
        ln_bufs["sqd"] = (ln_bufs["sqw"][:, sqw_tiles * D:nsq * D]
                          if with_sqd else None)
        ln_bufs["xn"] = [g(f"xn{i}", [128, D], BF16) for i in range(2)]
        ln_bufs["lgf"] = g("lgf", [128, 8], F32)
        ln_bufs["tmask"] = g("tmask", [128, 8], F32)
        ln_bufs["idxm"] = g("idxm", [128, 1], F32)
        ln_bufs["sq_war"] = [0]
        ln_bufs["xn_gate"] = None

    def emit_ln_xn(tt, k, i, trt0):
        lb = ln_bufs
        if tt >= 2:
            VEC.wait_ge(s_trT.h, trt0 + tt - 1)
            if lb["xn_gate"] is not None:
                lb["xn_gate"](tt)
        s_xn.inc(VEC.tensor_scalar(
            out=lb["xn"][tt % 2], in0=X[:, tt * D:(tt + 1) * D],
            scalar1=lb["rstdc"][k][:, i:i + 1],
            scalar2=lb["negmrc"][k][:, i:i + 1],
            op0=ALU.mult, op1=ALU.add))

    def emit_ln_quad(tt0, nq, k, trt0):
        lb = ln_bufs
        Xq = X[:, tt0 * D:(tt0 + nq) * D]
        X3 = Xq.rearrange("p (t d) -> p t d", d=D)
        sc = lb["scol"][k][:, 0:nq]
        barV(VEC.tensor_reduce(sc, X3, axis=AX, op=ALU.add))
        VEC.tensor_scalar(out=lb["nmc"][k][:, 0:nq], in0=sc, scalar1=-1.0 / D,
                          scalar2=None, op0=ALU.mult)
        barV(VEC.tensor_scalar(out=lb["m2c"][k][:, 0:nq], in0=sc,
                               scalar1=None, scalar2=1.0 / (D * D),
                               op0=ALU.mult, op1=ALU.mult)
             if False else
             VEC.tensor_tensor(lb["m2c"][k][:, 0:nq], sc, sc, op=ALU.mult))
        barV(VEC.tensor_scalar(out=lb["m2c"][k][:, 0:nq],
                               in0=lb["m2c"][k][:, 0:nq], scalar1=1.0 / (D * D),
                               scalar2=None, op0=ALU.mult))
        SCA.wait_ge(serV.h, lb["sq_war"][0])
        v_sq = barS(SCA.activation(lb["sqw"][:, 0:nq * D], Xq, AF.Square))
        VEC.wait_ge(serS.h, v_sq)
        sq3 = lb["sqw"][:, 0:nq * D].rearrange("p (t d) -> p t d", d=D)
        v_red = barV(VEC.tensor_reduce(lb["ssqc"][k][:, 0:nq], sq3, axis=AX,
                                       op=ALU.add))
        lb["sq_war"][0] = v_red
        v_var = barV(VEC.scalar_tensor_tensor(
            out=lb["varc"][k][:, 0:nq], in0=lb["ssqc"][k][:, 0:nq],
            scalar=1.0 / D, in1=lb["m2c"][k][:, 0:nq],
            op0=ALU.mult, op1=ALU.subtract))
        SCA.wait_ge(serV.h, v_var)
        barS(SCA.activation(lb["lnvc"][k][:, 0:nq], lb["varc"][k][:, 0:nq],
                            AF.Ln, bias=cb[:, 0:1]))
        v_rstd = barS(SCA.activation(lb["rstdc"][k][:, 0:nq],
                                     lb["lnvc"][k][:, 0:nq], AF.Exp, scale=-0.5))
        VEC.wait_ge(serS.h, v_rstd)
        barV(VEC.tensor_mul(lb["negmrc"][k][:, 0:nq], lb["nmc"][k][:, 0:nq],
                            lb["rstdc"][k][:, 0:nq]))


    _restaged = [False]

    def emit_ffn_call():
        ret = {}
        with ExitStack() as fs:
            _uid[0] += 1
            _u = _uid[0]
            g = lambda n, sh, dt: fs.enter_context(
                nc.sbuf_tensor(f"{n}_u{_u}", sh, dt))[:]
            alloc_ln_bufs(fs)
            xnts = [g(f"xnts{i}", [128, D], BF16) for i in range(2)]
            b2n = g("b2n", [8, D], BF16)
            mTb = g("mTb", [8, 128], BF16)
            wb1 = [g(f"wb1_{i}", [128, NKC * 1024], BF16) for i in range(2)]
            wb2 = [g(f"wb2_{i}", [128, NKC * 1024], BF16) for i in range(2)]
            xst = [g(f"xst{i}", [128, NKC * TG], BF16) for i in range(2)]
            hT = [g(f"hT{i}", [128, NKC * TG], BF16) for i in range(2)]
            y_acc = g("y_acc", [128, 4 * D], BF16)
            b1sel = [g(f"b1sel{i}", [128, 32], F32) for i in range(2)]
            didxq = g("didxq", [128, NT], mybir.dt.int32)
            tokall = g("tokall", [128, NT], mybir.dt.int32)
            srq_sb = g("srq_sb", [128, 60], mybir.dt.int32)
            zi = g("zi", [128, 60], mybir.dt.int32)
            gst = [g(f"gst{i}", [128, D], BF16) for i in range(2)]
            ystage = [g(f"ystage{i}", [128, D], BF16) for i in range(2)]
            selqi = g("selqi", [1, 16], mybir.dt.int32)
            maskb = g("maskb", [128, NT * E], BF16)
            ibx = g("ibx", [128, 912], F32)
            didxp = g("didxp", [128, NT], mybir.dt.int32)
            selbase = g("selbase", [128, 16], F32)
            wixi = [g(f"wixi{i}", [128, 2], mybir.dt.int32) for i in range(2)]
            with ExitStack() as ps:
                psT = ps.enter_context(nc.psum_tensor(f"psT_u{_u}", [128, D], BF16))[:]
                lg_ps = ps.enter_context(nc.psum_tensor(f"lg_ps_u{_u}", [128, 136], F32))[:]
                s1p = [ps.enter_context(nc.psum_tensor(f"s1p{i}_u{_u}", [128, TG], F32))[:]
                       for i in range(2)]
                s2p = [ps.enter_context(nc.psum_tensor(f"s2p{i}_u{_u}", [128, D], F32))[:]
                       for i in range(2)]

                TRT0, PST0, LGT0 = s_trT.v, s_psTev.v, s_lgT.v
                XN0 = s_xn.v
                ret["YEV0"] = s_yev.v
                S10, HEV0, S20, YEV0 = s_s1.v, s_hev.v, s_s2.v, s_yev.v
                NTI = 15
                NJQ = NTI * 4
                lgev_val = [0] * NT
                b2add_val = [0] * NT
                xw_val = [0] * NT
                wl_val = [0] * NJQ
                b1_val = [0] * NTI
                xl2_val = [0] * NTI
                gws_j_val = [0] * 60
                v_psr = [0] * NT
                ret["XU0"] = s_xupd.v
                XU0, SC0 = s_xupd.v, s_sc.v
                ln_bufs["xn_gate"] = (
                    lambda tt: VEC.wait_ge(s_xw[tt % 2].h, xw_val[tt - 2]))
                sqw = None

                SCA.wait_ge(serV.h, prep_done_v)
                TEN.wait_ge(serV.h, prep_done_v)
                SYN.wait_ge(serV.h, prep_done_v)
                SYN.wait_ge(s_cast_b2.h, 16)
                SYN.dma_start(out=b2n, in_=d_b2b).then_inc(s_b2l.h, 16)
                s_b2l.v += 16
                TEN.wait_ge(s_b2l.h, s_b2l.v)

                sqw = ln_bufs["sqw"]
                barV(VEC.memset(ibx[:, 0:60], 4096.0))
                barV(VEC.tensor_copy(out=zi, in_=ibx[:, 0:60]))
                if not _restaged[0]:
                    v_g0 = barV(VEC.memset(gst[0], 0.0))
                    GPS.wait_ge(serV.h, v_g0)
                    GPS.dma_start(out=d_xns[T_TOK:T_TOK + 128, :],
                                  in_=gst[0]).then_inc(s_zf.h, 16)
                    s_zf.v += 16
                for t_ in range(NT):
                    ins = VEC.tensor_scalar(out=ibx[:, 64 + t_:65 + t_],
                                            in0=i128, scalar1=float(t_ * 128),
                                            scalar2=None, op0=ALU.add)
                barV(ins)
                v_pref = barV(VEC.tensor_copy(out=tokall, in_=ibx[:, 64:96]))
                ln_bufs["sq_war"][0] = v_pref
                GPS.wait_ge(serV.h, v_pref)
                GPS.dma_start(out=d_srq.rearrange("(p t) o -> p (t o)", p=128),
                              in_=zi).then_inc(s_zf.h, 16)
                s_zf.v += 16

                for tt in range(NT):
                    k = tt % 2
                    if tt % 4 == 0:
                        emit_ln_quad(tt, 4, (tt // 4) % 2, TRT0)
                    emit_ln_xn(tt, (tt // 4) % 2, tt % 4, TRT0)
                    TEN.wait_ge(s_xn.h, XN0 + tt + 1)
                    if tt >= 1:
                        TEN.wait_ge(s_psTev.h, PST0 + tt)
                    for j in range(NKC):
                        ins = TEN.transpose(psT[:, j * 128:(j + 1) * 128],
                                            ln_bufs["xn"][k][:, j * 128:(j + 1) * 128],
                                            id_bf)
                    s_trT.inc(ins)
                    VEC.wait_ge(s_trT.h, TRT0 + tt + 1)
                    if tt >= 2:
                        VEC.wait_ge(s_xw[tt % 2].h, xw_val[tt - 2])
                    s_psTev.inc(VEC.tensor_copy(out=xnts[k], in_=psT))

                    def _emit_xnt_write(tw):
                        kk = tw % 2
                        SCA.wait_ge(s_xn.h, XN0 + tw + 1)
                        SCA.dma_start(out=d_xns[tw * 128:(tw + 1) * 128, :],
                                      in_=ln_bufs["xn"][kk]).then_inc(
                            s_xw[tw % 2].h, 16)
                        s_xw[tw % 2].v += 16
                        xw_val[tw] = s_xw[tw % 2].v

                    if tt >= 1:
                        _emit_xnt_write(tt - 1)
                    if tt == NT - 1:
                        _emit_xnt_write(tt)
                    TEN.wait_ge(s_psTev.h, PST0 + tt + 1)
                    if tt >= 1:
                        TEN.wait_ge(serV.h, lgev_val[tt - 1])
                    for kc in range(NKC):
                        TEN.matmul(lg_ps[:, 0:8],
                                   xnts[k][:, kc * 128:(kc + 1) * 128],
                                   gwT[:, kc * 8:(kc + 1) * 8],
                                   start=(kc == 0), stop=False)
                    s_lgT.inc(TEN.matmul(lg_ps[:, 0:8], ones, gbr, start=False,
                                         stop=True))
                    VEC.wait_ge(s_lgT.h, s_lgT.v)
                    lgev_val[tt] = barV(VEC.tensor_copy(out=ln_bufs["lgf"],
                                                        in_=lg_ps[:, 0:8]))
                    barV(VEC.reduce_max(ln_bufs["idxm"], ln_bufs["lgf"], axis=AX))
                    barV(VEC.tensor_scalar(out=ln_bufs["tmask"], in0=ln_bufs["lgf"],
                                           scalar1=ln_bufs["idxm"], scalar2=-BIGI,
                                           op0=ALU.is_ge, op1=ALU.mult))
                    barV(VEC.tensor_add(ln_bufs["tmask"], ln_bufs["tmask"], iotaB))
                    barV(VEC.tensor_reduce(ln_bufs["idxm"], ln_bufs["tmask"],
                                           axis=AX, op=ALU.min))
                    for e in range(E):
                        ins = VEC.tensor_scalar(
                            out=masks[:, (tt * E + e):(tt * E + e + 1)],
                            in0=ln_bufs["idxm"], scalar1=float(e),
                            scalar2=0.5, op0=ALU.is_equal, op1=ALU.mult)
                    v_mask = barV(ins)
                    TEN.wait_ge(serV.h, v_mask)
                    ins = TEN.transpose(lg_ps[0:8, 8:136],
                                        masks[:, tt * E:(tt + 1) * E], id_f32)
                    s_lgT.inc(ins)
                    VEC.wait_ge(s_lgT.h, s_lgT.v)
                    v_mtb = barV(VEC.tensor_copy(out=mTb, in_=lg_ps[0:8, 8:136]))
                    TEN.wait_ge(serV.h, v_mtb)
                    if tt >= 2:
                        TEN.wait_ge(serV.h, b2add_val[tt - 2])
                    for nh in range(2):
                        ins = TEN.matmul(s2p[tt % 2][:, nh * TG:(nh + 1) * TG],
                                         mTb, b2n[0:8, nh * TG:(nh + 1) * TG],
                                         start=True, stop=True)
                        s_lgT.inc(ins)
                    VEC.wait_ge(s_lgT.h, s_lgT.v)
                    for nh in range(2):
                        Xh = X[:, tt * D + nh * TG:tt * D + (nh + 1) * TG]
                        ins = VEC.tensor_add(Xh, Xh,
                                             s2p[tt % 2][:, nh * TG:(nh + 1) * TG])
                    b2add_val[tt] = barV(ins)

                if not _restaged[0]:
                    rst = xst[0]
                    rl_val = [0] * 4
                    rw_val = [0] * 4
                    g2 = 0
                    for e in range(E):
                        for dsrc, dT, csem in ((d_w1b, d_w1bT, s_cw1e[e]),
                                               (d_w2b, d_w2bT, s_cw2e[e])):
                            for ti in range(32):
                                q_, kc_ = divmod(ti, 8)
                                sl = g2 % 4
                                if ti == 0:
                                    SYN.wait_ge(csem.h, csem.v)
                                if g2 >= 4:
                                    SYN.wait_ge(s_rw4[sl].h, rw_val[sl])
                                if dsrc is d_w1b:
                                    wsrc = dsrc[e, q_ * 1024:(q_ + 1) * 1024,
                                                kc_ * 128:(kc_ + 1) * 128]
                                else:
                                    wsrc = dsrc[e, :, q_ * 1024 + kc_ * 128:
                                                q_ * 1024 + (kc_ + 1) * 128]
                                SYN.dma_start(
                                    out=rst[:, sl * 1024:(sl + 1) * 1024],
                                    in_=wsrc, transpose=True).then_inc(
                                    s_rl4[sl].h, 16)
                                s_rl4[sl].v += 16
                                rl_val[sl] = s_rl4[sl].v
                                SCA.wait_ge(s_rl4[sl].h, rl_val[sl])
                                SCA.dma_start(
                                    out=dT[e, q_, :,
                                           kc_ * 1024:(kc_ + 1) * 1024],
                                    in_=rst[:, sl * 1024:(sl + 1) * 1024]
                                ).then_inc(s_rw4[sl].h, 16)
                                s_rw4[sl].v += 16
                                rw_val[sl] = s_rw4[sl].v
                                g2 += 1
                    _restaged[0] = True

                ib = ibx[0:1, 0:912]
                onesf = ibx[0:1, 0:128]
                Z32 = ib[:, 512:544]
                v0 = barV(VEC.tensor_scalar(out=maskb, in0=masks, scalar1=2.0,
                                            scalar2=None, op0=ALU.mult))
                TEN.wait_ge(serV.h, v0)
                for tt in range(NT):
                    ins = TEN.matmul(s1p[0][0:1, tt * 8:(tt + 1) * 8], onec,
                                     maskb[:, tt * 8:(tt + 1) * 8],
                                     start=True, stop=True)
                s_lgT.inc(ins)
                VEC.wait_ge(s_lgT.h, s_lgT.v)
                barV(VEC.tensor_copy(out=ib[:, 0:256], in_=s1p[0][0:1, 0:256]))
                barV(VEC.tensor_copy(
                    out=ib[:, 256:512].rearrange("p (e t) -> p e t", t=NT),
                    in_=ib[:, 0:256].rearrange("p (t e) -> p e t", e=E)))
                barV(VEC.memset(Z32, 0.0))
                for e in range(E):
                    ins = VEC.tensor_tensor_scan(
                        out=ib[:, 544 + e * NT:544 + (e + 1) * NT],
                        data0=ib[:, 256 + e * NT:256 + (e + 1) * NT],
                        data1=Z32, initial=0.0, op0=ALU.add, op1=ALU.add)
                barV(ins)
                barV(VEC.tensor_sub(ib[:, 256:512], ib[:, 544:800],
                                    ib[:, 256:512]))
                barV(VEC.tensor_copy(
                    out=ib[:, 800:808],
                    in_=ib[:, 544:800].rearrange(
                        "p (e t) -> p e t", t=NT)[:, :, NT - 1]))
                barV(VEC.memset(ib[:, 816:824], 0.0))
                for jth in range(8):
                    barV(VEC.tensor_scalar(out=ib[:, 808:816],
                                           in0=ib[:, 800:808],
                                           scalar1=float(512 * jth),
                                           scalar2=512.0, op0=ALU.is_gt,
                                           op1=ALU.mult))
                    barV(VEC.tensor_add(ib[:, 816:824], ib[:, 816:824],
                                        ib[:, 808:816]))
                barV(VEC.tensor_tensor_scan(out=ib[:, 824:832],
                                            data0=ib[:, 816:824],
                                            data1=Z32[:, 0:8], initial=0.0,
                                            op0=ALU.add, op1=ALU.add))
                barV(VEC.tensor_sub(ib[:, 832:840], ib[:, 824:832],
                                    ib[:, 816:824]))
                for e in range(E):
                    ins = VEC.tensor_scalar(
                        out=ib[:, 256 + e * NT:256 + (e + 1) * NT],
                        in0=ib[:, 256 + e * NT:256 + (e + 1) * NT],
                        scalar1=ib[:, 832 + e:833 + e], scalar2=None,
                        op0=ALU.add)
                barV(ins)
                barV(VEC.tensor_copy(
                    out=ib[:, 544:800].rearrange("p (t e) -> p t e", e=E),
                    in_=ib[:, 256:512].rearrange("p (e t) -> p t e", t=NT)))
                for i in range(NTI):
                    barV(VEC.tensor_scalar(out=ib[:, 856:864],
                                           in0=ib[:, 832:840],
                                           scalar1=float(512 * i),
                                           scalar2=None, op0=ALU.is_le))
                    barV(VEC.tensor_reduce(ib[:, 840 + i:841 + i],
                                           ib[:, 856:864], axis=AX,
                                           op=ALU.add))
                barV(VEC.tensor_scalar(out=ib[:, 840:855], in0=ib[:, 840:855],
                                       scalar1=-1.0, scalar2=None,
                                       op0=ALU.add))
                v_selq = barV(VEC.tensor_copy(out=selqi[:, 0:NTI],
                                              in_=ib[:, 840:855]))
                barV(VEC.tensor_scalar(out=ib[:, 256:271], in0=ib[:, 840:855],
                                       scalar1=512.0, scalar2=None,
                                       op0=ALU.mult))
                v_1f = barV(VEC.memset(onesf, 1.0))
                TEN.wait_ge(serV.h, v_1f)
                ins = TEN.matmul(lg_ps[:, 16:31], onesf, ib[:, 256:271],
                                 start=True, stop=True)
                s_lgT.inc(ins)
                VEC.wait_ge(s_lgT.h, s_lgT.v)
                barV(VEC.tensor_copy(out=selbase[:, 0:15],
                                     in_=lg_ps[:, 16:31]))

                for tt in range(NT):
                    TEN.wait_ge(serV.h, v_selq if tt == 0 else v_psr[tt - 1])
                    TEN.matmul(lg_ps[:, 8:16], onesf,
                               ib[:, 544 + tt * 8:544 + (tt + 1) * 8],
                               start=True, stop=False)
                    ins = TEN.matmul(lg_ps[:, 8:16], us_bf,
                                     maskb[:, tt * 8:(tt + 1) * 8],
                                     start=False, stop=True)
                    s_lgT.inc(ins)
                    VEC.wait_ge(s_lgT.h, s_lgT.v)
                    v_psr[tt] = barV(VEC.tensor_mul(
                        ln_bufs["tmask"], lg_ps[:, 8:16],
                        maskb[:, tt * 8:(tt + 1) * 8]))
                    idxm = ln_bufs["idxm"]
                    barV(VEC.tensor_reduce(idxm, ln_bufs["tmask"],
                                           axis=AX, op=ALU.add))
                    barV(VEC.tensor_copy(out=didxp[:, tt:tt + 1], in_=idxm))
                    lo_ = ln_bufs["tmask"][:, 0:1]
                    t1_ = ln_bufs["tmask"][:, 1:2]
                    if tt == 0:
                        VEC.wait_ge(s_zf.h, s_zf.v)
                    barV(VEC.tensor_copy(out=zi[:, 8:9], in_=idxm))
                    barV(VEC.tensor_scalar(out=zi[:, 9:10], in0=zi[:, 8:9],
                                           scalar1=127, scalar2=None,
                                           op0=ALU.bitwise_and))
                    barV(VEC.tensor_copy(out=lo_, in_=zi[:, 9:10]))
                    barV(VEC.tensor_sub(t1_, idxm, lo_))
                    barV(VEC.tensor_scalar(out=t1_, in0=t1_,
                                           scalar1=1.0 / 128.0, scalar2=None,
                                           op0=ALU.mult))
                    barV(VEC.scalar_tensor_tensor(out=idxm, in0=lo_,
                                                  scalar=60.0, in1=t1_,
                                                  op0=ALU.mult, op1=ALU.add))
                    v_d = barV(VEC.tensor_copy(out=didxq[:, tt:tt + 1],
                                               in_=idxm))
                    GPS.wait_ge(serV.h, v_d)
                    GPS.wait_ge(s_zf.h, s_zf.v) if tt == 0 else \
                        GPS.wait_ge(s_scat.h, s_scat.v)
                    GPS.indirect_dma_start(
                        out=d_srq,
                        out_offset=IndirectOffsetOnAxis(didxq[:, tt:tt + 1], 0),
                        in_=tokall[:, tt:tt + 1], in_offset=None,
                    ).then_inc(s_scat.h, 16)
                    s_scat.v += 16

                GPS.wait_ge(s_scat.h, s_scat.v)
                GPS.dma_start(
                    out=srq_sb,
                    in_=d_srq.rearrange("(p t) o -> p (t o)", p=128)
                ).then_inc(s_gq.h, 16)
                s_gq.v += 16
                GPS.wait_ge(s_gq.h, s_gq.v)
                GPS.wait_ge(s_xw[0].h, xw_val[30])
                GPS.wait_ge(s_xw[1].h, xw_val[31])
                GPS.wait_ge(s_zf.h, s_zf.v)
                for j in range(60):
                    sl = j % 2
                    if j >= 2:
                        GPS.wait_ge(s_gws[sl].h, gws_j_val[j - 2])
                    GPS.indirect_dma_start(
                        out=gst[sl], out_offset=None, in_=d_xns,
                        in_offset=IndirectOffsetOnAxis(srq_sb[:, j:j + 1], 0),
                    ).then_inc(s_gws[sl].h, 16)
                    s_gws[sl].v += 16
                    GPS.wait_ge(s_gws[sl].h, s_gws[sl].v)
                    GPS.dma_start(out=d_xs[j * 128:(j + 1) * 128, :],
                                  in_=gst[sl]).then_inc(s_gws[sl].h, 16)
                    s_gws[sl].v += 16
                    gws_j_val[j] = s_gws[sl].v

                d_w1bT1 = d_w1bT.rearrange("e q p c -> (e q p) c")
                d_w2bT1 = d_w2bT.rearrange("e q p c -> (e q p) c")
                d_b1t1 = d_b1t.rearrange("e p c -> (e p) c")
                wix_val = [0] * NJQ

                def emit_wix(jq):
                    i, q = divmod(jq, 4)
                    sl = jq % 2
                    if jq >= 2:
                        VEC.wait_ge(s_wl[sl].h, wl_val[jq - 2])
                    ins = VEC.tensor_scalar(out=wixi[sl][:, 0:1], in0=i128,
                                            scalar1=selbase[:, i:i + 1],
                                            scalar2=float(q * 128),
                                            op0=ALU.add, op1=ALU.add)
                    if q == 0:
                        ins = VEC.scalar_tensor_tensor(
                            out=wixi[sl][:, 1:2], in0=selbase[:, i:i + 1],
                            scalar=0.25, in1=i128, op0=ALU.mult,
                            op1=ALU.add)
                    wix_val[jq] = barV(ins)

                def emit_wdyn(jq):
                    i, q = divmod(jq, 4)
                    slot = jq % 2
                    sl = jq % 2
                    if jq < 2:
                        for s4_ in range(4):
                            GPS.wait_ge(s_rw4[s4_].h, s_rw4[s4_].v)
                        GPS.wait_ge(s_b1t.h, s_b1t.v)
                    if jq >= 2:
                        GPS.wait_ge(s_s2.h, S20 + 4 * (jq - 1))
                    GPS.wait_ge(serV.h, wix_val[jq])
                    for dT1, wb_ in ((d_w1bT1, wb1), (d_w2bT1, wb2)):
                        GPS.indirect_dma_start(
                            out=wb_[slot], out_offset=None, in_=dT1,
                            in_offset=IndirectOffsetOnAxis(
                                wixi[sl][:, 0:1], 0),
                        ).then_inc(s_wl[slot].h, 16)
                        s_wl[slot].v += 16
                    wl_val[jq] = s_wl[slot].v
                    if q == 0:
                        if i >= 2:
                            GPS.wait_ge(s_hev.h, HEV0 + 32 * (i - 1))
                        GPS.indirect_dma_start(
                            out=b1sel[i % 2], out_offset=None, in_=d_b1t1,
                            in_offset=IndirectOffsetOnAxis(
                                wixi[sl][:, 1:2], 0),
                        ).then_inc(s_b1g.h, 16)
                        s_b1g.v += 16
                        b1_val[i] = s_b1g.v

                def emit_xst(i):
                    sl2 = i % 2
                    SYN.wait_ge(s_gws[(4 * i + 3) % 2].h, gws_j_val[4 * i + 3])
                    SYN.wait_ge(s_gws[(4 * i + 2) % 2].h, gws_j_val[4 * i + 2])
                    if i < 2:
                        for s4_ in range(4):
                            SYN.wait_ge(s_rw4[s4_].h, s_rw4[s4_].v)
                    if i >= 2:
                        SYN.wait_ge(s_s1.h, S10 + 32 * (i - 1))
                    for kc in range(NKC):
                        SYN.dma_start(
                            out=xst[sl2][:, kc * TG:(kc + 1) * TG],
                            in_=d_xs[i * TG:(i + 1) * TG,
                                     kc * 128:(kc + 1) * 128],
                            transpose=True).then_inc(s_xl[sl2].h, 16)
                        s_xl[sl2].v += 16
                    xl2_val[i] = s_xl[sl2].v

                def emit_s1s(jq):
                    i, q = divmod(jq, 4)
                    p = jq % 2
                    pj = jq % 2
                    TEN.wait_ge(s_wl[p].h, wl_val[jq])
                    TEN.wait_ge(s_xl[i % 2].h, xl2_val[i])
                    for hm in range(8):
                        gidx = jq * 8 + hm
                        if gidx >= 2:
                            TEN.wait_ge(s_hev.h, HEV0 + gidx - 1)
                        bank = s1p[hm % 2]
                        for kc in range(NKC):
                            ins = TEN.matmul(
                                bank,
                                wb1[p][:, kc * 1024 + hm * 128:
                                       kc * 1024 + (hm + 1) * 128],
                                xst[i % 2][:, kc * TG:(kc + 1) * TG],
                                start=(kc == 0), stop=(kc == NKC - 1))
                        s_s1.inc(ins)
                        SCA.wait_ge(s_s1.h, S10 + gidx + 1)
                        if hm == 0:
                            if jq >= 2:
                                SCA.wait_ge(s_s2.h, S20 + 4 * (jq - 1))
                            SCA.wait_ge(s_b1g.h, b1_val[i])
                        s_hev.inc(SCA.activation(
                            hT[pj][:, hm * TG:(hm + 1) * TG], bank, AF.Relu,
                            bias=b1sel[i % 2][:, (q * 8 + hm):
                                              (q * 8 + hm + 1)]))

                def emit_s2s(jq):
                    i, q = divmod(jq, 4)
                    p = jq % 2
                    pj = jq % 2
                    TEN.wait_ge(s_hev.h, HEV0 + (jq + 1) * 8)
                    for tc in range(4):
                        c2 = jq * 4 + tc
                        if c2 >= 2:
                            TEN.wait_ge(s_yev.h, YEV0 + c2 - 1)
                        else:
                            TEN.wait_ge(serV.h, b2add_val[NT - 2 + c2])
                        bank = s2p[tc % 2]
                        for nh in range(2):
                            half = bank[:, nh * TG:(nh + 1) * TG]
                            for hc in range(NKC):
                                ins = TEN.matmul(
                                    half,
                                    hT[pj][:, hc * TG + tc * 128:
                                           hc * TG + (tc + 1) * 128],
                                    wb2[p][:, hc * 1024 + nh * TG:
                                           hc * 1024 + (nh + 1) * TG],
                                    start=(hc == 0),
                                    stop=(hc == NKC - 1))
                        s_s2.inc(ins)
                        VEC.wait_ge(s_s2.h, S20 + c2 + 1)
                        ysl = y_acc[:, tc * 1024:(tc + 1) * 1024]
                        if q == 0:
                            if i >= 1 and tc == 0:
                                VEC.wait_ge(s_sc.h, SC0 + 64 * i)
                            ins2 = VEC.tensor_copy(out=ysl, in_=bank)
                        else:
                            ins2 = VEC.tensor_add(ysl, ysl, bank)
                        s_yev.inc(ins2)
                    if q == 3:
                        GPS.wait_ge(s_yev.h, YEV0 + 4 * (jq + 1))
                        for tc in range(4):
                            r0_ = i * TG + tc * 128
                            GPS.dma_start(
                                out=d_ysort[r0_:r0_ + 128, :],
                                in_=y_acc[:, tc * 1024:(tc + 1) * 1024],
                            ).then_inc(s_sc.h, 16)
                            s_sc.v += 16

                emit_wix(0)
                emit_wdyn(0)
                emit_wix(1)
                emit_wdyn(1)
                emit_xst(0)
                emit_xst(1)
                emit_s1s(0)
                for jq in range(NJQ):
                    if jq + 2 < NJQ:
                        emit_wix(jq + 2)
                        emit_wdyn(jq + 2)
                    if jq + 1 < NJQ:
                        if (jq + 1) % 4 == 0:
                            ii = (jq + 1) // 4
                            if ii + 1 < NTI:
                                emit_xst(ii + 1)
                        emit_s1s(jq + 1)
                    emit_s2s(jq)

                SC_ALL = s_sc.v
                for jt in range(NT):
                    if jt == 0:
                        GPS.wait_ge(s_sc.h, SC_ALL)
                    if jt >= 2:
                        GPS.wait_ge(s_xupd.h, XU0 + jt - 1)
                    GPS.wait_ge(serV.h, v_psr[NT - 1])
                    GPS.indirect_dma_start(
                        out=ystage[jt % 2], out_offset=None, in_=d_ysort,
                        in_offset=IndirectOffsetOnAxis(didxp[:, jt:jt + 1],
                                                       0),
                    ).then_inc(s_gws[jt % 2].h, 16)
                    s_gws[jt % 2].v += 16
                    VEC.wait_ge(s_gws[jt % 2].h, s_gws[jt % 2].v)
                    Xsl = X[:, jt * D:(jt + 1) * D]
                    s_xupd.inc(VEC.scalar_tensor_tensor(
                        out=Xsl, in0=ystage[jt % 2], scalar=0.5, in1=Xsl,
                        op0=ALU.mult, op1=ALU.add))

                TEN.wait_ge(s_yev.h, s_yev.v)
                SCA.wait_ge(s_yev.h, s_yev.v)
                VEC.wait_ge(s_hev.h, s_hev.v)
        return ret

    def emit_gla_call():
        for dst, srct in ((d_wqb, d_wq), (d_wkb, d_wk), (d_wvb, d_wv),
                          (d_wgb, d_wg), (d_wob, d_wo), (d_wgk1b, d_wgk1),
                          (d_wgk2b, d_wgk2)):
            GPS.dma_start(out=dst, in_=srct).then_inc(s_cast_g.h, 16)
            s_cast_g.v += 16
        with ExitStack() as gs:
            _uid[0] += 1
            _u = _uid[0]
            g = lambda n, sh, dt: gs.enter_context(
                nc.sbuf_tensor(f"{n}_u{_u}", sh, dt))[:]
            alloc_ln_bufs(gs, with_sqd=True, sqw_tiles=2)
            wqs = g("wqs", [128, NKC * 1024], BF16)
            wks = g("wks", [128, NKC * 1024], BF16)
            wvs = g("wvs", [128, NKC * 1024], BF16)
            wgs = g("wgs", [128, NKC * 1024], BF16)
            wos = g("wos", [128, NKC * 1024], BF16)
            wgk1s = g("wgk1s", [128, NKC * GLR], BF16)
            wgk2s = g("wgk2s", [GLR, D], BF16)
            xg = g("xg", [128, NKC * 256], BF16)
            qT = g("qT", [128, NHEAD * 256], BF16)
            kTf = g("kTf", [128, NHEAD * 256], BF16)
            z1s = g("z1s", [GLR, 256], BF16)
            gks = g("gks", [128, 2 * D], BF16)
            enbt = g("enbt", [128, D], BF16)
            ktt = g("ktt", [128, D], BF16)
            vtok = g("vtok", [128, D], BF16)
            gtok = g("gtok", [128, D], BF16)
            eb = g("eb", [128, D], BF16)
            enb = g("enb", [128, D], BF16)
            ebC = g("ebC", [128, 8], F32)
            qt = g("qt", [128, D], BF16)
            kt = g("kt", [128, D], BF16)
            smb = g("smb", [128, D], BF16)
            ofb = g("ofb", [128, D], BF16)
            oTa = g("oTa", [128, D], BF16)
            ssqa = g("ssqa", [128, 8], F32)
            lnva = g("lnva", [128, 8], F32)
            rsta = g("rsta", [128, 8], F32)
            Hf = g("Hf", [128, D], F32)
            Hb = g("Hb", [128, D], BF16)

            with ExitStack() as ps:
                pE = ps.enter_context(
                    nc.psum_tensor(f"pE_u{_u}", [128, 512], F32))[:]
                pA = ps.enter_context(
                    nc.psum_tensor(f"pA_u{_u}", [128, D], F32))[:]
                pB = ps.enter_context(
                    nc.psum_tensor(f"pB_u{_u}", [128, D], F32))[:]
                pC = ps.enter_context(
                    nc.psum_tensor(f"pC_u{_u}", [128, D], F32))[:]
                ptr = ps.enter_context(
                    nc.psum_tensor(f"ptr_u{_u}", [128, D], BF16))[:]

                silg = ln_bufs["sqw"][:, 0:D]
                SYN.wait_ge(serV.h, prep_done_v)
                SYN.wait_ge(s_cast_g.h, s_cast_g.v)
                for dst, srct in ((wqs, d_wqb), (wks, d_wkb), (wvs, d_wvb),
                                  (wgs, d_wgb), (wos, d_wob)):
                    dv = dst.rearrange("p (k m) -> p k m", k=NKC)
                    sv = srct.rearrange("(k p) m -> p k m", p=128)
                    SYN.dma_start(out=dv, in_=sv).then_inc(s_gld.h, 16)
                    s_gld.v += 16
                dv = wgk1s.rearrange("p (k m) -> p k m", k=NKC)
                sv = d_wgk1b.rearrange("(k p) m -> p k m", p=128)
                SYN.dma_start(out=dv, in_=sv).then_inc(s_gld.h, 16)
                s_gld.v += 16
                SYN.dma_start(out=wgk2s, in_=d_wgk2b).then_inc(s_gld.h, 16)
                s_gld.v += 16
                gld_all = s_gld.v

                VEC.memset(Hf, 0.0)
                hinit = barV(VEC.memset(Hb, 0.0))
                TEN.wait_ge(serV.h, hinit)
                TEN.wait_ge(serV.h, prep_done_v)
                SCA.wait_ge(serV.h, prep_done_v)

                XGE0, Z1T0, Z1E0, ZT0 = s_xgev.v, s_z1T.v, s_z1ev.v, s_zT.v
                TRT0g = s_trT.v
                XN0g, PRJ0, PEV0 = s_xn.v, s_projT.v, s_pev.v
                B0, S0_, O0, H0_, WO0, WOE0 = (s_b.v, s_S.v, s_o.v, s_H.v,
                                               s_wo.v, s_woev.v)
                gk_vals = []
                osq_vals = []
                state_vals = []
                ofb_vals = []
                sm_vals = []
                otev_vals = []
                sq_vals = []
                pev_cnt = 0

                x3 = xg.rearrange("p (j c) -> p j c", j=NKC)
                qT3 = qT.rearrange("p (h t) -> p h t", t=256)
                kT3 = kTf.rearrange("p (h t) -> p h t", t=256)

                def hview(t):
                    return t.rearrange("p (h c) -> p h c", c=128)

                for cg in range(16):
                    emit_ln_quad(cg * 2, 2, cg % 2, TRT0g)
                    for lt in range(2):
                        tt = cg * 2 + lt
                        k = tt % 2
                        emit_ln_xn(tt, cg % 2, lt, TRT0g)
                        TEN.wait_ge(s_xn.h, XN0g + tt + 1)
                        if cg > 0:
                            TEN.wait_ge(serV.h, otev_vals[2 * cg - 1])
                        if tt >= 1:
                            TEN.wait_ge(s_xgev.h, XGE0 + tt)
                        for j in range(NKC):
                            ins = TEN.transpose(
                                ptr[:, j * 128:(j + 1) * 128],
                                ln_bufs["xn"][k][:, j * 128:(j + 1) * 128], id_bf)
                        s_trT.inc(ins)
                        VEC.wait_ge(s_trT.h, s_trT.v)
                        if cg > 0 and lt == 0:
                            VEC.wait_ge(s_z1T.h, Z1T0 + cg)
                            VEC.wait_ge(s_projT.h, PRJ0 + 28 * cg)
                        dstv = x3[:, :, lt * 128:(lt + 1) * 128]
                        srcv = ptr.rearrange("p (j c) -> p j c", j=NKC)
                        s_xgev.inc(VEC.tensor_copy(out=dstv, in_=srcv))

                    TEN.wait_ge(s_gld.h, gld_all)
                    TEN.wait_ge(s_xgev.h, XGE0 + 2 * (cg + 1))
                    if cg > 0:
                        TEN.wait_ge(s_z1ev.h, Z1E0 + cg)
                        TEN.wait_ge(serV.h, state_vals[2 * cg - 1])
                        SCA.wait_ge(serV.h, state_vals[2 * cg - 1])
                    TEN.wait_ge(s_pev.h, PEV0 + pev_cnt)
                    for kc in range(NKC):
                        ins = TEN.matmul(pE[0:GLR, 0:256],
                                         wgk1s[:, kc * GLR:(kc + 1) * GLR],
                                         xg[:, kc * 256:(kc + 1) * 256],
                                         start=(kc == 0), stop=(kc == NKC - 1))
                    s_z1T.inc(ins)
                    s_z1T.wait(VEC)
                    s_z1ev.inc(VEC.tensor_copy(out=z1s, in_=pE[0:GLR, 0:256]))

                    proj_groups = []
                    for wsb, dstT, scale in ((wqs, qT, HKS), (wks, kTf, 1.0)):
                        for mq in range(8):
                            proj_groups.append((wsb, dstT, scale, mq))

                    def emit_proj_group(idx):
                        nonlocal pev_cnt
                        wsb, dstT, scale, mq = proj_groups[idx]
                        region = (pE[:, 0:256] if idx % 2 == 0
                                  else pB[:, 512:768])
                        if pev_cnt >= 2:
                            TEN.wait_ge(s_pev.h, PEV0 + pev_cnt - 1)
                        for kc in range(NKC):
                            ins = TEN.matmul(
                                region,
                                wsb[:, kc * 1024 + mq * 128:
                                    kc * 1024 + (mq + 1) * 128],
                                xg[:, kc * 256:(kc + 1) * 256],
                                start=(kc == 0), stop=(kc == NKC - 1))
                        s_projT.inc(ins)
                        s_projT.wait(SCA)
                        s_pev.inc(SCA.activation(
                            dstT[:, mq * 256:(mq + 1) * 256], region,
                            AF.Copy, scale=scale))
                        pev_cnt += 1

                    for tc in range(2):
                        ch_prev_reader = 2 * (cg - 1) + tc + 1
                        TEN.wait_ge(s_z1ev.h, Z1E0 + cg + 1)
                        if cg > 0 and tc == 0:
                            TEN.wait_ge(s_woev.h, WOE0 + 2 * cg)
                        if tc == 1:
                            TEN.wait_ge(serV.h, gk_vals[2 * cg])
                        for nh in range(2):
                            half = pC[:, nh * TG:(nh + 1) * TG]
                            TEN.matmul(half, z1s[:, tc * 128:(tc + 1) * 128],
                                       wgk2s[:, nh * TG:(nh + 1) * TG],
                                       start=True, stop=False)
                            ins = TEN.matmul(half, ones,
                                             bgk2r[:, nh * TG:(nh + 1) * TG],
                                             start=False, stop=True)
                        s_zT.inc(ins)
                        for gi in range(8):
                            emit_proj_group(tc * 8 + gi)
                        SCA.wait_ge(s_zT.h, ZT0 + 2 * cg + tc + 1)
                        barS(SCA.activation(ln_bufs["sqd"], pC, AF.Abs))
                        barS(SCA.activation(ln_bufs["sqd"], ln_bufs["sqd"],
                                            AF.Exp, scale=-1.0))
                        v_zs = barS(SCA.activation(ln_bufs["sqd"], ln_bufs["sqd"],
                                                   AF.Ln, bias=cb[:, 1:2]))
                        VEC.wait_ge(serS.h, v_zs)
                        VEC.wait_ge(s_zT.h, ZT0 + 2 * cg + tc + 1)
                        if cg > 0:
                            VEC.wait_ge(s_b.h, B0 + ch_prev_reader)
                        gk_sl = gks[:, tc * D:(tc + 1) * D]
                        barV(VEC.tensor_scalar(out=gk_sl, in0=pC, scalar1=0.0,
                                               scalar2=None, op0=ALU.min))
                        v_gk = barV(VEC.tensor_sub(gk_sl, gk_sl, ln_bufs["sqd"]))
                        gk_vals.append(v_gk)

                    for lt in range(2):
                        ch = cg * 2 + lt
                        TEN.wait_ge(serV.h, gk_vals[2 * cg + lt])
                        if ch > 0:
                            TEN.wait_ge(serV.h, state_vals[ch - 1])
                            TEN.wait_ge(serV.h, ofb_vals[ch - 1])
                        TEN.wait_ge(s_pev.h, PEV0 + pev_cnt)
                        for nh in range(2):
                            ins = TEN.matmul(
                                pB[:, nh * TG:(nh + 1) * TG], u_bf,
                                gks[:, lt * D + nh * TG:lt * D + (nh + 1) * TG],
                                start=True, stop=True)
                        for h in range(NHEAD):
                            ins = TEN.matmul(
                                pA[:, h * 128:(h + 1) * 128],
                                gks[:, lt * D + h * 128:lt * D + (h + 1) * 128],
                                u_bf, start=True, stop=True)
                        s_b.inc(ins)
                        s_b.wait(SCA)
                        SCA.activation(enbt, pB, AF.Exp, scale=-1.0 / GLN)
                        SCA.activation(eb, pA, AF.Exp, scale=1.0 / GLN)
                        SCA.activation(enb, pA, AF.Exp, scale=-1.0 / GLN)
                        v_ebs = barS(SCA.activation(
                            ebC, hview(pA)[:, :, 127], AF.Exp, scale=1.0 / GLN))
                        if ch > 0:
                            SCA.wait_ge(s_o.h, O0 + ch)
                            SCA.wait_ge(s_H.h, H0_ + ch)
                            VEC.wait_ge(s_o.h, O0 + ch)
                            VEC.wait_ge(s_H.h, H0_ + ch)
                        g_pev_val = [0]
                        for wsb, kind in ((wvs, "v"), (wgs, "g"), (wks, "k")):
                            for nh in range(2):
                                TEN.wait_ge(s_pev.h, PEV0 + pev_cnt)
                                for kc in range(NKC):
                                    ins = TEN.matmul(
                                        pE,
                                        xg[:, kc * 256 + lt * 128:
                                           kc * 256 + (lt + 1) * 128],
                                        wsb[:, kc * 1024 + nh * TG:
                                            kc * 1024 + (nh + 1) * TG],
                                        start=(kc == 0), stop=(kc == NKC - 1))
                                s_projT.inc(ins)
                                if kind == "k":
                                    s_projT.wait(VEC)
                                    VEC.wait_ge(serS.h, v_ebs)
                                    ins2 = VEC.tensor_mul(
                                        ktt[:, nh * TG:(nh + 1) * TG], pE,
                                        enbt[:, nh * TG:(nh + 1) * TG])
                                    s_pev.inc(ins2)
                                else:
                                    dst = vtok if kind == "v" else gtok
                                    s_projT.wait(SCA)
                                    s_pev.inc(SCA.activation(
                                        dst[:, nh * TG:(nh + 1) * TG], pE,
                                        AF.Copy))
                                    if kind == "g":
                                        g_pev_val[0] = s_pev.v
                                pev_cnt += 1
                        if ch > 0:
                            SCA.wait_ge(serV.h, ofb_vals[ch - 1])
                        SCA.wait_ge(s_pev.h, g_pev_val[0])
                        barS(SCA.activation(silg, gtok, AF.Exp, scale=-1.0))
                        barS(SCA.activation(silg, silg, AF.Ln, bias=cb[:, 1:2]))
                        v_sg = barS(SCA.activation(silg, silg, AF.Exp,
                                                   scale=-1.0))
                        sq_vals.append(v_sg)
                        VEC.wait_ge(serS.h, v_sg)
                        for h in range(NHEAD):
                            sl = slice(h * 128, (h + 1) * 128)
                            ins = VEC.tensor_mul(gtok[:, sl], gtok[:, sl], gnwb)
                        barV(ins)
                        barV(VEC.tensor_mul(silg, gtok, silg))
                        VEC.wait_ge(serS.h, v_ebs)
                        VEC.tensor_mul(hview(qt), qT3[:, :, lt * 128:(lt + 1) * 128],
                                       hview(eb))
                        v_qt = barV(VEC.tensor_mul(
                            hview(kt), kT3[:, :, lt * 128:(lt + 1) * 128],
                            hview(enb)))
                        TEN.wait_ge(serV.h, v_qt)
                        TEN.wait_ge(serV.h, gk_vals[2 * cg + 1])
                        if ch > 0:
                            TEN.wait_ge(s_woev.h, WOE0 + ch)
                        for h in range(NHEAD):
                            ins = TEN.matmul(
                                pC[:, h * 128:(h + 1) * 128],
                                kt[:, h * 128:(h + 1) * 128],
                                qt[:, h * 128:(h + 1) * 128],
                                start=True, stop=True)
                        s_S.inc(ins)
                        s_S.wait(VEC)
                        if ch > 0:
                            VEC.wait_ge(serS.h, osq_vals[ch - 1])
                        for h in range(NHEAD):
                            ins = VEC.tensor_mul(smb[:, h * 128:(h + 1) * 128],
                                                 pC[:, h * 128:(h + 1) * 128],
                                                 u_bf)
                        v_sm = barV(ins)
                        sm_vals.append(v_sm)
                        if ch > 0:
                            TEN.wait_ge(serV.h, state_vals[ch - 1])
                        for h in range(NHEAD):
                            sl = slice(h * 128, (h + 1) * 128)
                            ins = TEN.matmul(pB[:, sl], ktt[:, sl], vtok[:, sl],
                                             start=True, stop=True)
                        s_H.inc(ins)
                        TEN.wait_ge(serV.h, v_sm)
                        for h in range(NHEAD):
                            sl = slice(h * 128, (h + 1) * 128)
                            TEN.matmul(pA[:, sl], smb[:, sl], vtok[:, sl],
                                       start=True, stop=False)
                            ins = TEN.matmul(pA[:, sl], qt[:, sl], Hb[:, sl],
                                             start=False, stop=True)
                        s_o.inc(ins)
                        s_o.wait(SCA)
                        if ch > 0:
                            SCA.wait_ge(serV.h, sm_vals[ch - 1] if False else v_sm)
                        v_osq = barS(SCA.activation(smb, pA, AF.Square))
                        osq_vals.append(v_osq)
                        VEC.wait_ge(serS.h, v_osq)
                        v_ssq = barV(VEC.tensor_reduce(
                            ssqa, hview(smb), axis=AX, op=ALU.add))
                        SCA.wait_ge(serV.h, v_ssq)
                        barS(SCA.activation(lnva, ssqa, AF.Ln, scale=1.0 / DH,
                                            bias=cb[:, 0:1]))
                        v_rst = barS(SCA.activation(rsta, lnva, AF.Exp,
                                                    scale=-0.5))
                        VEC.wait_ge(serS.h, v_rst)
                        for h in range(NHEAD):
                            sl = slice(h * 128, (h + 1) * 128)
                            ins = VEC.scalar_tensor_tensor(
                                out=ofb[:, sl], in0=pA[:, sl],
                                scalar=rsta[:, h:h + 1], in1=silg[:, sl],
                                op0=ALU.mult, op1=ALU.mult)
                        v_ofb = barV(ins)
                        ofb_vals.append(v_ofb)
                        ln_bufs["sq_war"][0] = v_ofb
                        s_H.wait(VEC)
                        barV(VEC.tensor_add(Hf, Hf, pB))
                        for h in range(NHEAD):
                            sl = slice(h * 128, (h + 1) * 128)
                            ins = VEC.tensor_scalar(
                                out=Hf[:, sl], in0=Hf[:, sl],
                                scalar1=ebC[:, h:h + 1], scalar2=None,
                                op0=ALU.mult)
                        barV(ins)
                        s_o.wait(VEC)
                        v_state = barV(VEC.tensor_copy(out=Hb, in_=Hf))
                        state_vals.append(v_state)
                        TEN.wait_ge(serV.h, v_ofb)
                        if ch > 0:
                            TEN.wait_ge(serV.h, otev_vals[ch - 1])
                        for h in range(NHEAD):
                            sl = slice(h * 128, (h + 1) * 128)
                            ins = TEN.transpose(ptr[:, sl], ofb[:, sl], id_bf)
                        s_otr.inc(ins)
                        s_otr.wait(VEC)
                        v_ot = barV(VEC.tensor_copy(out=oTa, in_=ptr))
                        otev_vals.append(v_ot)
                        TEN.wait_ge(serV.h, v_ot)
                        for nh in range(2):
                            half = pC[:, nh * TG:(nh + 1) * TG]
                            for hh in range(NHEAD):
                                ins = TEN.matmul(
                                    half, oTa[:, hh * 128:(hh + 1) * 128],
                                    wos[:, hh * 1024 + nh * TG:
                                        hh * 1024 + (nh + 1) * TG],
                                    start=(hh == 0), stop=(hh == NHEAD - 1))
                        s_wo.inc(ins)
                        s_wo.wait(VEC)
                        Xsl = X[:, ch * D:(ch + 1) * D]
                        s_woev.inc(VEC.tensor_add(Xsl, Xsl, pC))
                TEN.wait_ge(s_woev.h, s_woev.v)
                SCA.wait_ge(s_woev.h, s_woev.v)

    def x_fence():
        for eng in (TEN, VEC, SCA, SYN):
            for s in (s_xin2[0], s_yev, s_woev, s_xupd):
                s.wait(eng)

    x_fence()
    last_ffn = None
    if "ffn1" in phases:
        last_ffn = emit_ffn_call()
        x_fence()
    if "gla" in phases:
        emit_gla_call()
        last_ffn = None
        x_fence()
    if "ffn2" in phases:
        last_ffn = emit_ffn_call()
        x_fence()

    for s in (s_xin2[0], s_yev, s_woev):
        s.wait(GPS)
    for jt in range(NT):
        if last_ffn is not None:
            GPS.wait_ge(s_xupd.h, last_ffn["XU0"] + jt + 1)
        GPS.dma_start(out=d_y[jt * 128:(jt + 1) * 128, :],
                      in_=X[:, jt * D:(jt + 1) * D]).then_inc(s_yw2[0].h, 16)
        s_yw2[0].v += 16
    SCA.wait_ge(s_yw2[0].h, s_yw2[0].v)

    ctx.close()
    return nc


def _in_map(inputs, core):
    a = lambda v: np.ascontiguousarray(np.asarray(v, dtype=np.float32))
    return {
        "x": a(inputs["L0_attn_x"][core]),
        "w1": a(inputs["L0_ffn_w1"]),
        "b1": a(inputs["L0_ffn_b1"]),
        "w2": a(inputs["L0_ffn_w2"]),
        "b2": a(inputs["L0_ffn_b2"]),
        "gw": a(inputs["L0_ffn_gate_w"]),
        "gb": a(inputs["L0_ffn_gate_b"]),
        "wq": a(inputs["L0_attn_Wq"]),
        "wk": a(inputs["L0_attn_Wk"]),
        "wv": a(inputs["L0_attn_Wv"]),
        "wg": a(inputs["L0_attn_Wg"]),
        "wo": a(inputs["L0_attn_Wo"]),
        "wgk1": a(inputs["L0_attn_Wgk1"]),
        "wgk2": a(inputs["L0_attn_Wgk2"]),
        "bgk2": a(inputs["L0_attn_bgk2"]),
        "gnw": a(inputs["L0_attn_g_norm_w"]),
    }


def kernel(**inputs):
    n = 8
    nc = build_core_program()
    in_maps = [_in_map(inputs, c) for c in range(n)]
    res = run_bass_kernel_spmd(nc, in_maps, core_ids=list(range(n)))
    return np.stack([res.results[c]["y"] for c in range(n)], axis=0)


# version 16
# speedup vs baseline: 1.0198x; 1.0004x over previous
import numpy as np
import ml_dtypes
from contextlib import ExitStack

import concourse.bass as bass
import concourse.mybir as mybir
from concourse.bass_utils import run_bass_kernel_spmd
from concourse.bass_core_rust import IndirectOffsetOnAxis
from concourse.expressions import RuntimeValue

dsl = bass.ds

F32 = mybir.dt.float32
BF16 = mybir.dt.bfloat16
AF = mybir.ActivationFunctionType
ALU = mybir.AluOpType
AX = mybir.AxisListType.X

T_TOK = 4096
D = 1024
E = 8
HID = 4096
NHEAD = 8
DH = 128
GLR = 16
GLN = 16.0
EPS = 1e-5
NT = T_TOK // 128
NTG = 8
TG = 512
NKC = D // 128
HKS = float(DH) ** -0.5
BIGI = 1024.0


class Sem:
    __slots__ = ("h", "v")

    def __init__(self, nc, ctx, name):
        self.h = ctx.enter_context(nc.semaphore(name))
        self.v = 0

    def inc(self, ins, n=1):
        ins.then_inc(self.h, n)
        self.v += n
        return self.v

    def wait(self, eng, v=None):
        eng.wait_ge(self.h, self.v if v is None else v)


def build_core_program(phases=("ffn1", "gla", "ffn2")):
    nc = bass.Bass("TRN2", target_bir_lowering=False)
    TEN, VEC, SCA, SYN, GPS = nc.tensor, nc.vector, nc.scalar, nc.sync, nc.gpsimd

    d_x = nc.dram_tensor("x", [T_TOK, D], F32, kind="ExternalInput").ap()
    d_w1 = nc.dram_tensor("w1", [E, HID, D], F32, kind="ExternalInput").ap()
    d_b1 = nc.dram_tensor("b1", [E, HID], F32, kind="ExternalInput").ap()
    d_w2 = nc.dram_tensor("w2", [E, D, HID], F32, kind="ExternalInput").ap()
    d_b2 = nc.dram_tensor("b2", [E, D], F32, kind="ExternalInput").ap()
    d_gw = nc.dram_tensor("gw", [E, D], F32, kind="ExternalInput").ap()
    d_gb = nc.dram_tensor("gb", [E], F32, kind="ExternalInput").ap()
    d_wq = nc.dram_tensor("wq", [D, D], F32, kind="ExternalInput").ap()
    d_wk = nc.dram_tensor("wk", [D, D], F32, kind="ExternalInput").ap()
    d_wv = nc.dram_tensor("wv", [D, D], F32, kind="ExternalInput").ap()
    d_wg = nc.dram_tensor("wg", [D, D], F32, kind="ExternalInput").ap()
    d_wo = nc.dram_tensor("wo", [D, D], F32, kind="ExternalInput").ap()
    d_wgk1 = nc.dram_tensor("wgk1", [D, GLR], F32, kind="ExternalInput").ap()
    d_wgk2 = nc.dram_tensor("wgk2", [GLR, D], F32, kind="ExternalInput").ap()
    d_bgk2 = nc.dram_tensor("bgk2", [D], F32, kind="ExternalInput").ap()
    d_gnw = nc.dram_tensor("gnw", [DH], F32, kind="ExternalInput").ap()
    d_y = nc.dram_tensor("y", [T_TOK, D], F32, kind="ExternalOutput").ap()

    d_w1b = nc.dram_tensor("w1b", [E, HID, D], BF16, kind="Internal").ap()
    d_w2b = nc.dram_tensor("w2b", [E, D, HID], BF16, kind="Internal").ap()
    d_wqb = nc.dram_tensor("wqb", [D, D], BF16, kind="Internal").ap()
    d_wkb = nc.dram_tensor("wkb", [D, D], BF16, kind="Internal").ap()
    d_wvb = nc.dram_tensor("wvb", [D, D], BF16, kind="Internal").ap()
    d_wgb = nc.dram_tensor("wgb", [D, D], BF16, kind="Internal").ap()
    d_wob = nc.dram_tensor("wob", [D, D], BF16, kind="Internal").ap()
    d_wgk1b = nc.dram_tensor("wgk1b", [D, GLR], BF16, kind="Internal").ap()
    d_wgk2b = nc.dram_tensor("wgk2b", [GLR, D], BF16, kind="Internal").ap()
    d_b2b = nc.dram_tensor("b2b", [E, D], BF16, kind="Internal").ap()
    d_xb = nc.dram_tensor("xb16", [T_TOK, D], BF16, kind="Internal").ap()
    d_xnt = nc.dram_tensor("xnt", [NKC, 128, T_TOK], BF16, kind="Internal").ap()
    d_xns = nc.dram_tensor("xns", [T_TOK + 128, D], BF16, kind="Internal").ap()
    d_xs = nc.dram_tensor("xs", [15 * TG, D], BF16, kind="Internal").ap()
    d_ysort = nc.dram_tensor("ysort", [15 * TG, D], BF16, kind="Internal").ap()
    d_srq = nc.dram_tensor("srq", [15 * TG, 1], mybir.dt.int32, kind="Internal").ap()
    d_w1bT = nc.dram_tensor("w1bT", [E, 4, 128, 8 * 1024], BF16,
                            kind="Internal").ap()
    d_w2bT = nc.dram_tensor("w2bT", [E, 4, 128, 8 * 1024], BF16,
                            kind="Internal").ap()
    d_b1t = nc.dram_tensor("b1t", [E, 128, 32], F32, kind="Internal").ap()

    c_id_bf = nc.inline_tensor(np.eye(128, dtype=ml_dtypes.bfloat16), "c_id_bf").ap()
    c_id_f32 = nc.inline_tensor(np.eye(128, dtype=np.float32), "c_id_f32").ap()
    c_ones = nc.inline_tensor(np.ones((1, 128), dtype=ml_dtypes.bfloat16), "c_ones").ap()
    c_iota = nc.inline_tensor(
        (np.arange(8, dtype=np.float32) + BIGI)[None, :].repeat(128, 0), "c_iota").ap()
    c_u = nc.inline_tensor(
        np.triu(np.ones((128, 128), dtype=np.float32)).astype(ml_dtypes.bfloat16),
        "c_u").ap()
    c_cb = nc.inline_tensor(
        np.array([[EPS, 1.0]], dtype=np.float32).repeat(128, 0), "c_cb").ap()
    c_onec = nc.inline_tensor(np.ones((128, 1), dtype=ml_dtypes.bfloat16),
                              "c_onec").ap()
    c_us = nc.inline_tensor(
        np.triu(np.ones((128, 128), dtype=np.float32), 1).astype(
            ml_dtypes.bfloat16), "c_us").ap()
    c_i128 = nc.inline_tensor(np.arange(128, dtype=np.float32)[:, None],
                              "c_i128").ap()

    ctx = ExitStack()
    sem = lambda n: Sem(nc, ctx, n)
    sbuf = lambda n, sh, dt: ctx.enter_context(nc.sbuf_tensor(n, sh, dt))[:]

    X = sbuf("X", [128, NT * D], BF16)
    id_bf = sbuf("id_bf", [128, 128], BF16)
    id_f32 = sbuf("id_f32", [128, 128], F32)
    ones = sbuf("ones", [1, 128], BF16)
    iotaB = sbuf("iotaB", [128, 8], F32)
    u_bf = sbuf("u_bf", [128, 128], BF16)
    masks = sbuf("masks", [128, NT * E], F32)
    b1c = sbuf("b1c", [128, 32 * E], F32)
    gbr = sbuf("gbr", [1, 8], BF16)
    bgk2r = sbuf("bgk2r", [1, D], BF16)
    gwT = sbuf("gwT", [128, NKC * E], BF16)
    gnwb = sbuf("gnwb", [128, 128], F32)
    cb = sbuf("cb", [128, 2], F32)
    onec = sbuf("onec", [128, 1], BF16)
    us_bf = sbuf("us_bf", [128, 128], BF16)
    i128 = sbuf("i128", [128, 1], F32)


    serV, serS = sem("serV"), sem("serS")

    def barV(ins):
        serV.inc(ins)
        serV.wait(VEC)
        return serV.v

    def barS(ins):
        serS.inc(ins)
        serS.wait(SCA)
        return serS.v

    s_cast_g = sem("s_cast_g")
    s_cast_b2 = sem("s_cast_b2")
    s_b2l = sem("s_b2l")
    s_cld = sem("s_cld")
    s_prep = sem("s_prep")
    s_xin2 = [sem("s_xin0"), sem("s_xin1")]
    s_xn = sem("s_xn")
    s_trT = sem("s_trT")
    s_psTev = sem("s_psTev")
    s_xw = [sem("s_xw0"), sem("s_xw1")]
    s_lgT = sem("s_lgT")
    s_wl = [sem("s_wl0"), sem("s_wl1")]
    s_xl = [sem("s_xl0"), sem("s_xl1"), sem("s_xl2")]
    s_s1 = sem("s_s1")
    s_hev = sem("s_hev")
    s_s2 = sem("s_s2")
    s_yev = sem("s_yev")
    s_gld = sem("s_gld")
    s_xgev = sem("s_xgev")
    s_pev = sem("s_pev")
    s_z1ev = sem("s_z1ev")
    s_projT = sem("s_projT")
    s_z1T = sem("s_z1T")
    s_zT = sem("s_zT")
    s_b = sem("s_b")
    s_eb = sem("s_eb")
    s_ehb = sem("s_ehb")
    s_qk = sem("s_qk")
    s_kh = sem("s_kh")
    s_S = sem("s_S")
    s_Sm = sem("s_Sm")
    s_tr = sem("s_tr")
    s_trev = sem("s_trev")
    s_o = sem("s_o")
    s_H = sem("s_H")
    s_silg = sem("s_silg")
    s_ofin = sem("s_ofin")
    s_otr = sem("s_otr")
    s_otev = sem("s_otev")
    s_wo = sem("s_wo")
    s_woev = sem("s_woev")
    s_yw2 = [sem("s_yw0"), sem("s_yw1")]
    s_cw1e = [sem(f"s_cw1_{e}") for e in range(E)]
    s_cw2e = [sem(f"s_cw2_{e}") for e in range(E)]
    s_rl4 = [sem(f"s_rl{i}") for i in range(4)]
    s_rw4 = [sem(f"s_rw{i}") for i in range(4)]
    s_gws = [sem("s_gws0"), sem("s_gws1")]
    s_gq = sem("s_gq")
    s_sc = sem("s_sc")
    s_xupd = sem("s_xupd")
    s_b1g = sem("s_b1g")
    s_scat = sem("s_scat")
    s_zf = sem("s_zf")

    GPS.dma_start(out=d_b2b, in_=d_b2).then_inc(s_cast_b2.h, 16)
    s_cast_b2.v += 16
    for e in range(E):
        GPS.dma_start(out=d_w1b[e], in_=d_w1[e]).then_inc(s_cw1e[e].h, 16)
        s_cw1e[e].v += 16
        GPS.dma_start(out=d_w2b[e], in_=d_w2[e]).then_inc(s_cw2e[e].h, 16)
        s_cw2e[e].v += 16

    for dst, src in ((id_bf, c_id_bf), (id_f32, c_id_f32), (ones, c_ones),
                     (iotaB, c_iota), (u_bf, c_u), (cb, c_cb),
                     (onec, c_onec), (us_bf, c_us), (i128, c_i128)):
        SYN.dma_start(out=dst, in_=src).then_inc(s_cld.h, 16)
        s_cld.v += 16

    with ExitStack() as p0:
        p_sb = lambda n, sh, dt: p0.enter_context(nc.sbuf_tensor(n, sh, dt))[:]
        b1n = p_sb("b1n", [8, HID], F32)
        gbf = p_sb("gbf", [1, 8], F32)
        bgk2f = p_sb("bgk2f", [1, D], F32)
        gwn = p_sb("gwn", [8, D], F32)
        gwnb = p_sb("gwnb", [8, D], BF16)
        gnr = p_sb("gnr", [1, DH], F32)
        gnrb = p_sb("gnrb", [1, DH], BF16)
        ps_prep = p0.enter_context(nc.psum_tensor("ps_prep", [128, 32], F32))[:]
        ps_gw = p0.enter_context(nc.psum_tensor("ps_gw", [128, 64], BF16))[:]
        ps_gn = p0.enter_context(nc.psum_tensor("ps_gn", [128, 128], F32))[:]

        for dst, srct in ((b1n, d_b1), (gbf, d_gb[None, :]),
                          (bgk2f, d_bgk2[None, :]),
                          (gwn, d_gw), (gnr, d_gnw[None, :])):
            SYN.dma_start(out=dst, in_=srct).then_inc(s_cld.h, 16)
            s_cld.v += 16
        ld_all = s_cld.v

        VEC.wait_ge(s_cld.h, ld_all)
        VEC.tensor_copy(out=gbr, in_=gbf)
        VEC.tensor_copy(out=bgk2r, in_=bgk2f)
        VEC.tensor_copy(out=gwnb, in_=gwn)
        casts_done = barV(VEC.tensor_copy(out=gnrb, in_=gnr))

        TEN.wait_ge(s_cld.h, ld_all)
        TEN.wait_ge(serV.h, casts_done)
        for blk in range(8):
            for j in range(4):
                c = blk * 4 + j
                ins = TEN.transpose(ps_prep[:, j * 8:(j + 1) * 8],
                                    b1n[0:8, c * 128:(c + 1) * 128],
                                    id_f32[0:8, 0:8])
            s_prep.inc(ins)
            s_prep.wait(VEC)
            ev = barV(VEC.tensor_copy(out=b1c[:, blk * 32:(blk + 1) * 32],
                                      in_=ps_prep))
            TEN.wait_ge(serV.h, ev)
        for kc in range(NKC):
            ins = TEN.transpose(ps_gw[:, kc * 8:(kc + 1) * 8],
                                gwnb[0:8, kc * 128:(kc + 1) * 128],
                                id_bf[0:8, 0:8])
        s_prep.inc(ins)
        s_prep.wait(VEC)
        ev = barV(VEC.tensor_copy(out=gwT, in_=ps_gw))
        TEN.wait_ge(serV.h, ev)
        s_prep.inc(TEN.matmul(ps_gn, ones, gnrb, start=True, stop=True))
        s_prep.wait(VEC)
        prep_done_v = barV(VEC.tensor_copy(out=gnwb, in_=ps_gn))
        TEN.wait_ge(serV.h, prep_done_v)
        SCA.wait_ge(serV.h, prep_done_v)
        s_b1t = sem("s_b1t")
        b1t_sb = p_sb("b1t_sb", [128, 32 * E], F32)
        v_b1t = barV(VEC.tensor_copy(
            out=b1t_sb.rearrange("p (e c) -> p e c", e=E),
            in_=b1c.rearrange("p (c e) -> p e c", e=E)))
        SCA.wait_ge(serV.h, v_b1t)
        SCA.dma_start(out=d_b1t.rearrange("e p c -> p e c"),
                      in_=b1t_sb.rearrange("p (e c) -> p e c", e=E)
                      ).then_inc(s_b1t.h, 16)
        s_b1t.v += 16

    Xsrc = d_x.rearrange("(t p) d -> p t d", p=128)
    Xdst = X.rearrange("p (t d) -> p t d", d=D)
    GPS.dma_start(out=Xdst, in_=Xsrc).then_inc(s_xin2[0].h, 16)
    s_xin2[0].v += 16

    ln_bufs = {}
    _uid = [0]

    def alloc_ln_bufs(scope, with_sqd=False, sqw_tiles=4):
        _uid[0] += 1
        u = _uid[0]
        g = lambda n, sh, dt: scope.enter_context(
            nc.sbuf_tensor(f"{n}_u{u}", sh, dt))[:]
        for nme in ("scol", "nmc", "m2c", "ssqc", "varc", "lnvc", "rstdc",
                    "negmrc"):
            ln_bufs[nme] = [g(f"{nme}{i}", [128, 4], F32) for i in range(2)]
        nsq = sqw_tiles + (1 if with_sqd else 0)
        ln_bufs["sqw"] = g("sqw", [128, nsq * D], BF16)
        ln_bufs["sqd"] = (ln_bufs["sqw"][:, sqw_tiles * D:nsq * D]
                          if with_sqd else None)
        ln_bufs["xn"] = [g(f"xn{i}", [128, D], BF16) for i in range(2)]
        ln_bufs["lgf"] = g("lgf", [128, 8], F32)
        ln_bufs["tmask"] = g("tmask", [128, 8], F32)
        ln_bufs["idxm"] = g("idxm", [128, 1], F32)
        ln_bufs["sq_war"] = [0]
        ln_bufs["xn_gate"] = None

    def emit_ln_xn(tt, k, i, trt0):
        lb = ln_bufs
        if tt >= 2:
            VEC.wait_ge(s_trT.h, trt0 + tt - 1)
            if lb["xn_gate"] is not None:
                lb["xn_gate"](tt)
        s_xn.inc(VEC.tensor_scalar(
            out=lb["xn"][tt % 2], in0=X[:, tt * D:(tt + 1) * D],
            scalar1=lb["rstdc"][k][:, i:i + 1],
            scalar2=lb["negmrc"][k][:, i:i + 1],
            op0=ALU.mult, op1=ALU.add))

    def emit_ln_quad(tt0, nq, k, trt0):
        lb = ln_bufs
        Xq = X[:, tt0 * D:(tt0 + nq) * D]
        X3 = Xq.rearrange("p (t d) -> p t d", d=D)
        sc = lb["scol"][k][:, 0:nq]
        barV(VEC.tensor_reduce(sc, X3, axis=AX, op=ALU.add))
        VEC.tensor_scalar(out=lb["nmc"][k][:, 0:nq], in0=sc, scalar1=-1.0 / D,
                          scalar2=None, op0=ALU.mult)
        barV(VEC.tensor_scalar(out=lb["m2c"][k][:, 0:nq], in0=sc,
                               scalar1=None, scalar2=1.0 / (D * D),
                               op0=ALU.mult, op1=ALU.mult)
             if False else
             VEC.tensor_tensor(lb["m2c"][k][:, 0:nq], sc, sc, op=ALU.mult))
        barV(VEC.tensor_scalar(out=lb["m2c"][k][:, 0:nq],
                               in0=lb["m2c"][k][:, 0:nq], scalar1=1.0 / (D * D),
                               scalar2=None, op0=ALU.mult))
        SCA.wait_ge(serV.h, lb["sq_war"][0])
        v_sq = barS(SCA.activation(lb["sqw"][:, 0:nq * D], Xq, AF.Square))
        VEC.wait_ge(serS.h, v_sq)
        sq3 = lb["sqw"][:, 0:nq * D].rearrange("p (t d) -> p t d", d=D)
        v_red = barV(VEC.tensor_reduce(lb["ssqc"][k][:, 0:nq], sq3, axis=AX,
                                       op=ALU.add))
        lb["sq_war"][0] = v_red
        v_var = barV(VEC.scalar_tensor_tensor(
            out=lb["varc"][k][:, 0:nq], in0=lb["ssqc"][k][:, 0:nq],
            scalar=1.0 / D, in1=lb["m2c"][k][:, 0:nq],
            op0=ALU.mult, op1=ALU.subtract))
        SCA.wait_ge(serV.h, v_var)
        barS(SCA.activation(lb["lnvc"][k][:, 0:nq], lb["varc"][k][:, 0:nq],
                            AF.Ln, bias=cb[:, 0:1]))
        v_rstd = barS(SCA.activation(lb["rstdc"][k][:, 0:nq],
                                     lb["lnvc"][k][:, 0:nq], AF.Exp, scale=-0.5))
        VEC.wait_ge(serS.h, v_rstd)
        barV(VEC.tensor_mul(lb["negmrc"][k][:, 0:nq], lb["nmc"][k][:, 0:nq],
                            lb["rstdc"][k][:, 0:nq]))


    _restaged = [False]

    def emit_ffn_call():
        ret = {}
        with ExitStack() as fs:
            _uid[0] += 1
            _u = _uid[0]
            g = lambda n, sh, dt: fs.enter_context(
                nc.sbuf_tensor(f"{n}_u{_u}", sh, dt))[:]
            alloc_ln_bufs(fs)
            xnts = [g(f"xnts{i}", [128, D], BF16) for i in range(2)]
            b2n = g("b2n", [8, D], BF16)
            mTb = g("mTb", [8, 128], BF16)
            wb1 = [g(f"wb1_{i}", [128, NKC * 1024], BF16) for i in range(2)]
            wb2 = [g(f"wb2_{i}", [128, NKC * 1024], BF16) for i in range(2)]
            xst = [g(f"xst{i}", [128, NKC * TG], BF16) for i in range(2)]
            hT = [g(f"hT{i}", [128, NKC * TG], BF16) for i in range(2)]
            y_acc = g("y_acc", [128, 4 * D], BF16)
            b1sel = [g(f"b1sel{i}", [128, 32], F32) for i in range(2)]
            didxq = g("didxq", [128, NT], mybir.dt.int32)
            tokall = g("tokall", [128, NT], mybir.dt.int32)
            srq_sb = g("srq_sb", [128, 60], mybir.dt.int32)
            zi = g("zi", [128, 60], mybir.dt.int32)
            gst = [g(f"gst{i}", [128, D], BF16) for i in range(2)]
            ystage = [g(f"ystage{i}", [128, D], BF16) for i in range(2)]
            selqi = g("selqi", [1, 16], mybir.dt.int32)
            maskb = g("maskb", [128, NT * E], BF16)
            ibx = g("ibx", [128, 912], F32)
            didxp = g("didxp", [128, NT], mybir.dt.int32)
            selbase = g("selbase", [128, 16], F32)
            wixi = [g(f"wixi{i}", [128, 2], mybir.dt.int32) for i in range(2)]
            with ExitStack() as ps:
                psT = ps.enter_context(nc.psum_tensor(f"psT_u{_u}", [128, D], BF16))[:]
                lg_ps = ps.enter_context(nc.psum_tensor(f"lg_ps_u{_u}", [128, 136], F32))[:]
                s1p = [ps.enter_context(nc.psum_tensor(f"s1p{i}_u{_u}", [128, TG], F32))[:]
                       for i in range(2)]
                s2p = [ps.enter_context(nc.psum_tensor(f"s2p{i}_u{_u}", [128, D], F32))[:]
                       for i in range(2)]

                TRT0, PST0, LGT0 = s_trT.v, s_psTev.v, s_lgT.v
                XN0 = s_xn.v
                ret["YEV0"] = s_yev.v
                S10, HEV0, S20, YEV0 = s_s1.v, s_hev.v, s_s2.v, s_yev.v
                NTI = 15
                NJQ = NTI * 4
                lgev_val = [0] * NT
                b2add_val = [0] * NT
                xw_val = [0] * NT
                wl_val = [0] * NJQ
                b1_val = [0] * NTI
                xl2_val = [0] * NTI
                gws_j_val = [0] * 60
                v_psr = [0] * NT
                ret["XU0"] = s_xupd.v
                XU0, SC0 = s_xupd.v, s_sc.v
                ln_bufs["xn_gate"] = (
                    lambda tt: VEC.wait_ge(s_xw[tt % 2].h, xw_val[tt - 2]))
                sqw = None

                SCA.wait_ge(serV.h, prep_done_v)
                TEN.wait_ge(serV.h, prep_done_v)
                SYN.wait_ge(serV.h, prep_done_v)
                SYN.wait_ge(s_cast_b2.h, 16)
                SYN.dma_start(out=b2n, in_=d_b2b).then_inc(s_b2l.h, 16)
                s_b2l.v += 16
                TEN.wait_ge(s_b2l.h, s_b2l.v)

                sqw = ln_bufs["sqw"]
                barV(VEC.memset(ibx[:, 0:60], 4096.0))
                barV(VEC.tensor_copy(out=zi, in_=ibx[:, 0:60]))
                if not _restaged[0]:
                    v_g0 = barV(VEC.memset(gst[0], 0.0))
                    GPS.wait_ge(serV.h, v_g0)
                    GPS.dma_start(out=d_xns[T_TOK:T_TOK + 128, :],
                                  in_=gst[0]).then_inc(s_zf.h, 16)
                    s_zf.v += 16
                for t_ in range(NT):
                    ins = VEC.tensor_scalar(out=ibx[:, 64 + t_:65 + t_],
                                            in0=i128, scalar1=float(t_ * 128),
                                            scalar2=None, op0=ALU.add)
                barV(ins)
                v_pref = barV(VEC.tensor_copy(out=tokall, in_=ibx[:, 64:96]))
                ln_bufs["sq_war"][0] = v_pref
                GPS.wait_ge(serV.h, v_pref)
                GPS.dma_start(out=d_srq.rearrange("(p t) o -> p (t o)", p=128),
                              in_=zi).then_inc(s_zf.h, 16)
                s_zf.v += 16

                for tt in range(NT):
                    k = tt % 2
                    if tt % 4 == 0:
                        emit_ln_quad(tt, 4, (tt // 4) % 2, TRT0)
                    emit_ln_xn(tt, (tt // 4) % 2, tt % 4, TRT0)
                    TEN.wait_ge(s_xn.h, XN0 + tt + 1)
                    if tt >= 1:
                        TEN.wait_ge(s_psTev.h, PST0 + tt)
                    for j in range(NKC):
                        ins = TEN.transpose(psT[:, j * 128:(j + 1) * 128],
                                            ln_bufs["xn"][k][:, j * 128:(j + 1) * 128],
                                            id_bf)
                    s_trT.inc(ins)
                    VEC.wait_ge(s_trT.h, TRT0 + tt + 1)
                    if tt >= 2:
                        VEC.wait_ge(s_xw[tt % 2].h, xw_val[tt - 2])
                    s_psTev.inc(VEC.tensor_copy(out=xnts[k], in_=psT))

                    def _emit_xnt_write(tw):
                        kk = tw % 2
                        SCA.wait_ge(s_xn.h, XN0 + tw + 1)
                        SCA.dma_start(out=d_xns[tw * 128:(tw + 1) * 128, :],
                                      in_=ln_bufs["xn"][kk]).then_inc(
                            s_xw[tw % 2].h, 16)
                        s_xw[tw % 2].v += 16
                        xw_val[tw] = s_xw[tw % 2].v

                    if tt >= 1:
                        _emit_xnt_write(tt - 1)
                    if tt == NT - 1:
                        _emit_xnt_write(tt)
                    TEN.wait_ge(s_psTev.h, PST0 + tt + 1)
                    if tt >= 1:
                        TEN.wait_ge(serV.h, lgev_val[tt - 1])
                    for kc in range(NKC):
                        TEN.matmul(lg_ps[:, 0:8],
                                   xnts[k][:, kc * 128:(kc + 1) * 128],
                                   gwT[:, kc * 8:(kc + 1) * 8],
                                   start=(kc == 0), stop=False)
                    s_lgT.inc(TEN.matmul(lg_ps[:, 0:8], ones, gbr, start=False,
                                         stop=True))
                    VEC.wait_ge(s_lgT.h, s_lgT.v)
                    lgev_val[tt] = barV(VEC.tensor_copy(out=ln_bufs["lgf"],
                                                        in_=lg_ps[:, 0:8]))
                    barV(VEC.reduce_max(ln_bufs["idxm"], ln_bufs["lgf"], axis=AX))
                    barV(VEC.tensor_scalar(out=ln_bufs["tmask"], in0=ln_bufs["lgf"],
                                           scalar1=ln_bufs["idxm"], scalar2=-BIGI,
                                           op0=ALU.is_ge, op1=ALU.mult))
                    barV(VEC.tensor_add(ln_bufs["tmask"], ln_bufs["tmask"], iotaB))
                    barV(VEC.tensor_reduce(ln_bufs["idxm"], ln_bufs["tmask"],
                                           axis=AX, op=ALU.min))
                    for e in range(E):
                        ins = VEC.tensor_scalar(
                            out=masks[:, (tt * E + e):(tt * E + e + 1)],
                            in0=ln_bufs["idxm"], scalar1=float(e),
                            scalar2=0.5, op0=ALU.is_equal, op1=ALU.mult)
                    v_mask = barV(ins)
                    TEN.wait_ge(serV.h, v_mask)
                    ins = TEN.transpose(lg_ps[0:8, 8:136],
                                        masks[:, tt * E:(tt + 1) * E], id_f32)
                    s_lgT.inc(ins)
                    VEC.wait_ge(s_lgT.h, s_lgT.v)
                    v_mtb = barV(VEC.tensor_copy(out=mTb, in_=lg_ps[0:8, 8:136]))
                    TEN.wait_ge(serV.h, v_mtb)
                    if tt >= 2:
                        TEN.wait_ge(serV.h, b2add_val[tt - 2])
                    for nh in range(2):
                        ins = TEN.matmul(s2p[tt % 2][:, nh * TG:(nh + 1) * TG],
                                         mTb, b2n[0:8, nh * TG:(nh + 1) * TG],
                                         start=True, stop=True)
                        s_lgT.inc(ins)
                    VEC.wait_ge(s_lgT.h, s_lgT.v)
                    for nh in range(2):
                        Xh = X[:, tt * D + nh * TG:tt * D + (nh + 1) * TG]
                        ins = VEC.tensor_add(Xh, Xh,
                                             s2p[tt % 2][:, nh * TG:(nh + 1) * TG])
                    b2add_val[tt] = barV(ins)

                if not _restaged[0]:
                    rst = xst[0]
                    rl_val = [0] * 4
                    rw_val = [0] * 4
                    g2 = 0
                    for e in range(E):
                        for dsrc, dT, csem in ((d_w1b, d_w1bT, s_cw1e[e]),
                                               (d_w2b, d_w2bT, s_cw2e[e])):
                            for ti in range(32):
                                q_, kc_ = divmod(ti, 8)
                                sl = g2 % 4
                                if ti == 0:
                                    SYN.wait_ge(csem.h, csem.v)
                                if g2 >= 4:
                                    SYN.wait_ge(s_rw4[sl].h, rw_val[sl])
                                if dsrc is d_w1b:
                                    wsrc = dsrc[e, q_ * 1024:(q_ + 1) * 1024,
                                                kc_ * 128:(kc_ + 1) * 128]
                                else:
                                    wsrc = dsrc[e, :, q_ * 1024 + kc_ * 128:
                                                q_ * 1024 + (kc_ + 1) * 128]
                                SYN.dma_start(
                                    out=rst[:, sl * 1024:(sl + 1) * 1024],
                                    in_=wsrc, transpose=True).then_inc(
                                    s_rl4[sl].h, 16)
                                s_rl4[sl].v += 16
                                rl_val[sl] = s_rl4[sl].v
                                SCA.wait_ge(s_rl4[sl].h, rl_val[sl])
                                SCA.dma_start(
                                    out=dT[e, q_, :,
                                           kc_ * 1024:(kc_ + 1) * 1024],
                                    in_=rst[:, sl * 1024:(sl + 1) * 1024]
                                ).then_inc(s_rw4[sl].h, 16)
                                s_rw4[sl].v += 16
                                rw_val[sl] = s_rw4[sl].v
                                g2 += 1
                    _restaged[0] = True

                ib = ibx[0:1, 0:912]
                onesf = ibx[0:1, 0:128]
                Z32 = ib[:, 512:544]
                v0 = barV(VEC.tensor_scalar(out=maskb, in0=masks, scalar1=2.0,
                                            scalar2=None, op0=ALU.mult))
                TEN.wait_ge(serV.h, v0)
                for tt in range(NT):
                    ins = TEN.matmul(s1p[0][0:1, tt * 8:(tt + 1) * 8], onec,
                                     maskb[:, tt * 8:(tt + 1) * 8],
                                     start=True, stop=True)
                s_lgT.inc(ins)
                VEC.wait_ge(s_lgT.h, s_lgT.v)
                barV(VEC.tensor_copy(out=ib[:, 0:256], in_=s1p[0][0:1, 0:256]))
                barV(VEC.tensor_copy(
                    out=ib[:, 256:512].rearrange("p (e t) -> p e t", t=NT),
                    in_=ib[:, 0:256].rearrange("p (t e) -> p e t", e=E)))
                barV(VEC.memset(Z32, 0.0))
                for e in range(E):
                    ins = VEC.tensor_tensor_scan(
                        out=ib[:, 544 + e * NT:544 + (e + 1) * NT],
                        data0=ib[:, 256 + e * NT:256 + (e + 1) * NT],
                        data1=Z32, initial=0.0, op0=ALU.add, op1=ALU.add)
                barV(ins)
                barV(VEC.tensor_sub(ib[:, 256:512], ib[:, 544:800],
                                    ib[:, 256:512]))
                barV(VEC.tensor_copy(
                    out=ib[:, 800:808],
                    in_=ib[:, 544:800].rearrange(
                        "p (e t) -> p e t", t=NT)[:, :, NT - 1]))
                barV(VEC.memset(ib[:, 816:824], 0.0))
                for jth in range(8):
                    barV(VEC.tensor_scalar(out=ib[:, 808:816],
                                           in0=ib[:, 800:808],
                                           scalar1=float(512 * jth),
                                           scalar2=512.0, op0=ALU.is_gt,
                                           op1=ALU.mult))
                    barV(VEC.tensor_add(ib[:, 816:824], ib[:, 816:824],
                                        ib[:, 808:816]))
                barV(VEC.tensor_tensor_scan(out=ib[:, 824:832],
                                            data0=ib[:, 816:824],
                                            data1=Z32[:, 0:8], initial=0.0,
                                            op0=ALU.add, op1=ALU.add))
                barV(VEC.tensor_sub(ib[:, 832:840], ib[:, 824:832],
                                    ib[:, 816:824]))
                for e in range(E):
                    ins = VEC.tensor_scalar(
                        out=ib[:, 256 + e * NT:256 + (e + 1) * NT],
                        in0=ib[:, 256 + e * NT:256 + (e + 1) * NT],
                        scalar1=ib[:, 832 + e:833 + e], scalar2=None,
                        op0=ALU.add)
                barV(ins)
                barV(VEC.tensor_copy(
                    out=ib[:, 544:800].rearrange("p (t e) -> p t e", e=E),
                    in_=ib[:, 256:512].rearrange("p (e t) -> p t e", t=NT)))
                for i in range(NTI):
                    barV(VEC.tensor_scalar(out=ib[:, 856:864],
                                           in0=ib[:, 832:840],
                                           scalar1=float(512 * i),
                                           scalar2=None, op0=ALU.is_le))
                    barV(VEC.tensor_reduce(ib[:, 840 + i:841 + i],
                                           ib[:, 856:864], axis=AX,
                                           op=ALU.add))
                barV(VEC.tensor_scalar(out=ib[:, 840:855], in0=ib[:, 840:855],
                                       scalar1=-1.0, scalar2=None,
                                       op0=ALU.add))
                v_selq = barV(VEC.tensor_copy(out=selqi[:, 0:NTI],
                                              in_=ib[:, 840:855]))
                barV(VEC.tensor_scalar(out=ib[:, 256:271], in0=ib[:, 840:855],
                                       scalar1=512.0, scalar2=None,
                                       op0=ALU.mult))
                v_1f = barV(VEC.memset(onesf, 1.0))
                TEN.wait_ge(serV.h, v_1f)
                ins = TEN.matmul(lg_ps[:, 16:31], onesf, ib[:, 256:271],
                                 start=True, stop=True)
                s_lgT.inc(ins)
                VEC.wait_ge(s_lgT.h, s_lgT.v)
                barV(VEC.tensor_copy(out=selbase[:, 0:15],
                                     in_=lg_ps[:, 16:31]))

                for tt in range(NT):
                    TEN.wait_ge(serV.h, v_selq if tt == 0 else v_psr[tt - 1])
                    TEN.matmul(lg_ps[:, 8:16], onesf,
                               ib[:, 544 + tt * 8:544 + (tt + 1) * 8],
                               start=True, stop=False)
                    ins = TEN.matmul(lg_ps[:, 8:16], us_bf,
                                     maskb[:, tt * 8:(tt + 1) * 8],
                                     start=False, stop=True)
                    s_lgT.inc(ins)
                    VEC.wait_ge(s_lgT.h, s_lgT.v)
                    v_psr[tt] = barV(VEC.tensor_mul(
                        ln_bufs["tmask"], lg_ps[:, 8:16],
                        maskb[:, tt * 8:(tt + 1) * 8]))
                    idxm = ln_bufs["idxm"]
                    barV(VEC.tensor_reduce(idxm, ln_bufs["tmask"],
                                           axis=AX, op=ALU.add))
                    barV(VEC.tensor_copy(out=didxp[:, tt:tt + 1], in_=idxm))
                    lo_ = ln_bufs["tmask"][:, 0:1]
                    t1_ = ln_bufs["tmask"][:, 1:2]
                    if tt == 0:
                        VEC.wait_ge(s_zf.h, s_zf.v)
                    barV(VEC.tensor_copy(out=zi[:, 8:9], in_=idxm))
                    barV(VEC.tensor_scalar(out=zi[:, 9:10], in0=zi[:, 8:9],
                                           scalar1=127, scalar2=None,
                                           op0=ALU.bitwise_and))
                    barV(VEC.tensor_copy(out=lo_, in_=zi[:, 9:10]))
                    barV(VEC.tensor_sub(t1_, idxm, lo_))
                    barV(VEC.tensor_scalar(out=t1_, in0=t1_,
                                           scalar1=1.0 / 128.0, scalar2=None,
                                           op0=ALU.mult))
                    barV(VEC.scalar_tensor_tensor(out=idxm, in0=lo_,
                                                  scalar=60.0, in1=t1_,
                                                  op0=ALU.mult, op1=ALU.add))
                    v_d = barV(VEC.tensor_copy(out=didxq[:, tt:tt + 1],
                                               in_=idxm))
                    GPS.wait_ge(serV.h, v_d)
                    GPS.wait_ge(s_zf.h, s_zf.v) if tt == 0 else \
                        GPS.wait_ge(s_scat.h, s_scat.v)
                    GPS.indirect_dma_start(
                        out=d_srq,
                        out_offset=IndirectOffsetOnAxis(didxq[:, tt:tt + 1], 0),
                        in_=tokall[:, tt:tt + 1], in_offset=None,
                    ).then_inc(s_scat.h, 16)
                    s_scat.v += 16

                GPS.wait_ge(s_scat.h, s_scat.v)
                GPS.dma_start(
                    out=srq_sb,
                    in_=d_srq.rearrange("(p t) o -> p (t o)", p=128)
                ).then_inc(s_gq.h, 16)
                s_gq.v += 16
                GPS.wait_ge(s_gq.h, s_gq.v)
                GPS.wait_ge(s_xw[0].h, xw_val[30])
                GPS.wait_ge(s_xw[1].h, xw_val[31])
                GPS.wait_ge(s_zf.h, s_zf.v)
                for j in range(60):
                    sl = j % 2
                    if j >= 2:
                        GPS.wait_ge(s_gws[sl].h, gws_j_val[j - 2])
                    GPS.indirect_dma_start(
                        out=gst[sl], out_offset=None, in_=d_xns,
                        in_offset=IndirectOffsetOnAxis(srq_sb[:, j:j + 1], 0),
                    ).then_inc(s_gws[sl].h, 16)
                    s_gws[sl].v += 16
                    GPS.wait_ge(s_gws[sl].h, s_gws[sl].v)
                    GPS.dma_start(out=d_xs[j * 128:(j + 1) * 128, :],
                                  in_=gst[sl]).then_inc(s_gws[sl].h, 16)
                    s_gws[sl].v += 16
                    gws_j_val[j] = s_gws[sl].v

                d_w1bT1 = d_w1bT.rearrange("e q p c -> (e q p) c")
                d_w2bT1 = d_w2bT.rearrange("e q p c -> (e q p) c")
                d_b1t1 = d_b1t.rearrange("e p c -> (e p) c")
                wix_val = [0] * NJQ

                def emit_wix(jq):
                    i, q = divmod(jq, 4)
                    sl = jq % 2
                    if jq >= 2:
                        VEC.wait_ge(s_wl[sl].h, wl_val[jq - 2])
                    ins = VEC.tensor_scalar(out=wixi[sl][:, 0:1], in0=i128,
                                            scalar1=selbase[:, i:i + 1],
                                            scalar2=float(q * 128),
                                            op0=ALU.add, op1=ALU.add)
                    if q == 0:
                        ins = VEC.scalar_tensor_tensor(
                            out=wixi[sl][:, 1:2], in0=selbase[:, i:i + 1],
                            scalar=0.25, in1=i128, op0=ALU.mult,
                            op1=ALU.add)
                    wix_val[jq] = barV(ins)

                def emit_wdyn(jq):
                    i, q = divmod(jq, 4)
                    slot = jq % 2
                    sl = jq % 2
                    if jq < 2:
                        for s4_ in range(4):
                            GPS.wait_ge(s_rw4[s4_].h, s_rw4[s4_].v)
                        GPS.wait_ge(s_b1t.h, s_b1t.v)
                    if jq >= 2:
                        GPS.wait_ge(s_s2.h, S20 + 4 * (jq - 1))
                    GPS.wait_ge(serV.h, wix_val[jq])
                    for dT1, wb_ in ((d_w1bT1, wb1), (d_w2bT1, wb2)):
                        GPS.indirect_dma_start(
                            out=wb_[slot], out_offset=None, in_=dT1,
                            in_offset=IndirectOffsetOnAxis(
                                wixi[sl][:, 0:1], 0),
                        ).then_inc(s_wl[slot].h, 16)
                        s_wl[slot].v += 16
                    wl_val[jq] = s_wl[slot].v
                    if q == 0:
                        if i >= 2:
                            GPS.wait_ge(s_hev.h, HEV0 + 32 * (i - 1))
                        GPS.indirect_dma_start(
                            out=b1sel[i % 2], out_offset=None, in_=d_b1t1,
                            in_offset=IndirectOffsetOnAxis(
                                wixi[sl][:, 1:2], 0),
                        ).then_inc(s_b1g.h, 16)
                        s_b1g.v += 16
                        b1_val[i] = s_b1g.v

                def emit_xst(i):
                    sl2 = i % 2
                    SYN.wait_ge(s_gws[(4 * i + 3) % 2].h, gws_j_val[4 * i + 3])
                    SYN.wait_ge(s_gws[(4 * i + 2) % 2].h, gws_j_val[4 * i + 2])
                    if i < 2:
                        for s4_ in range(4):
                            SYN.wait_ge(s_rw4[s4_].h, s_rw4[s4_].v)
                    if i >= 2:
                        SYN.wait_ge(s_s1.h, S10 + 32 * (i - 1))
                    for kc in range(NKC):
                        SYN.dma_start(
                            out=xst[sl2][:, kc * TG:(kc + 1) * TG],
                            in_=d_xs[i * TG:(i + 1) * TG,
                                     kc * 128:(kc + 1) * 128],
                            transpose=True).then_inc(s_xl[sl2].h, 16)
                        s_xl[sl2].v += 16
                    xl2_val[i] = s_xl[sl2].v

                def emit_s1s(jq):
                    i, q = divmod(jq, 4)
                    p = jq % 2
                    pj = jq % 2
                    TEN.wait_ge(s_wl[p].h, wl_val[jq])
                    TEN.wait_ge(s_xl[i % 2].h, xl2_val[i])
                    for hm in range(8):
                        gidx = jq * 8 + hm
                        if gidx >= 2:
                            TEN.wait_ge(s_hev.h, HEV0 + gidx - 1)
                        bank = s1p[hm % 2]
                        for kc in range(NKC):
                            ins = TEN.matmul(
                                bank,
                                wb1[p][:, kc * 1024 + hm * 128:
                                       kc * 1024 + (hm + 1) * 128],
                                xst[i % 2][:, kc * TG:(kc + 1) * TG],
                                start=(kc == 0), stop=(kc == NKC - 1))
                        s_s1.inc(ins)
                        SCA.wait_ge(s_s1.h, S10 + gidx + 1)
                        if hm == 0:
                            if jq >= 2:
                                SCA.wait_ge(s_s2.h, S20 + 4 * (jq - 1))
                            SCA.wait_ge(s_b1g.h, b1_val[i])
                        s_hev.inc(SCA.activation(
                            hT[pj][:, hm * TG:(hm + 1) * TG], bank, AF.Relu,
                            bias=b1sel[i % 2][:, (q * 8 + hm):
                                              (q * 8 + hm + 1)]))

                def emit_s2s(jq):
                    i, q = divmod(jq, 4)
                    p = jq % 2
                    pj = jq % 2
                    TEN.wait_ge(s_hev.h, HEV0 + (jq + 1) * 8)
                    for tc in range(4):
                        c2 = jq * 4 + tc
                        if c2 >= 2:
                            TEN.wait_ge(s_yev.h, YEV0 + c2 - 1)
                        else:
                            TEN.wait_ge(serV.h, b2add_val[NT - 2 + c2])
                        bank = s2p[tc % 2]
                        for nh in range(2):
                            half = bank[:, nh * TG:(nh + 1) * TG]
                            for hc in range(NKC):
                                ins = TEN.matmul(
                                    half,
                                    hT[pj][:, hc * TG + tc * 128:
                                           hc * TG + (tc + 1) * 128],
                                    wb2[p][:, hc * 1024 + nh * TG:
                                           hc * 1024 + (nh + 1) * TG],
                                    start=(hc == 0),
                                    stop=(hc == NKC - 1))
                        s_s2.inc(ins)
                        VEC.wait_ge(s_s2.h, S20 + c2 + 1)
                        ysl = y_acc[:, tc * 1024:(tc + 1) * 1024]
                        if q == 0:
                            if i >= 1 and tc == 0:
                                VEC.wait_ge(s_sc.h, SC0 + 64 * i)
                            ins2 = VEC.tensor_copy(out=ysl, in_=bank)
                        else:
                            ins2 = VEC.tensor_add(ysl, ysl, bank)
                        s_yev.inc(ins2)
                    if q == 3:
                        GPS.wait_ge(s_yev.h, YEV0 + 4 * (jq + 1))
                        for tc in range(4):
                            r0_ = i * TG + tc * 128
                            GPS.dma_start(
                                out=d_ysort[r0_:r0_ + 128, :],
                                in_=y_acc[:, tc * 1024:(tc + 1) * 1024],
                            ).then_inc(s_sc.h, 16)
                            s_sc.v += 16

                emit_wix(0)
                emit_wdyn(0)
                emit_wix(1)
                emit_wdyn(1)
                emit_xst(0)
                emit_xst(1)
                emit_s1s(0)
                for jq in range(NJQ):
                    if jq + 2 < NJQ:
                        emit_wix(jq + 2)
                        emit_wdyn(jq + 2)
                    if jq + 1 < NJQ:
                        if (jq + 1) % 4 == 0:
                            ii = (jq + 1) // 4
                            if ii + 1 < NTI:
                                emit_xst(ii + 1)
                        emit_s1s(jq + 1)
                    emit_s2s(jq)

                SC_ALL = s_sc.v
                for jt in range(NT):
                    if jt == 0:
                        GPS.wait_ge(s_sc.h, SC_ALL)
                    if jt >= 2:
                        GPS.wait_ge(s_xupd.h, XU0 + jt - 1)
                    GPS.wait_ge(serV.h, v_psr[NT - 1])
                    GPS.indirect_dma_start(
                        out=ystage[jt % 2], out_offset=None, in_=d_ysort,
                        in_offset=IndirectOffsetOnAxis(didxp[:, jt:jt + 1],
                                                       0),
                    ).then_inc(s_gws[jt % 2].h, 16)
                    s_gws[jt % 2].v += 16
                    VEC.wait_ge(s_gws[jt % 2].h, s_gws[jt % 2].v)
                    Xsl = X[:, jt * D:(jt + 1) * D]
                    s_xupd.inc(VEC.scalar_tensor_tensor(
                        out=Xsl, in0=ystage[jt % 2], scalar=0.5, in1=Xsl,
                        op0=ALU.mult, op1=ALU.add))

                TEN.wait_ge(s_yev.h, s_yev.v)
                SCA.wait_ge(s_yev.h, s_yev.v)
                VEC.wait_ge(s_hev.h, s_hev.v)
        return ret

    def emit_gla_call():
        for dst, srct in ((d_wqb, d_wq), (d_wkb, d_wk), (d_wvb, d_wv),
                          (d_wgb, d_wg), (d_wob, d_wo), (d_wgk1b, d_wgk1),
                          (d_wgk2b, d_wgk2)):
            GPS.dma_start(out=dst, in_=srct).then_inc(s_cast_g.h, 16)
            s_cast_g.v += 16
        with ExitStack() as gs:
            _uid[0] += 1
            _u = _uid[0]
            g = lambda n, sh, dt: gs.enter_context(
                nc.sbuf_tensor(f"{n}_u{_u}", sh, dt))[:]
            alloc_ln_bufs(gs, with_sqd=True, sqw_tiles=2)
            wqs = g("wqs", [128, NKC * 1024], BF16)
            wks = g("wks", [128, NKC * 1024], BF16)
            wvs = g("wvs", [128, NKC * 1024], BF16)
            wgs = g("wgs", [128, NKC * 1024], BF16)
            wos = g("wos", [128, NKC * 1024], BF16)
            wgk1s = g("wgk1s", [128, NKC * GLR], BF16)
            wgk2s = g("wgk2s", [GLR, D], BF16)
            xg = g("xg", [128, NKC * 256], BF16)
            qT = g("qT", [128, NHEAD * 256], BF16)
            kTf = g("kTf", [128, NHEAD * 256], BF16)
            z1s = g("z1s", [GLR, 256], BF16)
            gks = g("gks", [128, 2 * D], BF16)
            enbt = g("enbt", [128, D], BF16)
            ktt = g("ktt", [128, D], BF16)
            vtok = g("vtok", [128, D], BF16)
            gtok = g("gtok", [128, D], BF16)
            eb = g("eb", [128, D], BF16)
            enb = g("enb", [128, D], BF16)
            ebC = g("ebC", [128, 8], F32)
            qt = g("qt", [128, D], BF16)
            kt = g("kt", [128, D], BF16)
            smb = g("smb", [128, D], BF16)
            ofb = g("ofb", [128, D], BF16)
            oTa = g("oTa", [128, D], BF16)
            ssqa = g("ssqa", [128, 8], F32)
            lnva = g("lnva", [128, 8], F32)
            rsta = g("rsta", [128, 8], F32)
            Hf = g("Hf", [128, D], F32)
            Hb = g("Hb", [128, D], BF16)

            with ExitStack() as ps:
                pE = ps.enter_context(
                    nc.psum_tensor(f"pE_u{_u}", [128, 512], F32))[:]
                pA = ps.enter_context(
                    nc.psum_tensor(f"pA_u{_u}", [128, D], F32))[:]
                pB = ps.enter_context(
                    nc.psum_tensor(f"pB_u{_u}", [128, D], F32))[:]
                pC = ps.enter_context(
                    nc.psum_tensor(f"pC_u{_u}", [128, D], F32))[:]
                ptr = ps.enter_context(
                    nc.psum_tensor(f"ptr_u{_u}", [128, D], BF16))[:]

                silg = ln_bufs["sqw"][:, 0:D]
                SYN.wait_ge(serV.h, prep_done_v)
                SYN.wait_ge(s_cast_g.h, s_cast_g.v)
                for dst, srct in ((wqs, d_wqb), (wks, d_wkb), (wvs, d_wvb),
                                  (wgs, d_wgb), (wos, d_wob)):
                    dv = dst.rearrange("p (k m) -> p k m", k=NKC)
                    sv = srct.rearrange("(k p) m -> p k m", p=128)
                    SYN.dma_start(out=dv, in_=sv).then_inc(s_gld.h, 16)
                    s_gld.v += 16
                dv = wgk1s.rearrange("p (k m) -> p k m", k=NKC)
                sv = d_wgk1b.rearrange("(k p) m -> p k m", p=128)
                SYN.dma_start(out=dv, in_=sv).then_inc(s_gld.h, 16)
                s_gld.v += 16
                SYN.dma_start(out=wgk2s, in_=d_wgk2b).then_inc(s_gld.h, 16)
                s_gld.v += 16
                gld_all = s_gld.v

                VEC.memset(Hf, 0.0)
                hinit = barV(VEC.memset(Hb, 0.0))
                TEN.wait_ge(serV.h, hinit)
                TEN.wait_ge(serV.h, prep_done_v)
                SCA.wait_ge(serV.h, prep_done_v)

                XGE0, Z1T0, Z1E0, ZT0 = s_xgev.v, s_z1T.v, s_z1ev.v, s_zT.v
                TRT0g = s_trT.v
                XN0g, PRJ0, PEV0 = s_xn.v, s_projT.v, s_pev.v
                B0, S0_, O0, H0_, WO0, WOE0 = (s_b.v, s_S.v, s_o.v, s_H.v,
                                               s_wo.v, s_woev.v)
                gk_vals = []
                osq_vals = []
                state_vals = []
                ofb_vals = []
                sm_vals = []
                otev_vals = []
                sq_vals = []
                pev_cnt = 0

                x3 = xg.rearrange("p (j c) -> p j c", j=NKC)
                qT3 = qT.rearrange("p (h t) -> p h t", t=256)
                kT3 = kTf.rearrange("p (h t) -> p h t", t=256)

                def hview(t):
                    return t.rearrange("p (h c) -> p h c", c=128)

                for cg in range(16):
                    emit_ln_quad(cg * 2, 2, cg % 2, TRT0g)
                    for lt in range(2):
                        tt = cg * 2 + lt
                        k = tt % 2
                        emit_ln_xn(tt, cg % 2, lt, TRT0g)
                        TEN.wait_ge(s_xn.h, XN0g + tt + 1)
                        if cg > 0:
                            TEN.wait_ge(serV.h, otev_vals[2 * cg - 1])
                        if tt >= 1:
                            TEN.wait_ge(s_xgev.h, XGE0 + tt)
                        for j in range(NKC):
                            ins = TEN.transpose(
                                ptr[:, j * 128:(j + 1) * 128],
                                ln_bufs["xn"][k][:, j * 128:(j + 1) * 128], id_bf)
                        s_trT.inc(ins)
                        VEC.wait_ge(s_trT.h, s_trT.v)
                        if cg > 0 and lt == 0:
                            VEC.wait_ge(s_z1T.h, Z1T0 + cg)
                            VEC.wait_ge(s_projT.h, PRJ0 + 28 * cg)
                        dstv = x3[:, :, lt * 128:(lt + 1) * 128]
                        srcv = ptr.rearrange("p (j c) -> p j c", j=NKC)
                        s_xgev.inc(VEC.tensor_copy(out=dstv, in_=srcv))

                    TEN.wait_ge(s_gld.h, gld_all)
                    TEN.wait_ge(s_xgev.h, XGE0 + 2 * (cg + 1))
                    if cg > 0:
                        TEN.wait_ge(s_z1ev.h, Z1E0 + cg)
                        TEN.wait_ge(serV.h, state_vals[2 * cg - 1])
                        SCA.wait_ge(serV.h, state_vals[2 * cg - 1])
                    TEN.wait_ge(s_pev.h, PEV0 + pev_cnt)
                    for kc in range(NKC):
                        ins = TEN.matmul(pE[0:GLR, 0:256],
                                         wgk1s[:, kc * GLR:(kc + 1) * GLR],
                                         xg[:, kc * 256:(kc + 1) * 256],
                                         start=(kc == 0), stop=(kc == NKC - 1))
                    s_z1T.inc(ins)
                    s_z1T.wait(VEC)
                    s_z1ev.inc(VEC.tensor_copy(out=z1s, in_=pE[0:GLR, 0:256]))

                    proj_groups = []
                    for wsb, dstT, scale in ((wqs, qT, HKS), (wks, kTf, 1.0)):
                        for mq in range(8):
                            proj_groups.append((wsb, dstT, scale, mq))

                    def emit_proj_group(idx):
                        nonlocal pev_cnt
                        wsb, dstT, scale, mq = proj_groups[idx]
                        region = (pE[:, 0:256] if idx % 2 == 0
                                  else pB[:, 512:768])
                        if pev_cnt >= 2:
                            TEN.wait_ge(s_pev.h, PEV0 + pev_cnt - 1)
                        for kc in range(NKC):
                            ins = TEN.matmul(
                                region,
                                wsb[:, kc * 1024 + mq * 128:
                                    kc * 1024 + (mq + 1) * 128],
                                xg[:, kc * 256:(kc + 1) * 256],
                                start=(kc == 0), stop=(kc == NKC - 1))
                        s_projT.inc(ins)
                        s_projT.wait(SCA)
                        s_pev.inc(SCA.activation(
                            dstT[:, mq * 256:(mq + 1) * 256], region,
                            AF.Copy, scale=scale))
                        pev_cnt += 1

                    for tc in range(2):
                        ch_prev_reader = 2 * (cg - 1) + tc + 1
                        TEN.wait_ge(s_z1ev.h, Z1E0 + cg + 1)
                        if cg > 0 and tc == 0:
                            TEN.wait_ge(s_woev.h, WOE0 + 2 * cg)
                        if tc == 1:
                            TEN.wait_ge(serV.h, gk_vals[2 * cg])
                        for nh in range(2):
                            half = pC[:, nh * TG:(nh + 1) * TG]
                            TEN.matmul(half, z1s[:, tc * 128:(tc + 1) * 128],
                                       wgk2s[:, nh * TG:(nh + 1) * TG],
                                       start=True, stop=False)
                            ins = TEN.matmul(half, ones,
                                             bgk2r[:, nh * TG:(nh + 1) * TG],
                                             start=False, stop=True)
                        s_zT.inc(ins)
                        for gi in range(8):
                            emit_proj_group(tc * 8 + gi)
                        SCA.wait_ge(s_zT.h, ZT0 + 2 * cg + tc + 1)
                        barS(SCA.activation(ln_bufs["sqd"], pC, AF.Abs))
                        barS(SCA.activation(ln_bufs["sqd"], ln_bufs["sqd"],
                                            AF.Exp, scale=-1.0))
                        v_zs = barS(SCA.activation(ln_bufs["sqd"], ln_bufs["sqd"],
                                                   AF.Ln, bias=cb[:, 1:2]))
                        VEC.wait_ge(serS.h, v_zs)
                        VEC.wait_ge(s_zT.h, ZT0 + 2 * cg + tc + 1)
                        if cg > 0:
                            VEC.wait_ge(s_b.h, B0 + ch_prev_reader)
                        gk_sl = gks[:, tc * D:(tc + 1) * D]
                        barV(VEC.tensor_scalar(out=gk_sl, in0=pC, scalar1=0.0,
                                               scalar2=None, op0=ALU.min))
                        v_gk = barV(VEC.tensor_sub(gk_sl, gk_sl, ln_bufs["sqd"]))
                        gk_vals.append(v_gk)

                    for lt in range(2):
                        ch = cg * 2 + lt
                        TEN.wait_ge(serV.h, gk_vals[2 * cg + lt])
                        if ch > 0:
                            TEN.wait_ge(serV.h, state_vals[ch - 1])
                            TEN.wait_ge(serV.h, ofb_vals[ch - 1])
                        TEN.wait_ge(s_pev.h, PEV0 + pev_cnt)
                        for nh in range(2):
                            ins = TEN.matmul(
                                pB[:, nh * TG:(nh + 1) * TG], u_bf,
                                gks[:, lt * D + nh * TG:lt * D + (nh + 1) * TG],
                                start=True, stop=True)
                        for h in range(NHEAD):
                            ins = TEN.matmul(
                                pA[:, h * 128:(h + 1) * 128],
                                gks[:, lt * D + h * 128:lt * D + (h + 1) * 128],
                                u_bf, start=True, stop=True)
                        s_b.inc(ins)
                        s_b.wait(SCA)
                        SCA.activation(enbt, pB, AF.Exp, scale=-1.0 / GLN)
                        SCA.activation(eb, pA, AF.Exp, scale=1.0 / GLN)
                        SCA.activation(enb, pA, AF.Exp, scale=-1.0 / GLN)
                        v_ebs = barS(SCA.activation(
                            ebC, hview(pA)[:, :, 127], AF.Exp, scale=1.0 / GLN))
                        if ch > 0:
                            SCA.wait_ge(s_o.h, O0 + ch)
                            SCA.wait_ge(s_H.h, H0_ + ch)
                            VEC.wait_ge(s_o.h, O0 + ch)
                            VEC.wait_ge(s_H.h, H0_ + ch)
                        g_pev_val = [0]
                        for wsb, kind in ((wvs, "v"), (wgs, "g"), (wks, "k")):
                            for nh in range(2):
                                TEN.wait_ge(s_pev.h, PEV0 + pev_cnt)
                                for kc in range(NKC):
                                    ins = TEN.matmul(
                                        pE,
                                        xg[:, kc * 256 + lt * 128:
                                           kc * 256 + (lt + 1) * 128],
                                        wsb[:, kc * 1024 + nh * TG:
                                            kc * 1024 + (nh + 1) * TG],
                                        start=(kc == 0), stop=(kc == NKC - 1))
                                s_projT.inc(ins)
                                if kind == "k":
                                    s_projT.wait(VEC)
                                    VEC.wait_ge(serS.h, v_ebs)
                                    ins2 = VEC.tensor_mul(
                                        ktt[:, nh * TG:(nh + 1) * TG], pE,
                                        enbt[:, nh * TG:(nh + 1) * TG])
                                    s_pev.inc(ins2)
                                else:
                                    dst = vtok if kind == "v" else gtok
                                    s_projT.wait(SCA)
                                    s_pev.inc(SCA.activation(
                                        dst[:, nh * TG:(nh + 1) * TG], pE,
                                        AF.Copy))
                                    if kind == "g":
                                        g_pev_val[0] = s_pev.v
                                pev_cnt += 1
                        if ch > 0:
                            SCA.wait_ge(serV.h, ofb_vals[ch - 1])
                        SCA.wait_ge(s_pev.h, g_pev_val[0])
                        barS(SCA.activation(silg, gtok, AF.Exp, scale=-1.0))
                        barS(SCA.activation(silg, silg, AF.Ln, bias=cb[:, 1:2]))
                        v_sg = barS(SCA.activation(silg, silg, AF.Exp,
                                                   scale=-1.0))
                        sq_vals.append(v_sg)
                        VEC.wait_ge(serS.h, v_sg)
                        for h in range(NHEAD):
                            sl = slice(h * 128, (h + 1) * 128)
                            ins = VEC.tensor_mul(gtok[:, sl], gtok[:, sl], gnwb)
                        barV(ins)
                        barV(VEC.tensor_mul(silg, gtok, silg))
                        VEC.wait_ge(serS.h, v_ebs)
                        VEC.tensor_mul(hview(qt), qT3[:, :, lt * 128:(lt + 1) * 128],
                                       hview(eb))
                        v_qt = barV(VEC.tensor_mul(
                            hview(kt), kT3[:, :, lt * 128:(lt + 1) * 128],
                            hview(enb)))
                        TEN.wait_ge(serV.h, v_qt)
                        TEN.wait_ge(serV.h, gk_vals[2 * cg + 1])
                        if ch > 0:
                            TEN.wait_ge(s_woev.h, WOE0 + ch)
                        for h in range(NHEAD):
                            ins = TEN.matmul(
                                pC[:, h * 128:(h + 1) * 128],
                                kt[:, h * 128:(h + 1) * 128],
                                qt[:, h * 128:(h + 1) * 128],
                                start=True, stop=True)
                        s_S.inc(ins)
                        s_S.wait(VEC)
                        if ch > 0:
                            VEC.wait_ge(serS.h, osq_vals[ch - 1])
                        for h in range(NHEAD):
                            ins = VEC.tensor_mul(smb[:, h * 128:(h + 1) * 128],
                                                 pC[:, h * 128:(h + 1) * 128],
                                                 u_bf)
                        v_sm = barV(ins)
                        sm_vals.append(v_sm)
                        if ch > 0:
                            TEN.wait_ge(serV.h, state_vals[ch - 1])
                        for h in range(NHEAD):
                            sl = slice(h * 128, (h + 1) * 128)
                            ins = TEN.matmul(pB[:, sl], ktt[:, sl], vtok[:, sl],
                                             start=True, stop=True)
                        s_H.inc(ins)
                        TEN.wait_ge(serV.h, v_sm)
                        for h in range(NHEAD):
                            sl = slice(h * 128, (h + 1) * 128)
                            TEN.matmul(pA[:, sl], smb[:, sl], vtok[:, sl],
                                       start=True, stop=False)
                            ins = TEN.matmul(pA[:, sl], qt[:, sl], Hb[:, sl],
                                             start=False, stop=True)
                        s_o.inc(ins)
                        s_o.wait(SCA)
                        if ch > 0:
                            SCA.wait_ge(serV.h, sm_vals[ch - 1] if False else v_sm)
                        v_osq = barS(SCA.activation(smb, pA, AF.Square))
                        osq_vals.append(v_osq)
                        VEC.wait_ge(serS.h, v_osq)
                        v_ssq = barV(VEC.tensor_reduce(
                            ssqa, hview(smb), axis=AX, op=ALU.add))
                        SCA.wait_ge(serV.h, v_ssq)
                        barS(SCA.activation(lnva, ssqa, AF.Ln, scale=1.0 / DH,
                                            bias=cb[:, 0:1]))
                        v_rst = barS(SCA.activation(rsta, lnva, AF.Exp,
                                                    scale=-0.5))
                        VEC.wait_ge(serS.h, v_rst)
                        for h in range(NHEAD):
                            sl = slice(h * 128, (h + 1) * 128)
                            ins = VEC.scalar_tensor_tensor(
                                out=ofb[:, sl], in0=pA[:, sl],
                                scalar=rsta[:, h:h + 1], in1=silg[:, sl],
                                op0=ALU.mult, op1=ALU.mult)
                        v_ofb = barV(ins)
                        ofb_vals.append(v_ofb)
                        ln_bufs["sq_war"][0] = v_ofb
                        s_H.wait(VEC)
                        barV(VEC.tensor_add(Hf, Hf, pB))
                        for h in range(NHEAD):
                            sl = slice(h * 128, (h + 1) * 128)
                            ins = VEC.tensor_scalar(
                                out=Hf[:, sl], in0=Hf[:, sl],
                                scalar1=ebC[:, h:h + 1], scalar2=None,
                                op0=ALU.mult)
                        barV(ins)
                        s_o.wait(VEC)
                        v_state = barV(VEC.tensor_copy(out=Hb, in_=Hf))
                        state_vals.append(v_state)
                        TEN.wait_ge(serV.h, v_ofb)
                        if ch > 0:
                            TEN.wait_ge(serV.h, otev_vals[ch - 1])
                        for h in range(NHEAD):
                            sl = slice(h * 128, (h + 1) * 128)
                            ins = TEN.transpose(ptr[:, sl], ofb[:, sl], id_bf)
                        s_otr.inc(ins)
                        s_otr.wait(VEC)
                        v_ot = barV(VEC.tensor_copy(out=oTa, in_=ptr))
                        otev_vals.append(v_ot)
                        TEN.wait_ge(serV.h, v_ot)
                        for nh in range(2):
                            half = pC[:, nh * TG:(nh + 1) * TG]
                            for hh in range(NHEAD):
                                ins = TEN.matmul(
                                    half, oTa[:, hh * 128:(hh + 1) * 128],
                                    wos[:, hh * 1024 + nh * TG:
                                        hh * 1024 + (nh + 1) * TG],
                                    start=(hh == 0), stop=(hh == NHEAD - 1))
                        s_wo.inc(ins)
                        s_wo.wait(VEC)
                        Xsl = X[:, ch * D:(ch + 1) * D]
                        s_woev.inc(VEC.tensor_add(Xsl, Xsl, pC))
                TEN.wait_ge(s_woev.h, s_woev.v)
                SCA.wait_ge(s_woev.h, s_woev.v)

    def x_fence():
        for eng in (TEN, VEC, SCA, SYN):
            for s in (s_xin2[0], s_yev, s_woev, s_xupd):
                s.wait(eng)

    x_fence()
    last_ffn = None
    if "ffn1" in phases:
        last_ffn = emit_ffn_call()
        x_fence()
    if "gla" in phases:
        emit_gla_call()
        last_ffn = None
        x_fence()
    if "ffn2" in phases:
        last_ffn = emit_ffn_call()
        x_fence()

    for s in (s_xin2[0], s_yev, s_woev):
        s.wait(GPS)
    for jt in range(NT):
        if last_ffn is not None:
            GPS.wait_ge(s_xupd.h, last_ffn["XU0"] + jt + 1)
        GPS.dma_start(out=d_y[jt * 128:(jt + 1) * 128, :],
                      in_=X[:, jt * D:(jt + 1) * D]).then_inc(s_yw2[0].h, 16)
        s_yw2[0].v += 16
    SCA.wait_ge(s_yw2[0].h, s_yw2[0].v)

    ctx.close()
    return nc


def _in_map(inputs, core):
    a = lambda v: np.ascontiguousarray(np.asarray(v, dtype=np.float32))
    return {
        "x": a(inputs["L0_attn_x"][core]),
        "w1": a(inputs["L0_ffn_w1"]),
        "b1": a(inputs["L0_ffn_b1"]),
        "w2": a(inputs["L0_ffn_w2"]),
        "b2": a(inputs["L0_ffn_b2"]),
        "gw": a(inputs["L0_ffn_gate_w"]),
        "gb": a(inputs["L0_ffn_gate_b"]),
        "wq": a(inputs["L0_attn_Wq"]),
        "wk": a(inputs["L0_attn_Wk"]),
        "wv": a(inputs["L0_attn_Wv"]),
        "wg": a(inputs["L0_attn_Wg"]),
        "wo": a(inputs["L0_attn_Wo"]),
        "wgk1": a(inputs["L0_attn_Wgk1"]),
        "wgk2": a(inputs["L0_attn_Wgk2"]),
        "bgk2": a(inputs["L0_attn_bgk2"]),
        "gnw": a(inputs["L0_attn_g_norm_w"]),
    }


def kernel(**inputs):
    n = 8
    nc = build_core_program()
    in_maps = [_in_map(inputs, c) for c in range(n)]
    res = run_bass_kernel_spmd(nc, in_maps, core_ids=list(range(n)))
    return np.stack([res.results[c]["y"] for c in range(n)], axis=0)
